# version 09
# baseline (speedup 1.0000x reference)
import numpy as np
from contextlib import ExitStack

import concourse.bass as bass
import concourse.mybir as mybir
from concourse.bass_utils import run_bass_kernel_spmd

F32 = mybir.dt.float32
F32R = mybir.dt.float32r
BF16 = mybir.dt.bfloat16
AF = mybir.ActivationFunctionType
ALU = mybir.AluOpType
AX = mybir.AxisListType

D = 1024
L = 2048
NDC = D // 128
NTC = L // 128
TT = 512
NTT = L // TT
EPS = 1e-5
NCORES = 8

C_EPS = 0
C_NCONST = 4


def _host_cpool():
    p = np.zeros((128, C_NCONST), np.float32)
    p[:, C_EPS] = EPS
    return p


class PB:
    CENG = ("tensor", "vector", "scalar", "gpsimd", "sync")
    ENGSEM = {"tensor": "pe", "vector": "dve", "scalar": "act",
              "gpsimd": "pool", "sync": "sp"}

    def __init__(self, tag):
        self.tag = tag
        self.nc = bass.Bass("TRN2", target_bir_lowering=False)
        self.st = ExitStack()
        self.sem = {}
        self.cnt = {}
        self.water = {}
        self.opidx = {e: 0 for e in self.CENG}
        self.drained = {e: 0 for e in self.CENG}
        for s in ("pe", "dve", "act", "pool", "sp",
                  "ld0", "ld1", "ld2", "ld3", "ld4", "ld5",
                  "gp0", "gp1", "gp2", "gp3", "gp4", "gp5", "stq", "st2", "st3", "st4", "ldc",
                  "lx0", "lx1", "lx2", "lx3", "pc0", "pc1", "pc2", "pc3",
                  "pc4", "pc5", "pc6", "pc7"):
            self.sem[s] = self.st.enter_context(self.nc.semaphore(f"s_{s}"))
            self.cnt[s] = 0
        self._streams = None
        self.inputs = {}
        self._rot = []
        self._fixed = {}
        self._rr = 0
        self.prefix = ""
        self.moe_pre = None
        self.pre = {}

    def sbuf(self, name, shape, dt):
        return self.st.enter_context(
            self.nc.sbuf_tensor(self.prefix + name, shape, dt))

    def scope(self):
        pb = self

        class _Scope:
            def __enter__(selfs):
                selfs.saved = pb.st
                selfs.st = ExitStack()
                pb.st = selfs.st
                return selfs

            def __exit__(selfs, *exc):
                pb.st = selfs.saved
                selfs.st.close()
                return False
        return _Scope()

    def dram_in(self, name, arr, dt):
        ap = self.nc.dram_tensor(name, list(arr.shape), dt, kind="ExternalInput").ap()
        self.inputs[name] = np.ascontiguousarray(arr)
        return ap

    def dram_out(self, name, shape, dt):
        return self.nc.dram_tensor(name, shape, dt, kind="ExternalOutput").ap()

    def dram_tmp(self, name, shape, dt):
        return self.nc.dram_tensor(self.prefix + name, shape, dt,
                                   kind="Internal").ap()

    def psum_banks(self, n_rot=6):
        self._rot = [dict(t=self.st.enter_context(
            self.nc.psum_tensor(f"bank{i}", [128, 512], F32)), rd=[], wr=None)
            for i in range(n_rot)]
        nfx = 8 - n_rot
        self._fixed_slots = [dict(t=self.st.enter_context(
            self.nc.psum_tensor(f"fx{i}", [128, 512], F32)), rd=[], wr=None)
            for i in range(nfx)]
        self._fixed_map = {}
        self._fixed_rr = 0

    def bank(self):
        b = self._rot[self._rr % len(self._rot)]
        self._rr += 1
        return b

    def bank_fixed(self, name):
        if name not in self._fixed_map:
            slot = self._fixed_slots[self._fixed_rr % len(self._fixed_slots)]
            self._fixed_rr += 1
            for k in [k for k, v in self._fixed_map.items() if v is slot]:
                del self._fixed_map[k]
            self._fixed_map[name] = slot
        return self._fixed_map[name]

    def phase(self):
        pb = self

        class _Phase:
            def __enter__(selfp):
                assert pb._streams is None
                pb._streams = {e: [] for e in PB.CENG}
                return selfp

            def __exit__(selfp, *exc):
                if exc[0] is not None:
                    pb._streams = None
                    return False
                streams, pb._streams = pb._streams, None
                if any(streams.values()):
                    with pb.nc.Block() as blk:
                        for e in PB.CENG:
                            if streams[e]:
                                def runner(h, fns=tuple(streams[e])):
                                    for f in fns:
                                        f(h)
                                getattr(blk, e)(runner)
                return True
        return _Phase()

    def op(self, eng, fn, deps=(), inc=False):
        waits = []
        need_drain = False
        for ev in deps:
            if ev is None:
                continue
            kind, peng, sname, val, oi = ev
            if kind == "c" and peng == eng:
                if sname is not None and self.water.get((eng, sname), 0) >= val:
                    continue
                if oi > self.drained[eng]:
                    need_drain = True
            else:
                assert sname is not None, f"dep on {peng} without sem inc"
                if self.water.get((eng, sname), 0) < val:
                    waits.append((sname, val))
                    self.water[(eng, sname)] = val
        if need_drain:
            self.drained[eng] = self.opidx[eng]
        myinc = None
        if inc:
            sname = self.ENGSEM[eng]
            self.cnt[sname] += 1
            myinc = (sname, self.cnt[sname])
        self.opidx[eng] += 1
        oi = self.opidx[eng]
        semd = self.sem

        def closure(h, waits=tuple(waits), nd=need_drain, myinc=myinc, fn=fn):
            for s, v in waits:
                h.wait_ge(semd[s], v)
            if nd:
                h.drain()
            inst = fn(h)
            if myinc is not None:
                inst.then_inc(semd[myinc[0]], 1)
        self._streams[eng].append(closure)
        return ("c", eng, myinc[0] if myinc else None,
                myinc[1] if myinc else None, oi)

    def dma(self, queue, chan, out_ap, in_ap, deps=(), **kw):
        waits = []
        for ev in deps:
            if ev is None:
                continue
            kind, peng, sname, val, oi = ev
            assert sname is not None, "dma dep needs a sem"
            if self.water.get((queue, sname), 0) < val:
                waits.append((sname, val))
                self.water[(queue, sname)] = val
        self.cnt[chan] += 16
        val = self.cnt[chan]
        semd = self.sem
        self.opidx[queue] += 1

        def closure(h, waits=tuple(waits), kw=kw):
            for s, v in waits:
                h.wait_ge(semd[s], v)
            h.dma_start(out=out_ap, in_=in_ap, **kw).then_inc(semd[chan], 16)
        self._streams[queue].append(closure)
        return ("d", queue, chan, val, self.opidx[queue])

    def chan_total(self, chan):
        return ("d", None, chan, self.cnt[chan], 0)

    def waitev(self, eng, *evs):
        semd = self.sem
        waits = []
        for ev in evs:
            if ev is None:
                continue
            kind, peng, sname, val, oi = ev
            assert sname is not None
            if self.water.get((eng, sname), 0) < val:
                waits.append((sname, val))
                self.water[(eng, sname)] = val
        if waits:
            def closure(h, waits=tuple(waits)):
                for s, v in waits:
                    h.wait_ge(semd[s], v)
            self._streams[eng].append(closure)


def mm_seq(pb, bank, out_ap, mms, first_extra=(), inc_last=True, skip_gc=False):
    n = len(mms)
    ev = None
    for i, (lh, rh, deps) in enumerate(mms):
        dd = list(deps)
        if i == 0:
            dd += list(bank["rd"]) + list(first_extra)
            if bank["wr"] is not None:
                dd.append(bank["wr"])
        st_ = (i == 0)
        sp_ = (i == n - 1)
        ev = pb.op("tensor",
                   lambda t, lh=lh, rh=rh, st_=st_, sp_=sp_:
                   t.matmul(out_ap, lh, rh, start=st_, stop=sp_,
                            skip_group_check=skip_gc) if skip_gc else
                   t.matmul(out_ap, lh, rh, start=st_, stop=sp_),
                   deps=dd, inc=(inc_last and sp_))
    bank["wr"] = ev
    bank["rd"] = []
    return ev


def read_bank(pb, eng, bank, fn, deps=(), inc=True):
    ev = pb.op(eng, fn, deps=list(deps) + [bank["wr"]], inc=inc)
    bank["rd"].append(ev)
    return ev


class Consts:
    def __init__(self, pb):
        import ml_dtypes
        self.d = {}
        host = {
            "ones_col": np.ones((128, 1), np.float32),
            "ones_row": np.ones((1, 128), np.float32),
            "ident": np.eye(128, dtype=np.float32),
            "cpool": _host_cpool(),
        }
        self.d["ones_row_b"] = pb.dram_in(
            "c_ones_row_b", np.ones((1, 128), ml_dtypes.bfloat16), BF16)
        self.d["ones_col_b"] = pb.dram_in(
            "c_ones_col_b", np.ones((128, 1), ml_dtypes.bfloat16), BF16)
        dd = {k: pb.dram_in("c_" + k, v, F32) for k, v in host.items()}
        self.d.update(dd)
        self.d["ones_row_r"] = pb.dram_in("c_ones_row_r", host["ones_row"], F32R)
        self.d["ones_col_r"] = pb.dram_in("c_ones_col_r", host["ones_col"], F32R)
        self.ones_col = pb.sbuf("ones_col", [128, 1], F32)
        self.ones_row = pb.sbuf("ones_row", [1, 128], F32)
        self.ones_row_r = pb.sbuf("ones_row_r", [1, 128], F32R)
        self.ones_col_r = pb.sbuf("ones_col_r", [128, 1], F32R)
        self.ones_row_b = pb.sbuf("ones_row_b", [1, 128], BF16)
        self.ones_col_b = pb.sbuf("ones_col_b", [128, 1], BF16)
        self.ident = pb.sbuf("ident", [128, 128], F32)
        self.cpool = pb.sbuf("cpool", [128, C_NCONST], F32)

    def load(self, pb):
        for name, sb in [("ones_col", self.ones_col), ("ones_row", self.ones_row),
                         ("ones_row_r", self.ones_row_r),
                         ("ones_col_r", self.ones_col_r),
                         ("ones_row_b", self.ones_row_b),
                         ("ones_col_b", self.ones_col_b),
                         ("ident", self.ident), ("cpool", self.cpool)]:
            pb.dma("sync", "ldc", sb[:], self.d[name])

    def c(self, idx):
        return self.cpool[:, idx:idx + 1]


def ln_stats(pb, cst, get_x, mrow, rrow, name, beta_row=None, war_hook=None):
    sq_t = [pb.sbuf(f"{name}_sqa", [128, TT], F32R),
            pb.sbuf(f"{name}_sqb", [128, TT], F32R)]
    vrow = pb.sbuf(f"{name}_vrow", [1, TT], F32)
    grp_ev = [None, None]
    prev_small = []

    for tt in range(NTT):
        bank_s = pb.bank_fixed("stat")
        bank_q = pb.bank_fixed("c3")
        ps = bank_s["t"][0:1, 0:TT]
        pq = bank_q["t"][0:1, 0:TT]
        mev = None
        for dc in range(NDC):
            x_ap, off, x_ev = get_x(dc, tt)
            assert x_ap.dtype == F32R, f"{name}: ln_stats needs F32R source"
            sq = sq_t[dc % 2]
            sev = pb.op("scalar",
                        lambda s, sq=sq, x_ap=x_ap, o=off:
                        s.activation(out=sq[:], in_=x_ap[:, o:o + TT],
                                     func=AF.Square),
                        deps=[x_ev, grp_ev[dc % 2]], inc=True)
            dd = [x_ev]
            if dc == 0:
                dd += list(bank_s["rd"])
                if bank_s["wr"] is not None:
                    dd.append(bank_s["wr"])
            pb.op("tensor",
                  lambda t, x_ap=x_ap, o=off, f=(dc == 0), ps=ps:
                  t.matmul(ps, cst.ones_col_r[:], x_ap[:, o:o + TT],
                           start=f, stop=(False if dc < NDC - 1 else True),
                           skip_group_check=True),
                  deps=dd, inc=False)
            dd2 = [sev]
            if dc == 0:
                dd2 += list(bank_q["rd"])
                if bank_q["wr"] is not None:
                    dd2.append(bank_q["wr"])
            mev = pb.op("tensor",
                        lambda t, sq=sq, f=(dc == 0), pq=pq:
                        t.matmul(pq, cst.ones_col_r[:], sq[:],
                                 start=f, stop=(False if dc < NDC - 1 else True),
                                 skip_group_check=True),
                        deps=dd2, inc=True)
            grp_ev[dc % 2] = mev
        bank_s["wr"] = mev
        bank_q["wr"] = mev
        bank_s["rd"] = []
        bank_q["rd"] = []
        if war_hook is not None:
            war_hook(tt, mev)
        c0 = tt * TT
        e1 = read_bank(pb, "vector", bank_s,
                       lambda v, ps=ps, c0=c0:
                       v.tensor_scalar(mrow[0:1, c0:c0 + TT], ps,
                                       1.0 / D, None, ALU.mult),
                       deps=prev_small)
        e2 = read_bank(pb, "vector", bank_q,
                       lambda v, pq=pq, c0=c0:
                       v.tensor_scalar(vrow[0:1, :], pq,
                                       1.0 / D, None, ALU.mult),
                       deps=prev_small + [e1])
        e3 = pb.op("scalar",
                   lambda s, c0=c0:
                   s.activation(out=rrow[0:1, c0:c0 + TT],
                                in_=mrow[0:1, c0:c0 + TT], func=AF.Square),
                   deps=[e1], inc=True)
        e4 = pb.op("vector",
                   lambda v, c0=c0:
                   v.tensor_tensor(vrow[0:1, :], vrow[0:1, :],
                                   rrow[0:1, c0:c0 + TT], ALU.subtract),
                   deps=[e2, e3], inc=True)
        if beta_row is not None:
            b1 = pb.op("vector",
                       lambda v, c0=c0:
                       v.tensor_scalar(beta_row[0:1, c0:c0 + TT], vrow[0:1, :],
                                       1.0 + EPS, EPS * EPS, ALU.mult, ALU.add),
                       deps=[e4], inc=True)
            b2 = pb.op("scalar",
                       lambda s, c0=c0:
                       s.activation(out=beta_row[0:1, c0:c0 + TT],
                                    in_=beta_row[0:1, c0:c0 + TT], func=AF.Sqrt),
                       deps=[b1], inc=True)
            b3 = pb.op("vector",
                       lambda v, c0=c0:
                       v.reciprocal(beta_row[0:1, c0:c0 + TT],
                                    beta_row[0:1, c0:c0 + TT]),
                       deps=[b2], inc=True)
        e5 = pb.op("scalar",
                   lambda s, c0=c0:
                   s.activation(out=rrow[0:1, c0:c0 + TT], in_=vrow[0:1, :],
                                func=AF.Sqrt, bias=cst.cpool[0:1, C_EPS:C_EPS + 1],
                                scale=1.0),
                   deps=[e4], inc=True)
        e6 = pb.op("vector",
                   lambda v, c0=c0:
                   v.reciprocal(rrow[0:1, c0:c0 + TT], rrow[0:1, c0:c0 + TT]),
                   deps=[e5], inc=True)
        prev_small = [e6] + ([b3] if beta_row is not None else [])
    return prev_small


def bcast_rows(pb, cst, mrow, rrow, tt, width=TT):
    bm = pb.bank_fixed("bcm")
    mm_seq(pb, bm, bm["t"][:, 0:width],
           [(cst.ones_row[0:1, :], mrow[0:1, tt * TT:tt * TT + width], [])])
    br = pb.bank_fixed("bcr")
    mm_seq(pb, br, br["t"][:, 0:width],
           [(cst.ones_row[0:1, :], rrow[0:1, tt * TT:tt * TT + width], [])])
    return bm, br


def emit_l1ffn(pb, cst, x_d, xo_d, weights):
    x_f = x_d.bitcast(F32)
    xo_b = xo_d.bitcast(F32) if xo_d.dtype == F32R else xo_d
    need_pre(pb, weights, ["l1f_WrT", "l1f_WkT", "l1f_WvT"])
    WrT_d, Wr_ev = pb.pre["l1f_WrT"]
    WkT_d, Wk_ev = pb.pre["l1f_WkT"]
    WvT_d, Wv_ev = pb.pre["l1f_WvT"]
    mu_d = pb.dram_in("l1f_mu", weights["l1f_mu"], F32)

    x = [pb.sbuf(f"x{dc}", [128, L], F32R) for dc in range(NDC)]
    xbuf = [pb.sbuf(f"xb{dc}", [128, L], BF16) for dc in range(NDC)]
    r = [pb.sbuf(f"r{dc}", [128, L], BF16) for dc in range(NDC)]
    kact = [pb.sbuf(f"ka{dc}", [128, L], BF16) for dc in range(NDC)]
    mu = pb.sbuf("mu", [128, 16], F32)
    mu1m = pb.sbuf("mu1m", [128, 16], F32)
    mrow = pb.sbuf("mrow", [1, L], F32)
    rrow = pb.sbuf("rrow", [1, L], F32)
    scr = [pb.sbuf(f"scr{i}", [128, TT], F32) for i in range(3)]
    scrb = pb.sbuf("scrb", [128, TT], BF16)
    tail = pb.sbuf("tail", [128, NDC], F32)
    wbuf = [pb.sbuf(f"wbuf{i}", [128, D], BF16) for i in range(2)]

    with pb.phase():
        for dc in range(NDC):
            pb.dma("sync", "lx0", x[dc][:], x_d[dc * 128:(dc + 1) * 128, :])
        pb.dma("sync", "ld0", mu[:], mu_d)
        ld_all = pb.chan_total("ld0")
        gp_all = pb.chan_total("lx0")
        e = pb.op("vector",
                  lambda v: v.tensor_scalar(mu1m[:], mu[:], 1.0, None, ALU.subtract),
                  deps=[ld_all], inc=True)
        pb.op("vector",
              lambda v: v.tensor_scalar(mu1m[:], mu1m[:], -1.0, None, ALU.mult),
              deps=[e], inc=True)
        pb.waitev("vector", gp_all)
        ln_stats(pb, cst, lambda dc, tt: (x[dc], tt * TT, gp_all), mrow, rrow,
                 "snx")

    def emit_shift_mix(base):
        with pb.phase():
            tl_ev = [None] * NDC
            for dc in range(NDC):
                tl_ev[dc] = pb.op("vector",
                                  lambda v, dc=dc: v.memset(tail[:, dc:dc + 1], 0.0),
                                  inc=True)
            h_last = None
            ht2_last = None
            for tt in range(NTT):
                bm, br = bcast_rows(pb, cst, mrow, rrow, tt)
                for dc in range(NDC):
                    ht = scr[0]
                    h_t = scr[1]
                    e1 = read_bank(pb, "vector", bm,
                                   lambda v, dc=dc, tt=tt, bm=bm, ht=ht:
                                   v.tensor_tensor(ht[:], x[dc][:, tt * TT:(tt + 1) * TT],
                                                   bm["t"][:, 0:TT], ALU.subtract),
                                   deps=[h_last])
                    e2 = read_bank(pb, "vector", br,
                                   lambda v, br=br, ht=ht, h_t=h_t:
                                   v.tensor_tensor(h_t[:], ht[:], br["t"][:, 0:TT],
                                                   ALU.mult),
                                   deps=[e1])
                    et = pb.op("scalar",
                               lambda s, dc=dc, h_t=h_t:
                               s.activation(out=scrb[:], in_=h_t[:], func=AF.Copy,
                                            scale=mu[:, base + dc:base + dc + 1]),
                               deps=[e2, ht2_last], inc=True)
                    e3 = pb.op("vector",
                               lambda v, dc=dc, tt=tt, h_t=h_t:
                               v.scalar_tensor_tensor(
                                   xbuf[dc][:, tt * TT + 1:(tt + 1) * TT],
                                   h_t[:, 0:TT - 1],
                                   mu1m[:, base + dc:base + dc + 1],
                                   scrb[:, 1:TT], ALU.mult, ALU.add),
                               deps=[et, e2], inc=True)
                    e4 = pb.op("vector",
                               lambda v, dc=dc, tt=tt:
                               v.scalar_tensor_tensor(
                                   xbuf[dc][:, tt * TT:tt * TT + 1],
                                   tail[:, dc:dc + 1],
                                   mu1m[:, base + dc:base + dc + 1],
                                   scrb[:, 0:1], ALU.mult, ALU.add),
                               deps=[et, tl_ev[dc]], inc=True)
                    tl_ev[dc] = pb.op("vector",
                                      lambda v, dc=dc, h_t=h_t:
                                      v.tensor_copy(tail[:, dc:dc + 1],
                                                    h_t[:, TT - 1:TT]),
                                      deps=[e2, e4], inc=True)
                    h_last = tl_ev[dc]
                    ht2_last = e4

    def stream_w(w_pair, do, wrd, chan0=0):
        w_d, w_ev = w_pair
        wb = wbuf[do % 2]
        ev = pb.dma("sync", f"ld{1 + do % 2}",
                    wb[:].rearrange("p (c f) -> p c f", f=128),
                    w_d[:, do * 128:(do + 1) * 128].rearrange("(c p) f -> p c f", p=128),
                    deps=[wrd[do % 2]] + w_ev)
        return wb, ev

    r_ev = [[None] * NTT for _ in range(NDC)]
    ka_ev = [[None] * NTT for _ in range(NDC)]

    emit_shift_mix(0)
    with pb.phase():
        wrd = [None, None]
        for do in range(NDC):
            wb, wev = stream_w((WrT_d, Wr_ev), do, wrd)
            for tt in range(NTT):
                bo = pb.bank()
                mms = [(wb[:, dc * 128:(dc + 1) * 128],
                        xbuf[dc][:, tt * TT:(tt + 1) * TT], [wev])
                       for dc in range(NDC)]
                mev = mm_seq(pb, bo, bo["t"][:, 0:TT], mms)
                wrd[do % 2] = mev
                r_ev[do][tt] = read_bank(
                    pb, "scalar", bo,
                    lambda s, do=do, tt=tt, bo=bo:
                    s.activation(out=r[do][:, tt * TT:(tt + 1) * TT],
                                 in_=bo["t"][:, 0:TT], func=AF.Sigmoid))

    emit_shift_mix(8)
    with pb.phase():
        wrd = [None, None]
        relu_last = None
        for do in range(NDC):
            wb, wev = stream_w((WkT_d, Wk_ev), do, wrd)
            for tt in range(NTT):
                bo = pb.bank()
                mms = [(wb[:, dc * 128:(dc + 1) * 128],
                        xbuf[dc][:, tt * TT:(tt + 1) * TT], [wev])
                       for dc in range(NDC)]
                mev = mm_seq(pb, bo, bo["t"][:, 0:TT], mms)
                wrd[do % 2] = mev
                e1 = read_bank(pb, "vector", bo,
                               lambda v, bo=bo:
                               v.tensor_scalar(scr[2][:], bo["t"][:, 0:TT], 0.0,
                                               None, ALU.max),
                               deps=[relu_last])
                ka_ev[do][tt] = pb.op(
                    "scalar",
                    lambda s, do=do, tt=tt:
                    s.activation(out=kact[do][:, tt * TT:(tt + 1) * TT],
                                 in_=scr[2][:], func=AF.Square),
                    deps=[e1], inc=True)
                relu_last = ka_ev[do][tt]

    vrow = pb.sbuf("o_vrow", [1, TT], F32)
    oscr = [pb.sbuf(f"oscr{i}", [128, TT], F32R) for i in range(3)]
    with pb.phase():
        wrd = [None, None]
        bo_rd = [None, None]
        sq_rd = [None, None]
        prev_small = []
        for tt in range(NTT):
            bank_s = pb.bank_fixed("stat")
            bank_q = pb.bank_fixed("c3")
            ps = bank_s["t"][0:1, 0:TT]
            pq = bank_q["t"][0:1, 0:TT]
            last_me = None
            for do in range(NDC):
                wb, wev = stream_w((WvT_d, Wv_ev), do, wrd)
                bo = pb.bank()
                mms = [(wb[:, dc * 128:(dc + 1) * 128],
                        kact[dc][:, tt * TT:(tt + 1) * TT],
                        [wev, ka_ev[dc][tt]])
                       for dc in range(NDC)]
                mev = mm_seq(pb, bo, bo["t"][:, 0:TT], mms)
                wrd[do % 2] = mev
                bt = oscr[do % 2]
                sqt = oscr[2]
                e1 = read_bank(pb, "vector", bo,
                               lambda v, bt=bt, bo=bo, do=do, tt=tt:
                               v.tensor_tensor(bt[:], bo["t"][:, 0:TT],
                                               r[do][:, tt * TT:(tt + 1) * TT],
                                               ALU.mult),
                               deps=[r_ev[do][tt], bo_rd[do % 2]])
                e2 = pb.op("scalar",
                           lambda s, bt=bt, sqt=sqt:
                           s.activation(out=sqt[:], in_=bt[:], func=AF.Square),
                           deps=[e1, sq_rd[0]], inc=True)
                dd = [e1]
                if do == 0:
                    dd += list(bank_s["rd"])
                    if bank_s["wr"] is not None:
                        dd.append(bank_s["wr"])
                pb.op("tensor",
                      lambda t, bt=bt, f=(do == 0), l=(do == NDC - 1), ps=ps:
                      t.matmul(ps, cst.ones_col_r[:], bt[:],
                               start=f, stop=l,
                               skip_group_check=True),
                      deps=dd)
                dd2 = [e2]
                if do == 0:
                    dd2 += list(bank_q["rd"])
                    if bank_q["wr"] is not None:
                        dd2.append(bank_q["wr"])
                last_me = pb.op(
                    "tensor",
                    lambda t, sqt=sqt, f=(do == 0), l=(do == NDC - 1), pq=pq:
                    t.matmul(pq, cst.ones_col_r[:], sqt[:],
                             start=f, stop=l,
                             skip_group_check=True),
                    deps=dd2, inc=True)
                bo_rd[do % 2] = last_me
                sq_rd[0] = last_me
            bank_s["wr"] = last_me
            bank_q["wr"] = last_me
            bank_s["rd"] = []
            bank_q["rd"] = []
            c0 = tt * TT
            e1 = read_bank(pb, "vector", bank_s,
                           lambda v, ps=ps, c0=c0:
                           v.tensor_scalar(mrow[0:1, c0:c0 + TT], ps,
                                           1.0 / D, None, ALU.mult),
                           deps=prev_small)
            e2 = read_bank(pb, "vector", bank_q,
                           lambda v, pq=pq, c0=c0:
                           v.tensor_scalar(vrow[0:1, :], pq,
                                           1.0 / D, None, ALU.mult),
                           deps=prev_small + [e1])
            e3 = pb.op("scalar",
                       lambda s, c0=c0:
                       s.activation(out=rrow[0:1, c0:c0 + TT],
                                    in_=mrow[0:1, c0:c0 + TT], func=AF.Square),
                       deps=[e1], inc=True)
            e4 = pb.op("vector",
                       lambda v, c0=c0:
                       v.tensor_tensor(vrow[0:1, :], vrow[0:1, :],
                                       rrow[0:1, c0:c0 + TT], ALU.subtract),
                       deps=[e2, e3], inc=True)
            e5 = pb.op("scalar",
                       lambda s, c0=c0:
                       s.activation(out=rrow[0:1, c0:c0 + TT], in_=vrow[0:1, :],
                                    func=AF.Sqrt,
                                    bias=cst.cpool[0:1, C_EPS:C_EPS + 1],
                                    scale=1.0),
                       deps=[e4], inc=True)
            e6 = pb.op("vector",
                       lambda v, c0=c0:
                       v.reciprocal(rrow[0:1, c0:c0 + TT],
                                    rrow[0:1, c0:c0 + TT]),
                       deps=[e5], inc=True)
            prev_small = [e6]

    with pb.phase():
        wrd = [None, None]
        t1_rd = [None, None]
        xadd_ev = [[None] * NTT for _ in range(NDC)]
        for tt in range(NTT):
            bm, br = bcast_rows(pb, cst, mrow, rrow, tt)
            for do in range(NDC):
                wb, wev = stream_w((WvT_d, Wv_ev), do, wrd)
                bo = pb.bank()
                mms = [(wb[:, dc * 128:(dc + 1) * 128],
                        kact[dc][:, tt * TT:(tt + 1) * TT],
                        [wev, ka_ev[dc][tt]])
                       for dc in range(NDC)]
                mev = mm_seq(pb, bo, bo["t"][:, 0:TT], mms)
                wrd[do % 2] = mev
                tb = scr[do % 2]
                e1 = read_bank(pb, "vector", bo,
                               lambda v, tb=tb, bo=bo, do=do, tt=tt:
                               v.tensor_tensor(tb[:], bo["t"][:, 0:TT],
                                               r[do][:, tt * TT:(tt + 1) * TT],
                                               ALU.mult),
                               deps=[r_ev[do][tt], t1_rd[do % 2]])
                e2 = read_bank(pb, "vector", bm,
                               lambda v, tb=tb, bm=bm:
                               v.tensor_tensor(tb[:], tb[:], bm["t"][:, 0:TT],
                                               ALU.subtract),
                               deps=[e1])
                e3 = read_bank(pb, "vector", br,
                               lambda v, tb=tb, br=br:
                               v.tensor_tensor(tb[:], tb[:], br["t"][:, 0:TT],
                                               ALU.mult),
                               deps=[e2])
                e4 = pb.op("vector",
                           lambda v, do=do, tt=tt, tb=tb:
                           v.tensor_tensor(x[do][:, tt * TT:(tt + 1) * TT],
                                           x[do][:, tt * TT:(tt + 1) * TT],
                                           tb[:], ALU.add),
                           deps=[e3, xadd_ev[do][tt - 1] if tt > 0 else None],
                           inc=True)
                t1_rd[do % 2] = e4
                xadd_ev[do][tt] = e4
        for dc in range(NDC):
            pb.dma("sync", "stq", xo_b[dc * 128:(dc + 1) * 128, :],
                   x[dc][:].bitcast(F32),
                   deps=[xadd_ev[dc][tt] for tt in range(NTT)])
        pb.waitev("sync", pb.chan_total("stq"))

    return


PRE_ORDER = ["l0a_rqT", "l0a_mqT", "l0a_combT", "l0a_rvT", "l0a_mvT",
             "l0f_tapT", "l0f_wqkvT", "l0f_woT",
             "l0f_w1", "l0f_w2",
             "l1a_wk_lin", "l1a_wv_lin", "l1a_wq_lin", "l1a_wq_loc",
             "l1a_wk_loc", "l1a_wv_loc", "l1a_wg1", "l1a_wg2", "l1a_wo",
             "l1f_WrT", "l1f_WkT", "l1f_WvT"]


PRE_WAVES = [
    (("l0a_",), ["pc4"]),
    (("l0f_tapT", "l0f_wqkvT", "l0f_woT"), ["pc5"]),
    (("l0f_w1",), ["pc0", "pc1"]),
    (("l0f_w2",), ["pc2", "pc3"]),
    (("l1a_",), ["pc6"]),
    (("l1f_",), ["pc7"]),
]


def need_pre(pb, weights, names):
    names = [n for n in names if n not in pb.pre]
    if not names:
        return
    with pb.phase():
        for keys, chans in PRE_WAVES:
            wave = [n for n in names
                    if any(n == k or n.startswith(k) for k in keys)]
            if not wave:
                continue
            rr = 0
            dsts = {}
            for nm in wave:
                arr = weights[nm]
                src_ap = pb.dram_in(nm, arr, F32R)
                dst = pb.nc.dram_tensor(nm + "_bf", list(arr.shape), BF16,
                                        kind="Internal").ap()
                dsts[nm] = dst
                if arr.ndim == 3 and arr.shape[0] == 8:
                    for e in range(8):
                        pb.dma("gpsimd", chans[rr % len(chans)], dst[e],
                               src_ap[e])
                        rr += 1
                else:
                    pb.dma("gpsimd", chans[rr % len(chans)], dst, src_ap)
                    rr += 1
            ev = [pb.chan_total(c) for c in chans]
            for nm in wave:
                pb.pre[nm] = (dsts[nm], ev)


def emit_l0ffn(pb, cst, x_d, xo_d, weights):
    x_f = x_d.bitcast(F32)
    xo_b = xo_d.bitcast(F32) if xo_d.dtype == F32R else xo_d
    h_hbm = pb.dram_tmp("h_hbm", [D, L], BF16)
    ctx_hbm = pb.dram_tmp("ctx_hbm", [D, L], BF16)

    need_pre(pb, weights, ["l0f_tapT", "l0f_wqkvT", "l0f_woT",
                           "l0f_w1", "l0f_w2"])
    tapT_d, tap_ev = pb.pre["l0f_tapT"]
    wqkvT_d, wqkv_ev = pb.pre["l0f_wqkvT"]
    woT_d, wo_ev = pb.pre["l0f_woT"]
    convb_d = pb.dram_in("l0f_convb_c", weights["l0f_convb_c"], F32)
    bqkv_c_d = pb.dram_in("l0f_bqkv_c", weights["l0f_bqkv_c"], F32)
    bqkv_r_d = pb.dram_in("l0f_bqkv_row", weights["l0f_bqkv_row"], F32R)
    bo_c_d = pb.dram_in("l0f_bo_c", weights["l0f_bo_c"], F32)
    ln1g_d = pb.dram_in("l0f_ln1g_c", weights["l0f_ln1g_c"], F32)
    ln1b_d = pb.dram_in("l0f_ln1b_c", weights["l0f_ln1b_c"], F32)
    ln2g_d = pb.dram_in("l0f_ln2g_c", weights["l0f_ln2g_c"], F32)
    ln2b_d = pb.dram_in("l0f_ln2b_c", weights["l0f_ln2b_c"], F32)
    gatew_d = pb.dram_in("l0f_gatew", weights["l0f_gatew"], F32R)
    gateb_d = pb.dram_in("l0f_gateb_row", weights["l0f_gateb_row"], F32R)
    w1bf_d, w1_ev = pb.pre["l0f_w1"]
    w2bf_d, w2_ev = pb.pre["l0f_w2"]
    pre_ev = list(dict.fromkeys(w1_ev + w2_ev))
    b1c_d = pb.dram_in("l0f_b1_c", weights["l0f_b1_c"], F32)
    b2_d = pb.dram_in("l0f_b2", weights["l0f_b2"], F32R)

    xz = [pb.sbuf(f"xz{dc}", [128, L], F32R) for dc in range(NDC)]
    mrow = pb.sbuf("mrow", [1, L], F32)
    rrow = pb.sbuf("rrow", [1, L], F32)
    scr = [pb.sbuf(f"scr{i}", [128, TT], F32) for i in range(3)]
    bias_c = pb.sbuf("bias_c", [128, 64], F32)
    ln2_c = pb.sbuf("ln2_c", [128, 16], F32)
    bqkv_row = pb.sbuf("bqkv_row", [1, D], BF16)
    gateb_row = pb.sbuf("gateb_row", [1, 8], F32R)
    gatew = pb.sbuf("gatew", [128, 64], F32R)
    ones_row_bf = pb.sbuf("ones_row_bf", [1, 128], BF16)
    probs = pb.sbuf("probs", [128, 128], F32)

    with pb.scope():
        xs4 = [[pb.sbuf(f"xs{tt}_{dc}", [128, TT], F32R) for dc in range(NDC)]
               for tt in range(NTT)]
        scr_r = [pb.sbuf(f"scr_r{i}", [128, TT], BF16) for i in range(2)]
        with pb.phase():
            pb.dma("sync", "ld0", bias_c[:, 0:8], convb_d)
            pb.dma("sync", "ld0", bias_c[:, 8:32], bqkv_c_d)
            pb.dma("sync", "ld0", bias_c[:, 32:40], bo_c_d)
            pb.dma("sync", "ld0", bias_c[:, 40:48], ln1g_d)
            pb.dma("sync", "ld0", bias_c[:, 48:56], ln1b_d)
            pb.dma("sync", "ld0", ln2_c[:, 0:8], ln2g_d)
            pb.dma("sync", "ld0", ln2_c[:, 8:16], ln2b_d)
            pb.dma("gpsimd", "gp0", bqkv_row[:], bqkv_r_d[0:1, 2 * D:3 * D])
            pb.dma("sync", "ld0", gateb_row[:], gateb_d)
            pb.dma("sync", "ld0", gatew[:].rearrange("p (c e) -> p c e", e=8),
                   gatew_d.rearrange("(c p) e -> p c e", p=128))
            pb.dma("gpsimd", "gp0", ones_row_bf[:], cst.d["ones_row"])
            for _eng in PB.CENG:
                pb.waitev(_eng, pb.chan_total("ld0"), pb.chan_total("gp0"))
            tot = []
            for tt in range(NTT):
                ch = f"lx{tt}"
                for dc in range(NDC):
                    pb.dma("sync", ch, xs4[tt][dc][:],
                           x_d[dc * 128:(dc + 1) * 128, tt * TT:(tt + 1) * TT])
                tot.append(pb.chan_total(ch))
            ln_stats(pb, cst, lambda dc, tt: (xs4[tt][dc], 0, tot[tt]),
                     mrow, rrow, "s0x")
        with pb.phase():
            last_store = [None, None]
            last1 = None
            for tt in range(NTT):
                bm, br = bcast_rows(pb, cst, mrow, rrow, tt)
                for dc in range(NDC):
                    sr = scr_r[dc % 2]
                    e1 = read_bank(pb, "vector", bm,
                                   lambda v, dc=dc, tt=tt, bm=bm:
                                   v.tensor_tensor(scr[0][:], xs4[tt][dc][:],
                                                   bm["t"][:, 0:TT], ALU.subtract),
                                   deps=[last1])
                    e2 = read_bank(pb, "vector", br,
                                   lambda v, br=br, sr=sr:
                                   v.tensor_tensor(sr[:], scr[0][:],
                                                   br["t"][:, 0:TT], ALU.mult),
                                   deps=[e1, last_store[dc % 2]])
                    last_store[dc % 2] = pb.dma(
                        "sync", ("stq", "st2")[dc % 2],
                        h_hbm[dc * 128:(dc + 1) * 128, tt * TT:(tt + 1) * TT],
                        sr[:], deps=[e2])
                    last1 = e2
        h_done = pb.chan_total("stq")
        h_done2 = pb.chan_total("st2")

    with pb.scope():
        hw = [[pb.sbuf(f"hw{i}_{dc}", [128, TT + 2], BF16) for dc in range(NDC)]
              for i in range(2)]
        wtap = [pb.sbuf(f"wtap{i}", [128, D], BF16) for i in range(3)]
        with pb.phase():
            huse = [None, None]
            wrd = [None, None, None]
            for tt in range(NTT):
                ch = f"ld{1 + tt % 2}"
                lo, hi = tt * TT - 1, tt * TT + TT + 1
                clo, chi = max(lo, 0), min(hi, L)
                for dc in range(NDC):
                    buf = hw[tt % 2][dc]
                    mse = []
                    if lo < 0:
                        mse.append(pb.op("vector",
                                         lambda v, buf=buf: v.memset(buf[:, 0:1], 0.0),
                                         deps=[huse[tt % 2]], inc=True))
                    if hi > L:
                        mse.append(pb.op("vector",
                                         lambda v, buf=buf:
                                         v.memset(buf[:, TT + 1:TT + 2], 0.0),
                                         deps=[huse[tt % 2]], inc=True))
                    pb.dma("sync", ch, buf[:, clo - lo:chi - lo],
                           h_hbm[dc * 128:(dc + 1) * 128, clo:chi],
                           deps=[huse[tt % 2], h_done, h_done2] + mse)
                hev = pb.chan_total(ch)
                for do in range(NDC):
                    bo = pb.bank()
                    mms = []
                    for k in range(3):
                        wt = wtap[k]
                        wev = pb.dma("sync", f"ld{3 + k}",
                                     wt[:].rearrange("p (c f) -> p c f", f=128),
                                     tapT_d[k][:, do * 128:(do + 1) * 128]
                                     .rearrange("(c p) f -> p c f", p=128),
                                     deps=[wrd[k]] + tap_ev)
                        for dc in range(NDC):
                            mms.append((wt[:, dc * 128:(dc + 1) * 128],
                                        hw[tt % 2][dc][:, k:k + TT],
                                        [wev, hev]))
                    mev = mm_seq(pb, bo, bo["t"][:, 0:TT], mms)
                    wrd = [mev, mev, mev]
                    read_bank(pb, "vector", bo,
                              lambda v, do=do, tt=tt, bo=bo:
                              v.tensor_scalar(xz[do][:, tt * TT:(tt + 1) * TT],
                                              bo["t"][:, 0:TT],
                                              bias_c[:, do:do + 1], None, ALU.add))
                    huse[tt % 2] = mev

    with pb.scope():
        v_all = pb.sbuf("v_all", [128, NTC * 1056], BF16)
        xzb = [pb.sbuf(f"xzb{dc}", [128, L], BF16) for dc in range(NDC)]
        with pb.scope():
          wv = pb.sbuf("wv", [128, NDC * TT], BF16)
          with pb.phase():
              msv = pb.op("vector", lambda v: v.memset(v_all[:], 1.0), inc=True)
              xzb_ev = [None] * NDC
              for dc in range(NDC):
                  eng = ("vector", "scalar")[dc % 2]
                  if dc % 2 == 0:
                      xzb_ev[dc] = pb.op("vector",
                                         lambda v, dc=dc:
                                         v.tensor_copy(xzb[dc][:], xz[dc][:]),
                                         inc=True)
                  else:
                      xzb_ev[dc] = pb.op("scalar",
                                         lambda s, dc=dc:
                                         s.activation(out=xzb[dc][:],
                                                      in_=xz[dc][:], func=AF.Copy),
                                         inc=True)
              prev_cp = None
              last_mv = None
              for half in range(2):
                  wev = pb.dma("sync", "ld3",
                               wv[:].rearrange("p (c f) -> p c f", f=TT),
                               wqkvT_d[:, 2 * D + half * TT:2 * D + (half + 1) * TT]
                               .rearrange("(c p) f -> p c f", p=128),
                               deps=[pb.chan_total("ld3"), last_mv] + wqkv_ev)
                  for tc in range(NTC):
                      bv = pb.bank()
                      mms = [(ones_row_bf[0:1, :],
                              bqkv_row[0:1, half * TT:(half + 1) * TT],
                              [])]
                      mms += [(xzb[dc][:, tc * 128:(tc + 1) * 128],
                               wv[:, dc * TT:(dc + 1) * TT],
                               [wev, xzb_ev[dc]])
                              for dc in range(NDC)]
                      mv_ = mm_seq(pb, bv, bv["t"][:, 0:TT], mms)
                      last_mv = mv_
                      for bq in range(4):
                          hp = half * 4 + bq
                          base = tc * 1056 + hp * 132
                          prev_cp = read_bank(
                              pb, "vector", bv,
                              lambda v, bv=bv, bq=bq, base=base:
                              v.tensor_copy(v_all[:, base:base + 64],
                                            bv["t"][:, bq * 128:bq * 128 + 64]),
                              deps=[msv, prev_cp])
                          prev_cp = read_bank(
                              pb, "vector", bv,
                              lambda v, bv=bv, bq=bq, base=base:
                              v.tensor_copy(v_all[:, base + 66:base + 130],
                                            bv["t"][:, bq * 128 + 64:bq * 128 + 128]),
                              deps=[prev_cp])
        qp = pb.sbuf("qp", [128, L], BF16)
        kp = pb.sbuf("kp", [128, L], BF16)
        wq = [pb.sbuf(f"wq{i}", [128, D], BF16) for i in range(2)]
        expt = [pb.sbuf(f"expt{i}", [128, TT], BF16) for i in range(4)]
        ctxp = pb.sbuf("ctxp", [128, L], BF16)
        sums_r = pb.sbuf("sums_r", [1, TT], F32)
        rc_sb = pb.sbuf("rc_sb", [128, TT], F32)
        with pb.phase():
            qp_ev = [None] * NTT
            kp_ev = [None] * NTT
            wrd_q = [None, None]
            last_pe_prev_hp = None
            prev_ctx_store = None
            prev_sums = None
            prev_rc = None
            for hp in range(NDC):
                wev_q = pb.dma("sync", "ld1",
                               wq[0][:].rearrange("p (c f) -> p c f", f=128),
                               wqkvT_d[:, hp * 128:(hp + 1) * 128]
                               .rearrange("(c p) f -> p c f", p=128),
                               deps=[wrd_q[0]] + wqkv_ev)
                wev_k = pb.dma("sync", "ld2",
                               wq[1][:].rearrange("p (c f) -> p c f", f=128),
                               wqkvT_d[:, D + hp * 128:D + (hp + 1) * 128]
                               .rearrange("(c p) f -> p c f", p=128),
                               deps=[wrd_q[1]] + wqkv_ev)
                for tt in range(NTT):
                    bq = pb.bank()
                    mms = [(wq[0][:, dc * 128:(dc + 1) * 128],
                            xzb[dc][:, tt * TT:(tt + 1) * TT],
                            [wev_q, xzb_ev[dc]])
                           for dc in range(NDC)]
                    mq = mm_seq(pb, bq, bq["t"][:, 0:TT], mms)
                    qp_ev[tt] = read_bank(pb, "vector", bq,
                                          lambda v, tt=tt, bq=bq, hp=hp:
                                          v.tensor_scalar(
                                              qp[:, tt * TT:(tt + 1) * TT],
                                              bq["t"][:, 0:TT],
                                              bias_c[:, 8 + hp:9 + hp],
                                              0.125, ALU.add, ALU.mult),
                                          deps=[last_pe_prev_hp])
                    bk = pb.bank()
                    mms = [(wq[1][:, dc * 128:(dc + 1) * 128],
                            xzb[dc][:, tt * TT:(tt + 1) * TT],
                            [wev_k, xzb_ev[dc]])
                           for dc in range(NDC)]
                    mk = mm_seq(pb, bk, bk["t"][:, 0:TT], mms)
                    kp_ev[tt] = read_bank(pb, "vector", bk,
                                          lambda v, tt=tt, bk=bk, hp=hp:
                                          v.tensor_scalar(
                                              kp[:, tt * TT:(tt + 1) * TT],
                                              bk["t"][:, 0:TT],
                                              bias_c[:, 16 + hp:17 + hp],
                                              None, ALU.add),
                                          deps=[last_pe_prev_hp])
                    wrd_q[0] = mq
                    wrd_q[1] = mk
                last_mm = None
                for qt in range(NTT):
                    bca = pb.bank_fixed("bcm")
                    bcb = pb.bank_fixed("bcr")
                    exp_rd = [None, None, None, None]
                    mca = mcb = None
                    for skc in range(NTC):
                        bsA = pb.bank()
                        mA = mm_seq(pb, bsA, bsA["t"][:, 0:TT],
                                    [(kp[0:64, skc * 128:(skc + 1) * 128],
                                      qp[0:64, qt * TT:(qt + 1) * TT],
                                      [kp_ev[skc // 4], qp_ev[qt]])])
                        eA = read_bank(pb, "scalar", bsA,
                                       lambda s, bsA=bsA, i=(skc % 2):
                                       s.activation(out=expt[i][:],
                                                    in_=bsA["t"][:, 0:TT],
                                                    func=AF.Exp),
                                       deps=[exp_rd[skc % 2]])
                        bsB = pb.bank()
                        mB = mm_seq(pb, bsB, bsB["t"][:, 0:TT],
                                    [(kp[64:128, skc * 128:(skc + 1) * 128],
                                      qp[64:128, qt * TT:(qt + 1) * TT],
                                      [kp_ev[skc // 4], qp_ev[qt]])])
                        eB = read_bank(pb, "scalar", bsB,
                                       lambda s, bsB=bsB, i=2 + (skc % 2):
                                       s.activation(out=expt[i][:],
                                                    in_=bsB["t"][:, 0:TT],
                                                    func=AF.Exp),
                                       deps=[exp_rd[2 + skc % 2]])
                        vbase = skc * 1056 + hp * 132
                        ddA = [eA]
                        if skc == 0:
                            ddA += list(bca["rd"]) + ([bca["wr"]] if bca["wr"] else [])
                        mca = pb.op("tensor",
                                    lambda t, vbase=vbase, i=(skc % 2), bca=bca,
                                    skc=skc:
                                    t.matmul(bca["t"][0:66, 0:TT],
                                             v_all[:, vbase:vbase + 66],
                                             expt[i][:], start=(skc == 0),
                                             stop=(skc == NTC - 1)),
                                    deps=ddA, inc=True)
                        exp_rd[skc % 2] = mca
                        ddB = [eB]
                        if skc == 0:
                            ddB += list(bcb["rd"]) + ([bcb["wr"]] if bcb["wr"] else [])
                        mcb = pb.op("tensor",
                                    lambda t, vbase=vbase, i=2 + (skc % 2), bcb=bcb,
                                    skc=skc:
                                    t.matmul(bcb["t"][0:66, 0:TT],
                                             v_all[:, vbase + 66:vbase + 132],
                                             expt[i][:], start=(skc == 0),
                                             stop=(skc == NTC - 1)),
                                    deps=ddB, inc=True)
                        exp_rd[2 + skc % 2] = mcb
                    bca["wr"] = mca
                    bca["rd"] = []
                    bcb["wr"] = mcb
                    bcb["rd"] = []
                    last_mm = mcb
                    for side, bank_c, base in (("A", bca, 0), ("B", bcb, 64)):
                        e1 = read_bank(pb, "vector", bank_c,
                                       lambda v, bank_c=bank_c:
                                       v.tensor_copy(sums_r[0:1, :],
                                                     bank_c["t"][64:65, 0:TT]),
                                       deps=[prev_sums])
                        e2 = pb.op("vector",
                                   lambda v: v.reciprocal(sums_r[0:1, :],
                                                          sums_r[0:1, :]),
                                   deps=[e1], inc=True)
                        prev_sums = e2
                        brc = pb.bank()
                        mrc = mm_seq(pb, brc, brc["t"][0:64, 0:TT],
                                     [(cst.ones_row[0:1, 0:64], sums_r[0:1, :],
                                       [e2])])
                        e3 = read_bank(pb, "vector", brc,
                                       lambda v, base=base, brc=brc:
                                       v.tensor_copy(rc_sb[base:base + 64, :],
                                                     brc["t"][0:64, 0:TT]),
                                       deps=[prev_rc])
                        prev_rc = e3
                        e4 = read_bank(pb, "vector", bank_c,
                                       lambda v, base=base, qt=qt, bank_c=bank_c:
                                       v.tensor_tensor(
                                           ctxp[base:base + 64,
                                                qt * TT:(qt + 1) * TT],
                                           bank_c["t"][0:64, 0:TT],
                                           rc_sb[base:base + 64, :], ALU.mult),
                                       deps=[e3, prev_ctx_store])
                        last_ctx = e4
                prev_ctx_store = pb.dma("sync", "stq",
                                        ctx_hbm[hp * 128:(hp + 1) * 128, :], ctxp[:],
                                        deps=[last_ctx,
                                              pb.chan_total("stq")])
                last_pe_prev_hp = last_mm

    with pb.scope():
        ctxs = [[pb.sbuf(f"ctxs{i}_{c}", [128, TT], BF16) for c in range(NDC)]
                for i in range(2)]
        wo_bf = [pb.sbuf(f"wo_bf{i}", [128, D], BF16) for i in range(2)]
        with pb.phase():
            cuse = [None, None]
            wrd = [None, None]
            for tt in range(NTT):
                ch = f"ld{1 + tt % 2}"
                for c in range(NDC):
                    pb.dma("sync", ch, ctxs[tt % 2][c][:],
                           ctx_hbm[c * 128:(c + 1) * 128, tt * TT:(tt + 1) * TT],
                           deps=[cuse[tt % 2], pb.chan_total("stq")])
                ctot = pb.chan_total(ch)
                for do in range(NDC):
                    wb = wo_bf[do % 2]
                    wev = pb.dma("sync", f"ld{3 + do % 2}",
                                 wb[:].rearrange("p (c f) -> p c f", f=128),
                                 woT_d[:, do * 128:(do + 1) * 128]
                                 .rearrange("(c p) f -> p c f", p=128),
                                 deps=[wrd[do % 2]] + wo_ev)
                    bo = pb.bank()
                    mms = [(wb[:, c * 128:(c + 1) * 128], ctxs[tt % 2][c][:],
                            [wev, ctot]) for c in range(NDC)]
                    mev = mm_seq(pb, bo, bo["t"][:, 0:TT], mms)
                    wrd[do % 2] = mev
                    read_bank(pb, "vector", bo,
                              lambda v, do=do, tt=tt, bo=bo:
                              v.scalar_tensor_tensor(
                                  xz[do][:, tt * TT:(tt + 1) * TT],
                                  bo["t"][:, 0:TT], bias_c[:, 32 + do:33 + do],
                                  xz[do][:, tt * TT:(tt + 1) * TT],
                                  ALU.add, ALU.add))
                    cuse[tt % 2] = mev

    with pb.phase():
        ln_stats(pb, cst, lambda dc, tt: (xz[dc], tt * TT, None), mrow, rrow, "s1")
    with pb.phase():
        last = None
        for tt in range(NTT):
            bm, br = bcast_rows(pb, cst, mrow, rrow, tt)
            for dc in range(NDC):
                e1 = read_bank(pb, "vector", bm,
                               lambda v, dc=dc, tt=tt, bm=bm:
                               v.tensor_tensor(scr[0][:],
                                               xz[dc][:, tt * TT:(tt + 1) * TT],
                                               bm["t"][:, 0:TT], ALU.subtract),
                               deps=[last])
                e2 = read_bank(pb, "vector", br,
                               lambda v, br=br:
                               v.tensor_tensor(scr[1][:], scr[0][:],
                                               br["t"][:, 0:TT], ALU.mult),
                               deps=[e1])
                e3 = pb.op("vector",
                           lambda v, dc=dc, tt=tt:
                           v.tensor_scalar(xz[dc][:, tt * TT:(tt + 1) * TT],
                                           scr[1][:],
                                           bias_c[:, 40 + dc:41 + dc],
                                           bias_c[:, 48 + dc:49 + dc],
                                           ALU.mult, ALU.add),
                           deps=[e2], inc=True)
                last = e3

    with pb.phase():
        gp = pb.sbuf("gp", [128, 8], F32)
        mx = pb.sbuf("gmx", [128, 1], F32)
        exg = pb.sbuf("exg", [128, 8], F32)
        sm = pb.sbuf("gsm", [128, 1], F32)
        prevp = None
        for tc in range(NTC):
            bg = pb.bank()
            mms = [(cst.ones_row_r[0:1, :], gateb_row[0:1, :], [])]
            mms += [(xz[dc][:, tc * 128:(tc + 1) * 128],
                     gatew[:, dc * 8:(dc + 1) * 8], []) for dc in range(NDC)]
            mg = mm_seq(pb, bg, bg["t"][:, 0:8], mms)
            g1 = read_bank(pb, "vector", bg,
                           lambda v, bg=bg: v.tensor_copy(gp[:], bg["t"][:, 0:8]),
                           deps=[prevp])
            g2 = pb.op("vector", lambda v: v.reduce_max(mx[:], gp[:], axis=AX.X),
                       deps=[g1], inc=True)
            g3 = pb.op("vector", lambda v: v.tensor_scalar(mx[:], mx[:], -1.0,
                                                           None, ALU.mult),
                       deps=[g2], inc=True)
            g4 = pb.op("scalar", lambda s: s.activation(out=exg[:], in_=gp[:],
                                                        func=AF.Exp,
                                                        bias=mx[:, 0:1], scale=1.0),
                       deps=[g3], inc=True)
            g5 = pb.op("vector", lambda v: v.reduce_sum(sm[:], exg[:], axis=AX.X),
                       deps=[g4], inc=True)
            g6 = pb.op("vector", lambda v: v.reciprocal(sm[:], sm[:]),
                       deps=[g5], inc=True)
            g7 = pb.op("vector",
                       lambda v, tc=tc:
                       v.tensor_scalar(probs[:, tc * 8:(tc + 1) * 8], exg[:],
                                       sm[:, 0:1], None, ALU.mult),
                       deps=[g6], inc=True)
            prevp = g7

    with pb.scope():
        h_e = [pb.sbuf(f"he{fc}", [128, TT], BF16) for fc in range(32)]
        y_acc = [pb.sbuf(f"ya{tc}", [128, D], F32) for tc in range(4)]
        x2bf = [pb.sbuf(f"x2bf{dc}", [128, L], BF16) for dc in range(NDC)]
        w1b = [pb.sbuf(f"w1b{i}", [128, 8 * TT], BF16) for i in range(2)]
        w2b = [pb.sbuf(f"w2b{i}", [128, 2 * TT], BF16) for i in range(2)]
        b2row = pb.sbuf("b2row", [1, D], BF16)
        b1c = pb.sbuf("b1c", [128, 32], F32)
        with pb.phase():
            x2bf_ev = [None] * NDC
            for dc in range(NDC):
                x2bf_ev[dc] = pb.op("vector",
                                    lambda v, dc=dc:
                                    v.tensor_copy(x2bf[dc][:], xz[dc][:]),
                                    inc=True)
            he_ev = [None] * 32
            prev_y_last = None
            prev_tr_last = None
            wrd = [None, None]
            wrd2 = [None, None]
            zadd_prev = [None] * NDC
            for ts in range(NTT):
                ya_ev = [pb.op("vector",
                               lambda v, tc=tc: v.memset(y_acc[tc][:], 0.0),
                               deps=[prev_tr_last], inc=True) for tc in range(4)]
                yb = [pb.bank_fixed(f"moe{tc}") for tc in range(4)]
                for e in range(8):
                    b2ev = pb.dma("gpsimd", "gp5", b2row[:], b2_d[e:e + 1, :],
                                  deps=[prev_y_last])
                    b1ev = pb.dma("sync", "ld3", b1c[:], b1c_d[e],
                                  deps=[he_ev[31]])
                    wevg = [None] * 8

                    def w1_load(g, dep, e=e):
                        return pb.dma(
                            "sync", f"ld{1 + g % 2}",
                            w1b[g % 2][:].rearrange("p (c f) -> p c f", f=TT),
                            w1bf_d[e][:, g * TT:(g + 1) * TT]
                            .rearrange("(c p) f -> p c f", p=128),
                            deps=[dep] + pre_ev)
                    for g in range(2):
                        wevg[g] = w1_load(g, wrd[g % 2])
                    for fc in range(32):
                        g = fc // 4
                        wb = w1b[g % 2]
                        off = (fc % 4) * 128
                        bh = pb.bank()
                        mms = [(wb[:, dc * TT + off:dc * TT + off + 128],
                                x2bf[dc][:, ts * TT:(ts + 1) * TT],
                                [wevg[g], x2bf_ev[dc]])
                               for dc in range(NDC)]
                        mh = mm_seq(pb, bh, bh["t"][:, 0:TT], mms)
                        if fc % 4 == 3:
                            wrd[g % 2] = mh
                            if g + 2 < 8:
                                wevg[g + 2] = w1_load(g + 2, mh)
                        he_ev[fc] = read_bank(
                            pb, "scalar", bh,
                            lambda s, fc=fc, bh=bh:
                            s.activation(out=h_e[fc][:], in_=bh["t"][:, 0:TT],
                                         func=AF.Relu, bias=b1c[:, fc:fc + 1],
                                         scale=1.0),
                            deps=[b1ev, prev_y_last])
                    for dh in range(2):
                        for tc4 in range(4):
                            bank = yb[tc4]
                            dd = list(bank["rd"]) + \
                                ([bank["wr"]] if bank["wr"] else [])
                            pb.op("tensor",
                                  lambda t, dh=dh, bank=bank:
                                  t.matmul(bank["t"][:, 0:TT],
                                           ones_row_bf[0:1, :],
                                           b2row[0:1, dh * TT:(dh + 1) * TT],
                                           start=True, stop=False,
                                           skip_group_check=True),
                                  deps=dd + [b2ev])
                            bank["rd"] = []
                        myev = None
                        for fc in range(32):
                            g2 = fc // 2
                            wb2 = w2b[g2 % 2]
                            if fc % 2 == 0:
                                wev2 = pb.dma(
                                    "sync", f"ld{4 + g2 % 2}",
                                    wb2[:].rearrange("p (a f) -> p a f", f=TT),
                                    w2bf_d[e][fc * 128:(fc + 2) * 128,
                                              dh * TT:(dh + 1) * TT]
                                    .rearrange("(a p) f -> p a f", p=128),
                                    deps=[wrd2[g2 % 2]] + pre_ev)
                            woff = (fc % 2) * TT
                            for tc4 in range(4):
                                bank = yb[tc4]
                                myev = pb.op(
                                    "tensor",
                                    lambda t, fc=fc, tc4=tc4, bank=bank, wb2=wb2,
                                    woff=woff:
                                    t.matmul(bank["t"][:, 0:TT],
                                             h_e[fc][:, tc4 * 128:(tc4 + 1) * 128],
                                             wb2[:, woff:woff + TT],
                                             start=False, stop=(fc == 31),
                                             skip_group_check=True),
                                    deps=[wev2, he_ev[fc]],
                                    inc=(tc4 == 3))
                            if fc % 2 == 1:
                                wrd2[g2 % 2] = myev
                        prev_y_last = myev
                        for tc4 in range(4):
                            bank = yb[tc4]
                            bank["wr"] = myev
                            tcg = ts * 4 + tc4
                            ya_ev[tc4] = read_bank(
                                pb, "vector", bank,
                                lambda v, tc4=tc4, dh=dh, e=e, tcg=tcg, bank=bank:
                                v.scalar_tensor_tensor(
                                    y_acc[tc4][:, dh * TT:(dh + 1) * TT],
                                    bank["t"][:, 0:TT],
                                    probs[:, tcg * 8 + e:tcg * 8 + e + 1],
                                    y_acc[tc4][:, dh * TT:(dh + 1) * TT],
                                    ALU.mult, ALU.add),
                                deps=[ya_ev[tc4]])
                for tc4 in range(4):
                    tcg = ts * 4 + tc4
                    for dc in range(NDC):
                        bt = pb.bank()
                        dd = [ya_ev[tc4]] + list(bt["rd"]) + \
                            ([bt["wr"]] if bt["wr"] else [])
                        te = pb.op("tensor",
                                   lambda t, tc4=tc4, dc=dc, bt=bt:
                                   t.transpose(bt["t"][:, 0:128],
                                               y_acc[tc4][:, dc * 128:(dc + 1) * 128],
                                               cst.ident[:]),
                                   deps=dd, inc=True)
                        bt["wr"] = te
                        bt["rd"] = []
                        prev_tr_last = te
                        zadd_prev[dc] = read_bank(
                            pb, "vector", bt,
                            lambda v, dc=dc, tcg=tcg, bt=bt:
                            v.tensor_tensor(xz[dc][:, tcg * 128:(tcg + 1) * 128],
                                            xz[dc][:, tcg * 128:(tcg + 1) * 128],
                                            bt["t"][:, 0:128], ALU.add),
                            deps=[zadd_prev[dc]])

    with pb.phase():
        ln_stats(pb, cst, lambda dc, tt: (xz[dc], tt * TT, None), mrow, rrow, "s2")
    with pb.phase():
        last = None
        for tt in range(NTT):
            bm, br = bcast_rows(pb, cst, mrow, rrow, tt)
            for dc in range(NDC):
                e1 = read_bank(pb, "vector", bm,
                               lambda v, dc=dc, tt=tt, bm=bm:
                               v.tensor_tensor(scr[0][:],
                                               xz[dc][:, tt * TT:(tt + 1) * TT],
                                               bm["t"][:, 0:TT], ALU.subtract),
                               deps=[last])
                e2 = read_bank(pb, "vector", br,
                               lambda v, br=br:
                               v.tensor_tensor(scr[1][:], scr[0][:],
                                               br["t"][:, 0:TT], ALU.mult),
                               deps=[e1])
                e3 = pb.op("vector",
                           lambda v, dc=dc, tt=tt:
                           v.tensor_scalar(xz[dc][:, tt * TT:(tt + 1) * TT],
                                           scr[1][:],
                                           ln2_c[:, dc:dc + 1],
                                           ln2_c[:, 8 + dc:9 + dc],
                                           ALU.mult, ALU.add),
                           deps=[e2], inc=True)
                last = e3
    with pb.phase():
        ln_stats(pb, cst, lambda dc, tt: (xz[dc], tt * TT, None), mrow, rrow, "s3")

    with pb.scope():
        xs2 = [[pb.sbuf(f"xf{i}_{dc}", [128, TT], F32) for dc in range(NDC)]
               for i in range(2)]
        out_t = [pb.sbuf(f"out_t{i}", [128, TT], F32) for i in range(2)]
        with pb.phase():
            xuse = [None, None]
            ost = [None, None]
            last = None
            for tt in range(NTT):
                ch = f"ld{1 + tt % 2}"
                for dc in range(NDC):
                    pb.dma("sync", ch, xs2[tt % 2][dc][:],
                           x_f[dc * 128:(dc + 1) * 128, tt * TT:(tt + 1) * TT],
                           deps=[xuse[tt % 2]])
                tot = pb.chan_total(ch)
                bm, br = bcast_rows(pb, cst, mrow, rrow, tt)
                for dc in range(NDC):
                    ot = out_t[dc % 2]
                    e1 = read_bank(pb, "vector", bm,
                                   lambda v, dc=dc, tt=tt, bm=bm:
                                   v.tensor_tensor(scr[0][:],
                                                   xz[dc][:, tt * TT:(tt + 1) * TT],
                                                   bm["t"][:, 0:TT], ALU.subtract),
                                   deps=[last])
                    e2 = read_bank(pb, "vector", br,
                                   lambda v, br=br:
                                   v.tensor_tensor(scr[1][:], scr[0][:],
                                                   br["t"][:, 0:TT], ALU.mult),
                                   deps=[e1])
                    e3 = pb.op("vector",
                               lambda v, dc=dc, tt=tt, ot=ot:
                               v.tensor_tensor(ot[:], scr[1][:],
                                               xs2[tt % 2][dc][:], ALU.add),
                               deps=[e2, tot, ost[dc % 2]], inc=True)
                    ost[dc % 2] = pb.dma(
                        "sync", ("stq", "st2")[dc % 2],
                        xo_b[dc * 128:(dc + 1) * 128, tt * TT:(tt + 1) * TT],
                        ot[:], deps=[e3])
                    last = e3
                    xuse[tt % 2] = e3
            pb.waitev("sync", pb.chan_total("stq"), pb.chan_total("st2"))

    return


def emit_l1attn(pb, cst, x_d, xo_d, weights):
    x_f = x_d.bitcast(F32)
    xo_b = xo_d.bitcast(F32) if xo_d.dtype == F32R else xo_d
    lin_hbm = pb.dram_tmp("lin_hbm", [D, L], BF16)
    win_hbm = pb.dram_tmp("win_hbm", [D, L], BF16)
    z_hbm = pb.dram_tmp("z_hbm", [D, L], F32R)

    need_pre(pb, weights, ["l1a_wk_lin", "l1a_wv_lin", "l1a_wq_lin",
                           "l1a_wq_loc", "l1a_wk_loc", "l1a_wv_loc",
                           "l1a_wg1", "l1a_wg2", "l1a_wo"])
    wql_d, wql_ev = pb.pre["l1a_wq_lin"]
    wkl_d, wkl_ev = pb.pre["l1a_wk_lin"]
    wvl_d, wvl_ev = pb.pre["l1a_wv_lin"]
    wqo_d, wqo_ev = pb.pre["l1a_wq_loc"]
    wko_d, wko_ev = pb.pre["l1a_wk_loc"]
    wvo_d, wvo_ev = pb.pre["l1a_wv_loc"]
    wo_d, wol_ev = pb.pre["l1a_wo"]
    wg1_d, wg1_ev = pb.pre["l1a_wg1"]
    wg2_d, wg2_ev = pb.pre["l1a_wg2"]
    bias_d = pb.dram_in("l1a_bias_c", weights["l1a_bias_c"], F32)
    brow_d = pb.dram_in("l1a_brow", weights["l1a_brow"], F32R)
    bt_d = pb.dram_in("l1a_BT", weights["l1a_BT"], F32)

    h = [pb.sbuf(f"h{dc}", [128, L], BF16) for dc in range(NDC)]
    m0row = pb.sbuf("m0row", [1, L], F32)
    r0row = pb.sbuf("r0row", [1, L], F32)
    scr = [pb.sbuf(f"scr{i}", [128, TT], F32) for i in range(4)]
    bias_c = pb.sbuf("bias_c", [128, 64], F32)
    brow = pb.sbuf("brow", [1, 3 * D], BF16)

    with pb.scope():
        betarow = pb.sbuf("betarow", [1, L], F32)
        xs4 = [[pb.sbuf(f"xs{tt}_{dc}", [128, TT], F32R) for dc in range(NDC)]
               for tt in range(NTT)]
        with pb.phase():
            pb.dma("sync", "ld0", bias_c[:], bias_d)
            pb.dma("gpsimd", "gp0", brow[:], brow_d.rearrange("a b -> (a b)").unsqueeze(0))
            for _eng in PB.CENG:
                pb.waitev(_eng, pb.chan_total("ld0"), pb.chan_total("gp0"))
            tot = []
            for tt in range(NTT):
                ch = f"lx{tt}"
                for dc in range(NDC):
                    pb.dma("sync", ch, xs4[tt][dc][:],
                           x_d[dc * 128:(dc + 1) * 128, tt * TT:(tt + 1) * TT])
                tot.append(pb.chan_total(ch))
            ln_stats(pb, cst, lambda dc, tt: (xs4[tt][dc], 0, tot[tt]),
                     m0row, r0row, "sx", beta_row=betarow)
        with pb.phase():
            last = None
            for tt in range(NTT):
                bm, br = bcast_rows(pb, cst, m0row, betarow, tt)
                for dc in range(NDC):
                    e1 = read_bank(pb, "vector", bm,
                                   lambda v, dc=dc, tt=tt, bm=bm:
                                   v.tensor_tensor(scr[0][:], xs4[tt][dc][:],
                                                   bm["t"][:, 0:TT], ALU.subtract),
                                   deps=[last])
                    e2 = read_bank(pb, "vector", br,
                                   lambda v, br=br:
                                   v.tensor_tensor(scr[1][:], scr[0][:],
                                                   br["t"][:, 0:TT], ALU.mult),
                                   deps=[e1])
                    e3 = pb.op("vector",
                               lambda v, dc=dc, tt=tt:
                               v.tensor_scalar(h[dc][:, tt * TT:(tt + 1) * TT],
                                               scr[1][:],
                                               bias_c[:, 48 + dc:49 + dc],
                                               bias_c[:, 56 + dc:57 + dc],
                                               ALU.mult, ALU.add),
                               deps=[e2], inc=True)
                    last = e3

    with pb.scope():
        kv = [pb.sbuf(f"kv{dc}", [128, D], BF16) for dc in range(NDC)]
        ksum_c = pb.sbuf("ksum_c", [128, 8], BF16)
        kv_ev = [[None] * 2 for _ in range(NDC)]
        ksc_ev = None
        with pb.scope():
            kvf = [pb.sbuf(f"kvf{dc}", [128, D], F32) for dc in range(NDC)]
            ktok = [pb.sbuf(f"ktok{i}", [128, D], BF16) for i in range(2)]
            vtok = [pb.sbuf(f"vtok{i}", [128, D], BF16) for i in range(2)]
            wkh = [pb.sbuf(f"wkh{i}", [128, NDC * TT], BF16) for i in range(2)]
            wvh = [pb.sbuf(f"wvh{i}", [128, NDC * TT], BF16) for i in range(2)]
            with pb.phase():
                for half in range(2):
                    pb.dma("sync", f"ld{1 + half}",
                           wkh[half][:].rearrange("p (c f) -> p c f", f=TT),
                           wkl_d[:, half * TT:(half + 1) * TT]
                           .rearrange("(c p) f -> p c f", p=128),
                           deps=wkl_ev)
                    pb.dma("sync", f"ld{3 + half}",
                           wvh[half][:].rearrange("p (c f) -> p c f", f=TT),
                           wvl_d[:, half * TT:(half + 1) * TT]
                           .rearrange("(c p) f -> p c f", p=128),
                           deps=wvl_ev)
                wk_ev = [pb.chan_total("ld1"), pb.chan_total("ld2")]
                wv_ev = [pb.chan_total("ld3"), pb.chan_total("ld4")]
                el_last = {}
                ktok_rd = [None] * 2
                vtok_rd = [None] * 2
                bstat = pb.bank_fixed("stat")
                first_ks = True
                ks_last = None
                for sg in range(8):
                    kt_ev = [None] * 2
                    vt_ev = [None] * 2
                    for sc4 in range(2):
                        scg = sg * 2 + sc4
                        for half in range(2):
                            bk = pb.bank()
                            mms = [(cst.ones_row_b[0:1, :],
                                    brow[0:1, half * TT:(half + 1) * TT], [])]
                            mms += [(h[dc][:, scg * 128:(scg + 1) * 128],
                                     wkh[half][:, dc * TT:(dc + 1) * TT],
                                     [wk_ev[half]]) for dc in range(NDC)]
                            mk = mm_seq(pb, bk, bk["t"][:, 0:TT], mms)
                            e1 = read_bank(pb, "vector", bk,
                                           lambda v, bk=bk:
                                           v.tensor_scalar(scr[0][:],
                                                           bk["t"][:, 0:TT],
                                                           0.0, None, ALU.min),
                                           deps=[el_last.get("s0")])
                            e2 = pb.op("scalar",
                                       lambda s:
                                       s.activation(out=scr[1][:], in_=scr[0][:],
                                                    func=AF.Exp),
                                       deps=[e1, el_last.get("s1")], inc=True)
                            e3 = read_bank(pb, "vector", bk,
                                           lambda v, bk=bk:
                                           v.tensor_scalar(scr[2][:],
                                                           bk["t"][:, 0:TT],
                                                           0.0, None, ALU.max),
                                           deps=[el_last.get("s2")])
                            e4 = pb.op("vector",
                                       lambda v, sc4=sc4, half=half:
                                       v.tensor_tensor(
                                           ktok[sc4][:, half * TT:(half + 1) * TT],
                                           scr[1][:], scr[2][:], ALU.add),
                                       deps=[e2, e3, ktok_rd[sc4]], inc=True)
                            el_last["s0"] = e4
                            el_last["s1"] = e4
                            el_last["s2"] = e4
                            kt_ev[sc4] = e4
                            bv = pb.bank()
                            mms = [(cst.ones_row_b[0:1, :],
                                    brow[0:1, D + half * TT:D + (half + 1) * TT],
                                    [])]
                            mms += [(h[dc][:, scg * 128:(scg + 1) * 128],
                                     wvh[half][:, dc * TT:(dc + 1) * TT],
                                     [wv_ev[half]]) for dc in range(NDC)]
                            mv = mm_seq(pb, bv, bv["t"][:, 0:TT], mms)
                            vt_ev[sc4] = read_bank(
                                pb, "vector", bv,
                                lambda v, sc4=sc4, half=half, bv=bv:
                                v.tensor_copy(
                                    vtok[sc4][:, half * TT:(half + 1) * TT],
                                    bv["t"][:, 0:TT]),
                                deps=[vtok_rd[sc4]])
                        for dc in range(NDC):
                            dd = [kt_ev[sc4]]
                            if first_ks:
                                dd += list(bstat["rd"])
                                if bstat["wr"] is not None:
                                    dd.append(bstat["wr"])
                            ks_last = pb.op(
                                "tensor",
                                lambda t, sc4=sc4, dc=dc, f=first_ks:
                                t.matmul(bstat["t"][:, dc:dc + 1],
                                         ktok[sc4][:, dc * 128:(dc + 1) * 128],
                                         cst.ones_col_b[:], start=f, stop=False,
                                         skip_group_check=True),
                                deps=dd, inc=(dc == NDC - 1))
                            first_ks = False
                    mkv = None
                    for eh in range(2):
                        for dc in range(NDC):
                            bc = pb.bank()
                            mms = [(ktok[sc4][:, dc * 128:(dc + 1) * 128],
                                    vtok[sc4][:, eh * TT:(eh + 1) * TT],
                                    [kt_ev[sc4], vt_ev[sc4]])
                                   for sc4 in range(2)]
                            mkv = mm_seq(pb, bc, bc["t"][:, 0:TT], mms)
                            if sg == 0:
                                kv_ev[dc][eh] = read_bank(
                                    pb, "vector", bc,
                                    lambda v, dc=dc, eh=eh, bc=bc:
                                    v.tensor_copy(
                                        kvf[dc][:, eh * TT:(eh + 1) * TT],
                                        bc["t"][:, 0:TT]),
                                    deps=[kv_ev[dc][eh]])
                            else:
                                kv_ev[dc][eh] = read_bank(
                                    pb, "vector", bc,
                                    lambda v, dc=dc, eh=eh, bc=bc:
                                    v.tensor_tensor(
                                        kvf[dc][:, eh * TT:(eh + 1) * TT],
                                        kvf[dc][:, eh * TT:(eh + 1) * TT],
                                        bc["t"][:, 0:TT], ALU.add),
                                    deps=[kv_ev[dc][eh]])
                    for sc4 in range(2):
                        ktok_rd[sc4] = mkv
                        vtok_rd[sc4] = mkv
                bstat["wr"] = ks_last
                bstat["rd"] = []
                ksc_ev = read_bank(pb, "vector", bstat,
                                   lambda v: v.tensor_copy(ksum_c[:],
                                                           bstat["t"][:, 0:8]))
                for dc in range(NDC):
                    cev = pb.op("vector",
                                lambda v, dc=dc: v.tensor_copy(kv[dc][:],
                                                               kvf[dc][:]),
                                deps=[kv_ev[dc][0], kv_ev[dc][1]], inc=True)
                    kv_ev[dc][0] = cev
                    kv_ev[dc][1] = cev

        with pb.scope():
            wqh = [pb.sbuf(f"wqh{i}", [128, NDC * TT], BF16) for i in range(2)]
            qt_ = [pb.sbuf(f"qt{dc}", [128, TT], BF16) for dc in range(NDC)]
            den_r = pb.sbuf("den_r", [1, TT], F32)
            rcb = pb.sbuf("rcb", [128, TT], F32)
            lout = [pb.sbuf(f"lout{i}", [128, TT], BF16) for i in range(2)]
            with pb.phase():
                for half in range(2):
                    pb.dma("sync", f"ld{1 + half}",
                           wqh[half][:].rearrange("p (c f) -> p c f", f=TT),
                           wql_d[:, half * TT:(half + 1) * TT]
                           .rearrange("(c p) f -> p c f", p=128),
                           deps=wql_ev)
                wq_ev = [pb.chan_total("ld1"), pb.chan_total("ld2")]
                el_last = {}
                q_rd = [None] * NDC
                den_last = None
                rcb_last = None
                lin_st = [None, None]
                for tt in range(NTT):
                    q_ev = [None] * NDC
                    for do in range(NDC):
                        bq = pb.bank()
                        mms = [(wqh[do // 4][:, dc * TT + (do % 4) * 128:
                                            dc * TT + (do % 4) * 128 + 128],
                                h[dc][:, tt * TT:(tt + 1) * TT],
                                [wq_ev[do // 4]]) for dc in range(NDC)]
                        mq = mm_seq(pb, bq, bq["t"][:, 0:TT], mms)
                        e1 = read_bank(pb, "vector", bq,
                                       lambda v, bq=bq, do=do:
                                       v.tensor_scalar(scr[0][:], bq["t"][:, 0:TT],
                                                       bias_c[:, do:do + 1], 0.0,
                                                       ALU.add, ALU.min),
                                       deps=[el_last.get("s0")])
                        e2 = pb.op("scalar",
                                   lambda s: s.activation(out=scr[1][:],
                                                          in_=scr[0][:],
                                                          func=AF.Exp),
                                   deps=[e1, el_last.get("s1")], inc=True)
                        e3 = read_bank(pb, "vector", bq,
                                       lambda v, bq=bq, do=do:
                                       v.tensor_scalar(scr[2][:], bq["t"][:, 0:TT],
                                                       bias_c[:, do:do + 1], 0.0,
                                                       ALU.add, ALU.max),
                                       deps=[el_last.get("s2")])
                        e4 = pb.op("vector",
                                   lambda v, do=do:
                                   v.tensor_tensor(qt_[do][:], scr[1][:],
                                                   scr[2][:], ALU.add),
                                   deps=[e2, e3, q_rd[do]], inc=True)
                        el_last["s0"] = e4
                        el_last["s1"] = e4
                        el_last["s2"] = e4
                        q_ev[do] = e4
                    bden = pb.bank()
                    mms = [(ksum_c[:, dc:dc + 1], qt_[dc][:],
                            [q_ev[dc], ksc_ev]) for dc in range(NDC)]
                    mden = mm_seq(pb, bden, bden["t"][0:1, 0:TT], mms)
                    d1 = read_bank(pb, "vector", bden,
                                   lambda v, bden=bden:
                                   v.tensor_scalar(den_r[0:1, :],
                                                   bden["t"][0:1, 0:TT],
                                                   1e-6, None, ALU.add),
                                   deps=[den_last])
                    d2 = pb.op("vector",
                               lambda v: v.reciprocal(den_r[0:1, :],
                                                      den_r[0:1, :]),
                               deps=[d1], inc=True)
                    den_last = d2
                    brcb = pb.bank()
                    mm_seq(pb, brcb, brcb["t"][:, 0:TT],
                           [(cst.ones_row[0:1, :], den_r[0:1, :], [d2])])
                    rc_ev = read_bank(pb, "vector", brcb,
                                      lambda v, brcb=brcb:
                                      v.tensor_copy(rcb[:], brcb["t"][:, 0:TT]),
                                      deps=[rcb_last])
                    rcb_last = rc_ev
                    for ec in range(NDC):
                        bn = pb.bank()
                        mms = [(kv[dc][:, ec * 128:(ec + 1) * 128], qt_[dc][:],
                                [q_ev[dc], kv_ev[dc][ec // 4]])
                               for dc in range(NDC)]
                        mn = mm_seq(pb, bn, bn["t"][:, 0:TT], mms)
                        q_rd[ec] = mn
                        lo = lout[ec % 2]
                        le = read_bank(pb, "vector", bn,
                                       lambda v, bn=bn, lo=lo:
                                       v.tensor_tensor(lo[:], bn["t"][:, 0:TT],
                                                       rcb[:], ALU.mult),
                                       deps=[rc_ev, lin_st[ec % 2]])
                        lin_st[ec % 2] = pb.dma(
                            "sync", ("stq", "st2")[ec % 2],
                            lin_hbm[ec * 128:(ec + 1) * 128,
                                    tt * TT:(tt + 1) * TT],
                            lo[:], deps=[le])

    lin_done = [pb.chan_total("stq"), pb.chan_total("st2")]
    with pb.scope():
        expbt = pb.sbuf("expbt", [128, 16 * 384], BF16)
        expbt_f = pb.sbuf("expbt_f", [128, 384], F32)
        qloc = [pb.sbuf(f"qloc{dc}", [128, TT], BF16) for dc in range(NDC)]
        kspan = [pb.sbuf(f"kspan{dc}", [128, 768], BF16) for dc in range(NDC)]
        v66 = [pb.sbuf(f"v66_{ci}", [128, 16 * 66], BF16) for ci in range(6)]
        exps = pb.sbuf("exps", [128, 384], F32)
        expT = pb.sbuf("expT", [128, 384], BF16)
        srow = pb.sbuf("srow", [1, 128], F32)
        rcw = pb.sbuf("rcw", [128, 128], F32)
        win_t = [pb.sbuf(f"win_t{dc}", [128, TT], BF16) for dc in range(NDC)]
        wcol = [pb.sbuf(f"wcol{i}", [128, D], BF16) for i in range(2)]
        wvh2 = pb.sbuf("wvh2", [128, NDC * TT], BF16)
        with pb.phase():
            for _eng in PB.CENG:
                pb.waitev(_eng, *lin_done)
            bts = None
            for hh in range(16):
                ld = pb.dma("sync", "ld1", expbt_f[:],
                            bt_d[:, hh * 384:(hh + 1) * 384], deps=[bts])
                bts = pb.op("scalar",
                            lambda s, hh=hh:
                            s.activation(out=expbt[:, hh * 384:(hh + 1) * 384],
                                         in_=expbt_f[:], func=AF.Exp),
                            deps=[pb.chan_total("ld1")], inc=True)
            msz = []
            for ci in range(6):
                msz.append(pb.op("vector",
                                 lambda v, ci=ci:
                                 v.memset(v66[ci][:].bitcast(F32), 0.0),
                                 inc=True))
            ones_ev = None
            for ci in range(6):
                for hh in range(16):
                    ones_ev = pb.op(
                        "vector",
                        lambda v, ci=ci, hh=hh:
                        v.memset(v66[ci][:, hh * 66 + 64:hh * 66 + 65], 1.0),
                        deps=[msz[ci]], inc=True)
        with pb.phase():
            q_rd = [None] * NDC
            k_rd = [None] * NDC
            v_rd = [None] * 6
            w_rd = [None, None]
            wv_rd = None
            exps_rd = None
            expT_rd = None
            sr_rd = None
            rc_rd = None
            win_rd = {}
            win_st = [None, None]
            for tile in range(NTT):
                t0 = tile * TT
                for do in range(NDC):
                    wev = pb.dma("sync", f"ld{1 + do % 2}",
                                 wcol[do % 2][:].rearrange("p (c f) -> p c f",
                                                           f=128),
                                 wqo_d[:, do * 128:(do + 1) * 128]
                                 .rearrange("(c p) f -> p c f", p=128),
                                 deps=[w_rd[do % 2]] + wqo_ev)
                    bq = pb.bank()
                    mms = [(wcol[do % 2][:, dc * 128:(dc + 1) * 128],
                            h[dc][:, t0:t0 + TT], [wev]) for dc in range(NDC)]
                    mq = mm_seq(pb, bq, bq["t"][:, 0:TT], mms)
                    q_rd[do] = read_bank(
                        pb, "vector", bq,
                        lambda v, do=do, bq=bq:
                        v.tensor_scalar(qloc[do][:], bq["t"][:, 0:TT],
                                        bias_c[:, 8 + do:9 + do], 0.125,
                                        ALU.add, ALU.mult),
                        deps=[q_rd[do]])
                    wev2 = pb.dma("sync", f"ld{1 + do % 2}",
                                  wcol[do % 2][:].rearrange("p (c f) -> p c f",
                                                            f=128),
                                  wko_d[:, do * 128:(do + 1) * 128]
                                  .rearrange("(c p) f -> p c f", p=128),
                                  deps=[mq] + wko_ev)
                    if tile == 0:
                        segs = [(128, 640), (640, 768)]
                    elif tile == NTT - 1:
                        segs = [(0, 512), (512, 640)]
                    else:
                        segs = [(0, 512), (512, 768)]
                    mse = []
                    if tile == 0:
                        mse.append(pb.op("vector",
                                         lambda v, do=do:
                                         v.memset(kspan[do][:, 0:128]
                                                  .bitcast(F32), 0.0),
                                         deps=[k_rd[do]], inc=True))
                    if tile == NTT - 1:
                        mse.append(pb.op("vector",
                                         lambda v, do=do:
                                         v.memset(kspan[do][:, 640:768]
                                                  .bitcast(F32), 0.0),
                                         deps=[k_rd[do]], inc=True))
                    mk = None
                    for (a, b) in segs:
                        bkk = pb.bank()
                        mms = [(wcol[do % 2][:, dc * 128:(dc + 1) * 128],
                                h[dc][:, t0 - 128 + a:t0 - 128 + b], [wev2])
                               for dc in range(NDC)]
                        mk = mm_seq(pb, bkk, bkk["t"][:, 0:b - a], mms)
                        k_rd[do] = read_bank(
                            pb, "vector", bkk,
                            lambda v, do=do, a=a, b=b, bkk=bkk:
                            v.tensor_scalar(kspan[do][:, a:b],
                                            bkk["t"][:, 0:b - a],
                                            bias_c[:, 16 + do:17 + do],
                                            None, ALU.add),
                            deps=[k_rd[do]] + mse)
                    w_rd[do % 2] = mk
                for half in range(2):
                    wvev = pb.dma("sync", "ld3",
                                  wvh2[:].rearrange("p (c f) -> p c f", f=TT),
                                  wvo_d[:, half * TT:(half + 1) * TT]
                                  .rearrange("(c p) f -> p c f", p=128),
                                  deps=[wv_rd] + wvo_ev)
                    mv = None
                    for ci in range(6):
                        g = tile * 4 - 1 + ci
                        if g < 0 or g >= NTC:
                            continue
                        bv = pb.bank()
                        mms = [(cst.ones_row_b[0:1, :],
                                brow[0:1, 2 * D + half * TT:
                                     2 * D + (half + 1) * TT], [])]
                        mms += [(h[dc][:, g * 128:(g + 1) * 128],
                                 wvh2[:, dc * TT:(dc + 1) * TT], [wvev])
                                for dc in range(NDC)]
                        mv = mm_seq(pb, bv, bv["t"][:, 0:TT], mms)
                        for hh in range(half * 8, half * 8 + 8):
                            v_rd[ci] = read_bank(
                                pb, "vector", bv,
                                lambda v, ci=ci, hh=hh, half=half, bv=bv:
                                v.tensor_copy(
                                    v66[ci][:, hh * 66:hh * 66 + 64],
                                    bv["t"][:, (hh - half * 8) * 64:
                                            (hh - half * 8) * 64 + 64]),
                                deps=[v_rd[ci], ones_ev])
                    wv_rd = mv
                if tile == NTT - 1:
                    mzero = pb.op("vector",
                                  lambda v: v.memset(v66[5][:].bitcast(F32), 0.0),
                                  deps=[v_rd[5]], inc=True)
                    v_rd[5] = mzero
                for qc in range(4):
                    for hh in range(16):
                        hc, hb = hh // 2, (hh % 2) * 64
                        bs = pb.bank()
                        first = True
                        msc = None
                        for c in range(3):
                            dd = [q_rd[hc], k_rd[hc]]
                            if first:
                                dd += list(bs["rd"]) + \
                                    ([bs["wr"]] if bs["wr"] else [])
                            msc = pb.op(
                                "tensor",
                                lambda t, hc=hc, hb=hb, qc=qc, c=c, bs=bs,
                                first=first:
                                t.matmul(bs["t"][:, c * 128:(c + 1) * 128],
                                         kspan[hc][hb:hb + 64,
                                                   qc * 128 + c * 128:
                                                   qc * 128 + (c + 1) * 128],
                                         qloc[hc][hb:hb + 64,
                                                  qc * 128:(qc + 1) * 128],
                                         start=first, stop=(c == 2),
                                         skip_group_check=True),
                                deps=dd, inc=(c == 2))
                            first = False
                        bs["wr"] = msc
                        bs["rd"] = []
                        ee = read_bank(pb, "scalar", bs,
                                       lambda s, bs=bs:
                                       s.activation(out=exps[:],
                                                    in_=bs["t"][:, 0:384],
                                                    func=AF.Exp),
                                       deps=[exps_rd])
                        em = pb.op("vector",
                                   lambda v, hh=hh:
                                   v.tensor_tensor(expT[:], exps[:],
                                                   expbt[:, hh * 384:
                                                         (hh + 1) * 384],
                                                   ALU.mult),
                                   deps=[ee, expT_rd], inc=True)
                        exps_rd = em
                        bw = pb.bank()
                        mw = None
                        first = True
                        for c in range(3):
                            ci = qc + c
                            dd = [em, v_rd[ci]]
                            if first:
                                dd += list(bw["rd"]) + \
                                    ([bw["wr"]] if bw["wr"] else [])
                            mw = pb.op(
                                "tensor",
                                lambda t, ci=ci, hh=hh, c=c, bw=bw, first=first:
                                t.matmul(bw["t"][0:66, 0:128],
                                         v66[ci][:, hh * 66:hh * 66 + 66],
                                         expT[:, c * 128:(c + 1) * 128],
                                         start=first, stop=(c == 2),
                                         skip_group_check=True),
                                deps=dd, inc=(c == 2))
                            first = False
                        expT_rd = mw
                        bw["wr"] = mw
                        bw["rd"] = []
                        s1 = read_bank(pb, "vector", bw,
                                       lambda v, bw=bw:
                                       v.tensor_copy(srow[0:1, :],
                                                     bw["t"][64:65, 0:128]),
                                       deps=[sr_rd])
                        s2 = pb.op("vector",
                                   lambda v: v.reciprocal(srow[0:1, :],
                                                          srow[0:1, :]),
                                   deps=[s1], inc=True)
                        sr_rd = s2
                        brr = pb.bank()
                        mm_seq(pb, brr, brr["t"][0:64, 0:128],
                               [(cst.ones_row[0:1, 0:64], srow[0:1, :], [s2])])
                        r3 = read_bank(pb, "vector", brr,
                                       lambda v, brr=brr:
                                       v.tensor_copy(rcw[0:64, :],
                                                     brr["t"][0:64, 0:128]),
                                       deps=[rc_rd])
                        rc_rd = r3
                        wkey = (hc, hb)
                        wv4 = read_bank(
                            pb, "vector", bw,
                            lambda v, hc=hc, hb=hb, qc=qc, bw=bw:
                            v.tensor_tensor(
                                win_t[hc][hb:hb + 64,
                                          qc * 128:(qc + 1) * 128],
                                bw["t"][0:64, 0:128], rcw[0:64, :],
                                ALU.mult),
                            deps=[r3, win_rd.get(wkey)])
                        win_rd[wkey] = wv4
                wst_evs = list(win_rd.values())
                for dc in range(NDC):
                    win_st[dc % 2] = pb.dma(
                        "sync", ("stq", "st2")[dc % 2],
                        win_hbm[dc * 128:(dc + 1) * 128, t0:t0 + TT],
                        win_t[dc][:], deps=wst_evs + [win_st[dc % 2]])
                for kkey in list(win_rd):
                    win_rd[kkey] = win_st[kkey[0] % 2]

    win_done = [pb.chan_total("stq"), pb.chan_total("st2")]
    mzrow = pb.sbuf("mzrow", [1, L], F32)
    rzrow = pb.sbuf("rzrow", [1, L], F32)
    with pb.scope():
        lin_s = [[pb.sbuf(f"lins{i}_{c}", [128, TT], BF16) for c in range(NDC)]
                 for i in range(2)]
        win_s = [[pb.sbuf(f"wins{i}_{c}", [128, TT], BF16) for c in range(NDC)]
                 for i in range(2)]
        xs_s = [[pb.sbuf(f"xss{i}_{c}", [128, TT], F32) for c in range(NDC)]
                for i in range(2)]
        g1 = [pb.sbuf(f"g1_{dc}", [128, TT], BF16) for dc in range(NDC)]
        outc = [pb.sbuf(f"outc{dc}", [128, TT], BF16) for dc in range(NDC)]
        wg1b = [pb.sbuf(f"wg1b{i}", [128, 24 * 128], BF16) for i in range(2)]
        wcol2 = [pb.sbuf(f"wcol2_{i}", [128, D], BF16) for i in range(2)]
        gate_t = pb.sbuf("gate_t", [128, TT], F32)
        z_t = [pb.sbuf(f"z_t{i}", [128, TT], F32R) for i in range(2)]
        sqz = pb.sbuf("sqz", [128, TT], F32R)
        zvrow = pb.sbuf("zvrow", [1, TT], F32)
        with pb.phase():
            for _eng in PB.CENG:
                pb.waitev(_eng, *win_done)
            luse = [None, None]
            w_rd = [None, None]
            w2_rd = [None, None]
            g1_rd = [None] * NDC
            gate_rd = None
            outc_rd = [None] * NDC
            sq_rd = None
            z_st = [None, None]
            prev_small = []
            cmb_last = None
            for tt in range(NTT):
                ch = f"ld{1 + tt % 2}"
                for c in range(NDC):
                    pb.dma("sync", ch, lin_s[tt % 2][c][:],
                           lin_hbm[c * 128:(c + 1) * 128, tt * TT:(tt + 1) * TT],
                           deps=[luse[tt % 2]] + lin_done)
                    pb.dma("sync", ch, win_s[tt % 2][c][:],
                           win_hbm[c * 128:(c + 1) * 128, tt * TT:(tt + 1) * TT],
                           deps=[luse[tt % 2]] + win_done)
                    pb.dma("sync", ch, xs_s[tt % 2][c][:],
                           x_f[c * 128:(c + 1) * 128, tt * TT:(tt + 1) * TT],
                           deps=[luse[tt % 2]])
                ltot = pb.chan_total(ch)
                for do in range(NDC):
                    wev = pb.dma("sync", f"ld{3 + do % 2}",
                                 wg1b[do % 2][:].rearrange("p (c f) -> p c f",
                                                           f=128),
                                 wg1_d[:, do * 128:(do + 1) * 128]
                                 .rearrange("(c p) f -> p c f", p=128),
                                 deps=[w_rd[do % 2]] + wg1_ev)
                    bg = pb.bank()
                    mms = []
                    for dc in range(NDC):
                        mms.append((wg1b[do % 2][:, dc * 128:(dc + 1) * 128],
                                    lin_s[tt % 2][dc][:], [wev, ltot]))
                    for dc in range(NDC):
                        mms.append((wg1b[do % 2][:, (8 + dc) * 128:
                                                (9 + dc) * 128],
                                    win_s[tt % 2][dc][:], [wev, ltot]))
                    for dc in range(NDC):
                        mms.append((wg1b[do % 2][:, (16 + dc) * 128:
                                                (17 + dc) * 128],
                                    h[dc][:, tt * TT:(tt + 1) * TT], [wev]))
                    mg = mm_seq(pb, bg, bg["t"][:, 0:TT], mms)
                    w_rd[do % 2] = mg
                    g1_rd[do] = read_bank(
                        pb, "vector", bg,
                        lambda v, do=do, bg=bg:
                        v.tensor_scalar(g1[do][:], bg["t"][:, 0:TT],
                                        bias_c[:, 24 + do:25 + do], 0.0,
                                        ALU.add, ALU.max),
                        deps=[g1_rd[do]])
                bstat = pb.bank_fixed("stat")
                bqstat = pb.bank_fixed("c3")
                first_stat = True
                last_me = None
                for do in range(NDC):
                    wev = pb.dma("sync", ("ld5", "ld0")[do % 2],
                                 wcol2[do % 2][:].rearrange("p (c f) -> p c f",
                                                            f=128),
                                 wg2_d[:, do * 128:(do + 1) * 128]
                                 .rearrange("(c p) f -> p c f", p=128),
                                 deps=[w2_rd[do % 2]] + wg2_ev)
                    bgt = pb.bank()
                    mms = [(wcol2[do % 2][:, dc * 128:(dc + 1) * 128],
                            g1[dc][:], [wev, g1_rd[dc]]) for dc in range(NDC)]
                    mgt = mm_seq(pb, bgt, bgt["t"][:, 0:TT], mms)
                    w2_rd[do % 2] = mgt
                    eg = read_bank(pb, "scalar", bgt,
                                   lambda s, do=do, bgt=bgt:
                                   s.activation(out=gate_t[:],
                                                in_=bgt["t"][:, 0:TT],
                                                func=AF.Sigmoid,
                                                bias=bias_c[:, 32 + do:33 + do],
                                                scale=1.0),
                                   deps=[gate_rd])
                    c1 = pb.op("vector",
                               lambda v, do=do, tt=tt:
                               v.tensor_tensor(scr[0][:], lin_s[tt % 2][do][:],
                                               win_s[tt % 2][do][:],
                                               ALU.subtract),
                               deps=[ltot, cmb_last], inc=True)
                    c2 = pb.op("vector",
                               lambda v: v.tensor_tensor(scr[1][:], scr[0][:],
                                                         gate_t[:], ALU.mult),
                               deps=[c1, eg], inc=True)
                    gate_rd = c2
                    c3 = pb.op("vector",
                               lambda v, do=do, tt=tt:
                               v.tensor_tensor(outc[do][:], scr[1][:],
                                               win_s[tt % 2][do][:], ALU.add),
                               deps=[c2, outc_rd[do]], inc=True)
                    outc_rd[do] = c3
                    cmb_last = c3
                for do in range(NDC):
                    wev = pb.dma("sync", ("ld5", "ld0")[do % 2],
                                 wcol2[do % 2][:].rearrange("p (c f) -> p c f",
                                                            f=128),
                                 wo_d[:, do * 128:(do + 1) * 128]
                                 .rearrange("(c p) f -> p c f", p=128),
                                 deps=[w2_rd[do % 2], w_rd[do % 2]] + wol_ev)
                    bz = pb.bank()
                    mms = [(wcol2[do % 2][:, dc * 128:(dc + 1) * 128],
                            outc[dc][:], [wev, outc_rd[dc]])
                           for dc in range(NDC)]
                    mz = mm_seq(pb, bz, bz["t"][:, 0:TT], mms)
                    w2_rd[do % 2] = mz
                    w_rd[do % 2] = mz
                    zt = z_t[do % 2]
                    bmu, bru = bcast_rows(pb, cst, m0row, r0row, tt)
                    u1 = read_bank(pb, "vector", bmu,
                                   lambda v, do=do, tt=tt, bmu=bmu:
                                   v.tensor_tensor(scr[2][:],
                                                   xs_s[tt % 2][do][:],
                                                   bmu["t"][:, 0:TT],
                                                   ALU.subtract),
                                   deps=[ltot, z_st[do % 2]])
                    u2 = read_bank(pb, "vector", bru,
                                   lambda v, bru=bru:
                                   v.tensor_tensor(scr[3][:], scr[2][:],
                                                   bru["t"][:, 0:TT], ALU.mult),
                                   deps=[u1])
                    z1 = read_bank(pb, "vector", bz,
                                   lambda v, do=do, zt=zt, bz=bz:
                                   v.scalar_tensor_tensor(
                                       zt[:], bz["t"][:, 0:TT],
                                       bias_c[:, 40 + do:41 + do],
                                       scr[3][:], ALU.add, ALU.add),
                                   deps=[u2, z_st[do % 2]])
                    zs = pb.dma("sync", ("stq", "st2")[do % 2],
                                z_hbm[do * 128:(do + 1) * 128,
                                      tt * TT:(tt + 1) * TT],
                                zt[:], deps=[z1])
                    z_st[do % 2] = zs
                    sev = pb.op("scalar",
                                lambda s, zt=zt: s.activation(out=sqz[:],
                                                              in_=zt[:],
                                                              func=AF.Square),
                                deps=[z1, sq_rd], inc=True)
                    dd = [z1]
                    if first_stat:
                        dd += list(bstat["rd"])
                        if bstat["wr"] is not None:
                            dd.append(bstat["wr"])
                    pb.op("tensor",
                          lambda t, zt=zt, f=first_stat, l=(do == NDC - 1):
                          t.matmul(bstat["t"][0:1, 0:TT], cst.ones_col_r[:],
                                   zt[:], start=f, stop=l,
                                   skip_group_check=True),
                          deps=dd)
                    dd2 = [sev]
                    if first_stat:
                        dd2 += list(bqstat["rd"])
                        if bqstat["wr"] is not None:
                            dd2.append(bqstat["wr"])
                    first_stat = False
                    last_me = pb.op(
                        "tensor",
                        lambda t, f2=(do == 0), l2=(do == NDC - 1):
                        t.matmul(bqstat["t"][0:1, 0:TT],
                                 cst.ones_col_r[:], sqz[:],
                                 start=f2, stop=l2,
                                 skip_group_check=True),
                        deps=dd2, inc=True)
                    sq_rd = last_me
                bstat["wr"] = last_me
                bqstat["wr"] = last_me
                bstat["rd"] = []
                bqstat["rd"] = []
                c0 = tt * TT
                e1 = read_bank(pb, "vector", bstat,
                               lambda v, c0=c0:
                               v.tensor_scalar(mzrow[0:1, c0:c0 + TT],
                                               bstat["t"][0:1, 0:TT],
                                               1.0 / D, None, ALU.mult),
                               deps=prev_small)
                e2 = read_bank(pb, "vector", bqstat,
                               lambda v, c0=c0:
                               v.tensor_scalar(zvrow[0:1, :],
                                               bqstat["t"][0:1, 0:TT],
                                               1.0 / D, None, ALU.mult),
                               deps=prev_small + [e1])
                e3b = pb.op("scalar",
                            lambda s, c0=c0:
                            s.activation(out=rzrow[0:1, c0:c0 + TT],
                                         in_=mzrow[0:1, c0:c0 + TT],
                                         func=AF.Square),
                            deps=[e1], inc=True)
                e4 = pb.op("vector",
                           lambda v, c0=c0:
                           v.tensor_tensor(zvrow[0:1, :], zvrow[0:1, :],
                                           rzrow[0:1, c0:c0 + TT],
                                           ALU.subtract),
                           deps=[e2, e3b], inc=True)
                e5 = pb.op("scalar",
                           lambda s, c0=c0:
                           s.activation(out=rzrow[0:1, c0:c0 + TT],
                                        in_=zvrow[0:1, :], func=AF.Sqrt,
                                        bias=cst.cpool[0:1, C_EPS:C_EPS + 1],
                                        scale=1.0),
                           deps=[e4], inc=True)
                e6 = pb.op("vector",
                           lambda v, c0=c0:
                           v.reciprocal(rzrow[0:1, c0:c0 + TT],
                                        rzrow[0:1, c0:c0 + TT]),
                           deps=[e5], inc=True)
                prev_small = [e6]
                luse[tt % 2] = z1

    z_done = [pb.chan_total("stq"), pb.chan_total("st2")]
    with pb.scope():
        zs2 = [[pb.sbuf(f"zs2{i}_{c}", [128, TT], F32R) for c in range(NDC)]
               for i in range(2)]
        xs2 = [[pb.sbuf(f"xs2{i}_{c}", [128, TT], F32) for c in range(NDC)]
               for i in range(2)]
        out_t = [pb.sbuf(f"outt{i}", [128, TT], F32) for i in range(2)]
        with pb.phase():
            for _eng in PB.CENG:
                pb.waitev(_eng, *z_done)
            xuse = [None, None]
            ost = [None, None]
            last = None
            for tt in range(NTT):
                ch = f"ld{1 + tt % 2}"
                for dc in range(NDC):
                    pb.dma("sync", ch, zs2[tt % 2][dc][:],
                           z_hbm[dc * 128:(dc + 1) * 128, tt * TT:(tt + 1) * TT],
                           deps=[xuse[tt % 2]] + z_done)
                    pb.dma("sync", ch, xs2[tt % 2][dc][:],
                           x_f[dc * 128:(dc + 1) * 128, tt * TT:(tt + 1) * TT],
                           deps=[xuse[tt % 2]])
                tot = pb.chan_total(ch)
                bm, br = bcast_rows(pb, cst, mzrow, rzrow, tt)
                for dc in range(NDC):
                    ot = out_t[dc % 2]
                    e1 = read_bank(pb, "vector", bm,
                                   lambda v, dc=dc, tt=tt, bm=bm:
                                   v.tensor_tensor(scr[0][:],
                                                   zs2[tt % 2][dc][:],
                                                   bm["t"][:, 0:TT],
                                                   ALU.subtract),
                                   deps=[last, tot])
                    e2 = read_bank(pb, "vector", br,
                                   lambda v, br=br:
                                   v.tensor_tensor(scr[1][:], scr[0][:],
                                                   br["t"][:, 0:TT], ALU.mult),
                                   deps=[e1])
                    e3 = pb.op("vector",
                               lambda v, dc=dc, tt=tt, ot=ot:
                               v.tensor_tensor(ot[:], scr[1][:],
                                               xs2[tt % 2][dc][:], ALU.add),
                               deps=[e2, ost[dc % 2]], inc=True)
                    ost[dc % 2] = pb.dma(
                        "sync", ("stq", "st2")[dc % 2],
                        xo_b[dc * 128:(dc + 1) * 128, tt * TT:(tt + 1) * TT],
                        ot[:], deps=[e3])
                    last = e3
                    xuse[tt % 2] = e3
            pb.waitev("sync", pb.chan_total("stq"), pb.chan_total("st2"))

    return


def emit_l0attn(pb, cst, x_d, xo_d, weights):
    x_f = x_d.bitcast(F32)
    xo_b = xo_d.bitcast(F32) if xo_d.dtype == F32R else xo_d
    rv_hbm = pb.dram_tmp("rv_hbm", [D, L], BF16)
    mv_hbm = pb.dram_tmp("mv_hbm", [D, L], BF16)
    rq_hbm = pb.dram_tmp("rq_hbm", [D, L], BF16)
    mq_hbm = pb.dram_tmp("mq_hbm", [D, L], BF16)
    pre_hbm = pb.dram_tmp("pre_hbm", [D, L], F32R)
    rvp_hbm = pb.dram_tmp("rvp_hbm", [L, D], BF16)
    mvp_hbm = pb.dram_tmp("mvp_hbm", [L, D], BF16)
    cph_hbm = pb.dram_tmp("cph_hbm", [D, L], F32R)
    y_hbm = pb.dram_tmp("y_hbm", [D, L], F32R)

    need_pre(pb, weights, ["l0a_rqT", "l0a_mqT", "l0a_combT",
                           "l0a_rvT", "l0a_mvT"])
    wvalT_d, wval_ev = pb.pre["l0a_rvT"]
    wrqT_d, wrq_ev = pb.pre["l0a_rqT"]
    wmvT_d, wmv_ev = pb.pre["l0a_mvT"]
    wmqT_d, wmq_ev = pb.pre["l0a_mqT"]
    combT_d, comb_ev = pb.pre["l0a_combT"]
    wsmall_d = pb.dram_in("l0a_wsmall", weights["l0a_wsmall"], F32R)
    bsmall_d = pb.dram_in("l0a_bsmall", weights["l0a_bsmall"], F32R)
    bias_d = pb.dram_in("l0a_bias_c", weights["l0a_bias_c"], F32)
    dcol_d = pb.dram_in("l0a_dcol", weights["l0a_dcol"], F32)
    m4_d = pb.dram_in("l0a_m4", weights["l0a_m4"], F32)

    SC = 1.0 / 32.0

    h = [pb.sbuf(f"h{dc}", [128, L], F32R) for dc in range(NDC)]
    m0row = pb.sbuf("m0row", [1, L], F32)
    r0row = pb.sbuf("r0row", [1, L], F32)
    scr = [pb.sbuf(f"scr{i}", [128, TT], F32) for i in range(4)]
    bias_c = pb.sbuf("bias_c", [128, 56], F32)
    dcol = pb.sbuf("dcol", [128, 32], F32)
    m4 = pb.sbuf("m4", [128, 4 * TT], F32)
    w_r_all = pb.sbuf("w_r_all", [128, 16 * 8], F32)
    wm_all = pb.sbuf("wm_all", [128, 16 * 16], F32)

    with pb.scope():
        xs4 = [[pb.sbuf(f"xs{tt}_{dc}", [128, TT], F32R) for dc in range(NDC)]
               for tt in range(NTT)]
        wsmall = None
        bsmall = None
        with pb.phase():
            pb.dma("sync", "ld0", bias_c[:], bias_d)
            pb.dma("sync", "ld0", dcol[:], dcol_d)
            pb.dma("sync", "ld0", m4[:], m4_d)
            for _eng in PB.CENG:
                pb.waitev(_eng, pb.chan_total("ld0"))
            tot = []
            for tt in range(NTT):
                ch = f"lx{tt}"
                for dc in range(NDC):
                    pb.dma("sync", ch, xs4[tt][dc][:],
                           x_d[dc * 128:(dc + 1) * 128, tt * TT:(tt + 1) * TT])
                tot.append(pb.chan_total(ch))
            ln_stats(pb, cst, lambda dc, tt: (xs4[tt][dc], 0, tot[tt]),
                     m0row, r0row, "sx")
        with pb.phase():
            last = None
            for tt in range(NTT):
                bm, br = bcast_rows(pb, cst, m0row, r0row, tt)
                for dc in range(NDC):
                    e1 = read_bank(pb, "vector", bm,
                                   lambda v, dc=dc, tt=tt, bm=bm:
                                   v.tensor_tensor(scr[0][:], xs4[tt][dc][:],
                                                   bm["t"][:, 0:TT], ALU.subtract),
                                   deps=[last])
                    e2 = read_bank(pb, "vector", br,
                                   lambda v, dc=dc, tt=tt, br=br:
                                   v.tensor_tensor(h[dc][:, tt * TT:(tt + 1) * TT],
                                                   scr[0][:],
                                                   br["t"][:, 0:TT], ALU.mult),
                                   deps=[e1])
                    last = e2

    sc24 = pb.scope()
    sc24.__enter__()
    wrT = pb.sbuf("wrT", [8, L], BF16)
    wmT = pb.sbuf("wmT", [16, L], BF16)
    prT = pb.sbuf("prT", [8, L], BF16)
    pmT = pb.sbuf("pmT", [16, L], BF16)
    with pb.scope():
        wcolb = [pb.sbuf(f"wcolb{i}", [128, D], BF16) for i in range(2)]
        ptile = [pb.sbuf(f"pt{i}", [128, TT], F32R) for i in range(2)]
        ptb = [pb.sbuf(f"ptb{i}", [128, TT], BF16) for i in range(2)]
        hb = [pb.sbuf(f"hb{dc}", [128, L], BF16) for dc in range(NDC)]
        rvt = [pb.sbuf(f"rvt{dc}", [128, TT], BF16) for dc in range(NDC)]
        whalf = [pb.sbuf(f"whalf{i}", [128, NDC * TT], BF16) for i in range(2)]
        wsmall = pb.sbuf("wsmall", [128, 8 * 26], BF16)
        bsmall = pb.sbuf("bsmall", [1, 26], BF16)
        smT = pb.sbuf("smT", [128, 16], F32)
        sm = pb.sbuf("sm", [128, 26], F32)
        rgm = pb.sbuf("rgm", [128, 2], F32)
        smx = pb.sbuf("smx", [128, 1], F32)
        sme = pb.sbuf("sme", [128, 16], F32)
        sms = pb.sbuf("sms", [128, 1], F32)
        with pb.phase():
            pb.dma("gpsimd", "gp4", wsmall[:].rearrange("p (c e) -> p c e", e=26),
                   wsmall_d.rearrange("(c p) e -> p c e", p=128))
            pb.dma("gpsimd", "gp4", bsmall[:], bsmall_d)
            wsm_ev = pb.chan_total("gp4")
            hb_ev = [None] * NDC
            for dc in range(NDC):
                if dc % 2 == 0:
                    hb_ev[dc] = pb.op("vector",
                                      lambda v, dc=dc:
                                      v.tensor_copy(hb[dc][:], h[dc][:]),
                                      inc=True)
                else:
                    hb_ev[dc] = pb.op("scalar",
                                      lambda s, dc=dc:
                                      s.activation(out=hb[dc][:], in_=h[dc][:],
                                                   func=AF.Copy),
                                      inc=True)
            w_rd = [None, None]
            p_rd = [None, None]
            st_pr = [None, None]
            for (wd, dst, bcol, wrow0, wpre) in (
                    (wrqT_d, rq_hbm, 8, None, wrq_ev),
                    (wmqT_d, mq_hbm, 24, None, wmq_ev),
                    (combT_d, pre_hbm, 32, 0, comb_ev)):
                for tt in range(NTT):
                    for do in range(NDC):
                        src = wd if wrow0 is None else wd
                        r0 = 0 if wrow0 is None else wrow0
                        wev = pb.dma("sync", f"ld{1 + do % 2}",
                                     wcolb[do % 2][:]
                                     .rearrange("p (c f) -> p c f", f=128),
                                     src[r0:r0 + D, do * 128:(do + 1) * 128]
                                     .rearrange("(c p) f -> p c f", p=128),
                                     deps=[w_rd[do % 2]] + wpre)
                        bo = pb.bank()
                        mms = [(wcolb[do % 2][:, dc * 128:(dc + 1) * 128],
                                hb[dc][:, tt * TT:(tt + 1) * TT],
                                [wev, hb_ev[dc]])
                               for dc in range(NDC)]
                        mev = mm_seq(pb, bo, bo["t"][:, 0:TT], mms)
                        w_rd[do % 2] = mev
                        pt = (ptile if wrow0 is not None else ptb)[do % 2]
                        ev = read_bank(pb, "vector", bo,
                                       lambda v, pt=pt, do=do, bcol=bcol, bo=bo:
                                       v.tensor_scalar(pt[:], bo["t"][:, 0:TT],
                                                       bias_c[:, bcol + do:
                                                              bcol + do + 1],
                                                       None, ALU.add),
                                       deps=[p_rd[do % 2], st_pr[do % 2]])
                        p_rd[do % 2] = ev
                        st_pr[do % 2] = pb.dma(
                            "sync", ("stq", "st2")[do % 2],
                            dst[do * 128:(do + 1) * 128, tt * TT:(tt + 1) * TT],
                            pt[:], deps=[ev])
            rv_rd = [None] * NDC
            tokst = [None, None]
            tok_rd = [None, None]
            for (wd, dstfm, dsttok, bcol, crow0, wpre) in (
                    (wvalT_d, rv_hbm, rvp_hbm, 0, D, wval_ev),
                    (wmvT_d, mv_hbm, mvp_hbm, 16, 2 * D, wmv_ev)):
                for half in range(2):
                    pb.dma("sync", f"ld{3 + half}",
                           whalf[half][:].rearrange("p (c f) -> p c f", f=TT),
                           combT_d[crow0:crow0 + D, half * TT:(half + 1) * TT]
                           .rearrange("(c p) f -> p c f", p=128),
                           deps=[tok_rd[half]] + comb_ev)
                wh_ev = [pb.chan_total("ld3"), pb.chan_total("ld4")]
                for tt in range(NTT):
                    rv_ev = [None] * NDC
                    for do in range(NDC):
                        wev = pb.dma("sync", f"ld{1 + do % 2}",
                                     wcolb[do % 2][:]
                                     .rearrange("p (c f) -> p c f", f=128),
                                     wd[:, do * 128:(do + 1) * 128]
                                     .rearrange("(c p) f -> p c f", p=128),
                                     deps=[w_rd[do % 2]] + wpre)
                        bo = pb.bank()
                        mms = [(wcolb[do % 2][:, dc * 128:(dc + 1) * 128],
                                hb[dc][:, tt * TT:(tt + 1) * TT],
                                [wev, hb_ev[dc]])
                               for dc in range(NDC)]
                        mev = mm_seq(pb, bo, bo["t"][:, 0:TT], mms)
                        w_rd[do % 2] = mev
                        rv_ev[do] = read_bank(
                            pb, "vector", bo,
                            lambda v, do=do, bcol=bcol, bo=bo:
                            v.tensor_scalar(rvt[do][:], bo["t"][:, 0:TT],
                                            bias_c[:, bcol + do:bcol + do + 1],
                                            None, ALU.add),
                            deps=[rv_rd[do]])
                        st_pr[do % 2] = pb.dma(
                            "sync", ("stq", "st2")[do % 2],
                            dstfm[do * 128:(do + 1) * 128,
                                  tt * TT:(tt + 1) * TT],
                            rvt[do][:], deps=[rv_ev[do], st_pr[do % 2]])
                    for sc4 in range(4):
                        scg = tt * 4 + sc4
                        for half in range(2):
                            bo = pb.bank()
                            mms = [(rvt[dc][:, sc4 * 128:(sc4 + 1) * 128],
                                    whalf[half][:, dc * TT:(dc + 1) * TT],
                                    [rv_ev[dc], wh_ev[half]])
                                   for dc in range(NDC)]
                            mev = mm_seq(pb, bo, bo["t"][:, 0:TT], mms)
                            rv_rd = [mev] * NDC
                            tok_rd[half] = mev
                            pt = ptb[half]
                            ev = read_bank(pb, "vector", bo,
                                           lambda v, pt=pt, bo=bo:
                                           v.tensor_copy(pt[:],
                                                         bo["t"][:, 0:TT]),
                                           deps=[tokst[half], p_rd[half],
                                                 st_pr[half]])
                            p_rd[half] = ev
                            tokst[half] = pb.dma(
                                "sync", ("st3", "st4")[half],
                                dsttok[scg * 128:(scg + 1) * 128,
                                       half * TT:(half + 1) * TT],
                                pt[:], deps=[ev])
            prev = [wsm_ev]
            tr_rd = None
            for tc in range(NTC):
                bg = pb.bank()
                mms = [(cst.ones_row_b[0:1, :], bsmall[0:1, :], [wsm_ev])]
                mms += [(hb[dc][:, tc * 128:(tc + 1) * 128],
                         wsmall[:, dc * 26:(dc + 1) * 26], [wsm_ev, hb_ev[dc]])
                        for dc in range(NDC)]
                mg = mm_seq(pb, bg, bg["t"][:, 0:26], mms)
                g1 = read_bank(pb, "vector", bg,
                               lambda v, bg=bg: v.tensor_copy(sm[:],
                                                              bg["t"][:, 0:26]),
                               deps=prev)
                g2 = pb.op("scalar",
                           lambda s: s.activation(out=rgm[:, 0:1],
                                                  in_=sm[:, 0:1],
                                                  func=AF.Sigmoid, scale=5.0),
                           deps=[g1], inc=True)
                g3 = pb.op("scalar",
                           lambda s: s.activation(out=rgm[:, 1:2],
                                                  in_=sm[:, 9:10],
                                                  func=AF.Sigmoid, scale=5.0),
                           deps=[g1, g2], inc=True)
                evs = []
                for (c0, n, dstall, gcol) in ((1, 8, w_r_all, 0),
                                              (10, 16, wm_all, 1)):
                    a1 = pb.op("vector",
                               lambda v, c0=c0, n=n:
                               v.reduce_max(smx[:], sm[:, c0:c0 + n], axis=AX.X),
                               deps=[g1] + evs, inc=True)
                    a2 = pb.op("vector",
                               lambda v: v.tensor_scalar(smx[:], smx[:], -1.0,
                                                         None, ALU.mult),
                               deps=[a1], inc=True)
                    a3 = pb.op("scalar",
                               lambda s, c0=c0, n=n:
                               s.activation(out=sme[:, 0:n], in_=sm[:, c0:c0 + n],
                                            func=AF.Exp, bias=smx[:, 0:1],
                                            scale=1.0),
                               deps=[a2, g3], inc=True)
                    a4 = pb.op("vector",
                               lambda v, n=n:
                               v.reduce_sum(sms[:], sme[:, 0:n], axis=AX.X),
                               deps=[a3], inc=True)
                    a5 = pb.op("vector",
                               lambda v: v.reciprocal(sms[:], sms[:]),
                               deps=[a4], inc=True)
                    a6 = pb.op("vector",
                               lambda v, n=n:
                               v.tensor_scalar(sme[:, 0:n], sme[:, 0:n],
                                               sms[:, 0:1], None, ALU.mult),
                               deps=[a5], inc=True)
                    if gcol == 0:
                        a7 = pb.op("vector",
                                   lambda v, tc=tc, n=n, dstall=dstall:
                                   v.tensor_scalar(
                                       dstall[:, tc * n:(tc + 1) * n],
                                       sme[:, 0:n], rgm[:, 0:1], None,
                                       ALU.mult),
                                   deps=[a6, g2], inc=True)
                    else:
                        a7 = pb.op("vector",
                                   lambda v, tc=tc, n=n, dstall=dstall:
                                   v.tensor_scalar(
                                       dstall[:, tc * n:(tc + 1) * n],
                                       sme[:, 0:n], rgm[:, 1:2],
                                       dcol[:, tc:tc + 1],
                                       ALU.mult, ALU.mult),
                                   deps=[a6, g3], inc=True)
                    evs.append(a7)
                prev = evs
                for (dstall, n, dstT) in ((w_r_all, 8, wrT), (wm_all, 16, wmT)):
                    bt_ = pb.bank()
                    te = pb.op("tensor",
                               lambda t, bt_=bt_, dstall=dstall, n=n, tc=tc:
                               t.transpose(bt_["t"][0:n, 0:128],
                                           dstall[:, tc * n:(tc + 1) * n],
                                           cst.ident[:]),
                               deps=evs + list(bt_["rd"]) +
                                    ([bt_["wr"]] if bt_["wr"] else []),
                               inc=True)
                    bt_["wr"] = te
                    bt_["rd"] = []
                    tr_rd = read_bank(pb, "vector", bt_,
                                      lambda v, dstT=dstT, n=n, tc=tc, bt_=bt_:
                                      v.tensor_copy(
                                          dstT[0:n, tc * 128:(tc + 1) * 128],
                                          bt_["t"][0:n, 0:128]),
                                      deps=[tr_rd])

    with pb.scope():
        rqt = [pb.sbuf(f"rqt{dc}", [128, TT], BF16) for dc in range(NDC)]
        mqt = [pb.sbuf(f"mqt{dc}", [128, TT], BF16) for dc in range(NDC)]
        rvs = [pb.sbuf(f"rvs{dc}", [128, TT], BF16) for dc in range(NDC)]
        mvs = [pb.sbuf(f"mvs{dc}", [128, TT], BF16) for dc in range(NDC)]
        g_sb = [pb.sbuf(f"g_sb{i}", [128, TT], F32) for i in range(4)]
        sm8 = pb.sbuf("sm8", [128, 16], F32)
        smx2 = pb.sbuf("smx2", [128, 1], F32)
        sms2 = pb.sbuf("sms2", [128, 1], F32)
        pr_t = pb.sbuf("pr_t", [128, 16], F32)
        with pb.phase():
            for _eng in PB.CENG:
                pb.waitev(_eng, pb.chan_total("stq"), pb.chan_total("st2"),
                          pb.chan_total("st3"), pb.chan_total("st4"))
            g_rd = [None] * 4
            tr_rd = None
            sm_rd = None
            rsc_banks = [pb.bank_fixed(nm) for nm in ("stat", "bcm", "bcr", "c3")]
            for tt in range(NTT):
                ch = "ld1"
                for dc in range(NDC):
                    pb.dma("sync", ch, rqt[dc][:],
                           rq_hbm[dc * 128:(dc + 1) * 128, tt * TT:(tt + 1) * TT],
                           deps=[g_rd[0]])
                    pb.dma("sync", ch, mqt[dc][:],
                           mq_hbm[dc * 128:(dc + 1) * 128, tt * TT:(tt + 1) * TT],
                           deps=[g_rd[1]])
                q_ev = pb.chan_total(ch)
                first_rsc = [True] * 4
                last_rsc = [None] * 4
                for sct in range(tt + 1):
                    ch2 = f"ld{2 + sct % 2}"
                    for dc in range(NDC):
                        pb.dma("sync", ch2, rvs[dc][:],
                               rv_hbm[dc * 128:(dc + 1) * 128,
                                      sct * TT:(sct + 1) * TT],
                               deps=[g_rd[2]])
                        pb.dma("sync", ch2, mvs[dc][:],
                               mv_hbm[dc * 128:(dc + 1) * 128,
                                      sct * TT:(sct + 1) * TT],
                               deps=[g_rd[3]])
                    v_ev = pb.chan_total(ch2)
                    for sc4 in range(4):
                        scg = sct * 4 + sc4
                        if scg > tt * 4 + 3:
                            break
                        for (qt_src, v_src, gi, wall, nsl, ccol) in (
                                (rqt, rvs, 0, w_r_all, 8, 0),
                                (mqt, mvs, 1, wm_all, 16, 8)):
                            bgg = pb.bank()
                            mms = [(v_src[dc][:, sc4 * 128:(sc4 + 1) * 128],
                                    qt_src[dc][:], [q_ev, v_ev])
                                   for dc in range(NDC)]
                            mg = mm_seq(pb, bgg, bgg["t"][:, 0:TT], mms)
                            gs = g_sb[gi]
                            k = scg - tt * 4
                            if k >= 0:
                                ge = read_bank(pb, "vector", bgg,
                                               lambda v, gs=gs, k=k, bgg=bgg:
                                               v.tensor_tensor(
                                                   gs[:], bgg["t"][:, 0:TT],
                                                   m4[:, k * TT:(k + 1) * TT],
                                                   ALU.mult),
                                               deps=[g_rd[gi]])
                            else:
                                ge = read_bank(pb, "vector", bgg,
                                               lambda v, gs=gs, bgg=bgg:
                                               v.tensor_copy(gs[:],
                                                             bgg["t"][:, 0:TT]),
                                               deps=[g_rd[gi]])
                            lastm = None
                            for j in range(4):
                                bank = rsc_banks[j]
                                dd = [ge]
                                if first_rsc[j] and ccol == 0:
                                    dd += list(bank["rd"])
                                    if bank["wr"] is not None:
                                        bank_wr = bank["wr"]
                                        dd.append(bank_wr)
                                lastm = pb.op(
                                    "tensor",
                                    lambda t, gs=gs, j=j, bank=bank,
                                    f=(first_rsc[j] and ccol == 0),
                                    nsl=nsl, ccol=ccol, scg=scg:
                                    t.matmul(bank["t"][:, ccol:ccol + nsl],
                                             gs[:, j * 128:(j + 1) * 128],
                                             (w_r_all if nsl == 8 else wm_all)
                                             [:, scg * nsl:(scg + 1) * nsl],
                                             start=f, stop=False,
                                             skip_group_check=True),
                                    deps=dd, inc=True)
                                if ccol == 0:
                                    first_rsc[j] = False
                            g_rd[gi] = lastm
                            if gi == 0:
                                g_rd[2] = lastm
                            else:
                                g_rd[3] = lastm
                for j in range(4):
                    tcg = tt * 4 + j
                    bank = rsc_banks[j]
                    bank["wr"] = g_rd[1]
                    bank["rd"] = []
                    for (c0, n, dstT, dp) in ((0, 8, prT, False),
                                              (8, 16, pmT, True)):
                        a1 = pb.op("vector",
                                   lambda v, c0=c0, n=n, bank=bank, tcg=tcg, dp=dp:
                                   v.tensor_scalar(pr_t[:, 0:n],
                                                   bank["t"][:, c0:c0 + n],
                                                   SC, None, ALU.mult)
                                   if not dp else
                                   v.tensor_scalar(pr_t[:, 0:n],
                                                   bank["t"][:, c0:c0 + n],
                                                   SC,
                                                   dcol[:, 16 + tcg:17 + tcg],
                                                   ALU.mult, ALU.mult),
                                   deps=[sm_rd, bank["wr"]], inc=True)
                        bank["rd"].append(a1)
                        a2 = pb.op("vector",
                                   lambda v, n=n:
                                   v.reduce_max(smx2[:], pr_t[:, 0:n], axis=AX.X),
                                   deps=[a1], inc=True)
                        a3 = pb.op("vector",
                                   lambda v: v.tensor_scalar(smx2[:], smx2[:],
                                                             -1.0, None,
                                                             ALU.mult),
                                   deps=[a2], inc=True)
                        a4 = pb.op("scalar",
                                   lambda s, n=n:
                                   s.activation(out=pr_t[:, 0:n],
                                                in_=pr_t[:, 0:n],
                                                func=AF.Exp, bias=smx2[:, 0:1],
                                                scale=1.0),
                                   deps=[a3], inc=True)
                        a5 = pb.op("vector",
                                   lambda v, n=n:
                                   v.reduce_sum(sms2[:], pr_t[:, 0:n], axis=AX.X),
                                   deps=[a4], inc=True)
                        a6 = pb.op("vector",
                                   lambda v: v.reciprocal(sms2[:], sms2[:]),
                                   deps=[a5], inc=True)
                        if dp:
                            a7 = pb.op("vector",
                                       lambda v, n=n, tcg=tcg:
                                       v.tensor_scalar(pr_t[:, 0:n],
                                                       pr_t[:, 0:n],
                                                       sms2[:, 0:1],
                                                       dcol[:, 16 + tcg:
                                                            17 + tcg],
                                                       ALU.mult, ALU.mult),
                                       deps=[a6], inc=True)
                        else:
                            a7 = pb.op("vector",
                                       lambda v, n=n:
                                       v.tensor_scalar(pr_t[:, 0:n],
                                                       pr_t[:, 0:n],
                                                       sms2[:, 0:1], None,
                                                       ALU.mult),
                                       deps=[a6], inc=True)
                        bt_ = pb.bank()
                        te = pb.op("tensor",
                                   lambda t, bt_=bt_, n=n:
                                   t.transpose(bt_["t"][0:n, 0:128],
                                               pr_t[:, 0:n], cst.ident[:]),
                                   deps=[a7] + list(bt_["rd"]) +
                                        ([bt_["wr"]] if bt_["wr"] else []),
                                   inc=True)
                        bt_["wr"] = te
                        bt_["rd"] = []
                        tr_rd = read_bank(pb, "vector", bt_,
                                          lambda v, dstT=dstT, n=n, tcg=tcg,
                                          bt_=bt_:
                                          v.tensor_copy(
                                              dstT[0:n,
                                                   tcg * 128:(tcg + 1) * 128],
                                              bt_["t"][0:n, 0:128]),
                                          deps=[tr_rd])
                        sm_rd = a7
                    sm_rd = tr_rd

    with pb.scope():
        ar_sb = [pb.sbuf(f"ar{i}", [128, TT], BF16) for i in range(2)]
        am_sb = [pb.sbuf(f"am{i}", [128, TT], BF16) for i in range(2)]
        rvp_t = [pb.sbuf(f"rvp{i}", [128, D], BF16) for i in range(2)]
        mvp_t = [pb.sbuf(f"mvp{i}", [128, D], BF16) for i in range(2)]
        pre_t = pb.sbuf("pre_t", [128, TT], F32R)
        cph_t = [pb.sbuf(f"cph{i}", [128, TT], F32R) for i in range(2)]
        with pb.phase():
            for _eng in PB.CENG:
                pb.waitev(_eng, pb.chan_total("stq"), pb.chan_total("st2"),
                          pb.chan_total("st3"), pb.chan_total("st4"))
            a_rd = [None, None]
            tok_rd = [None, None]
            pre_rd = None
            cph_rd = [None, None]
            cst_st = [None, None]
            cbanks = [pb.bank_fixed(nm) for nm in ("stat", "bcm", "bcr", "c3")]
            for tt in range(NTT):
                for dh in range(2):
                    first_c = [True] * 4
                    lastc = [None] * 4
                    for sct in range(tt + 1):
                        for sc4 in range(4):
                            scg = sct * 4 + sc4
                            if scg > tt * 4 + 3:
                                break
                            k = scg - tt * 4
                            ba = pb.bank()
                            ma = mm_seq(pb, ba, ba["t"][:, 0:TT],
                                        [(wrT[0:8, scg * 128:(scg + 1) * 128],
                                          prT[0:8, tt * TT:(tt + 1) * TT], [])])
                            asb = ar_sb[scg % 2]
                            if k >= 0:
                                ae = read_bank(pb, "vector", ba,
                                               lambda v, asb=asb, k=k, ba=ba:
                                               v.tensor_tensor(
                                                   asb[:], ba["t"][:, 0:TT],
                                                   m4[:, k * TT:(k + 1) * TT],
                                                   ALU.mult),
                                               deps=[a_rd[scg % 2]])
                            else:
                                ae = read_bank(pb, "vector", ba,
                                               lambda v, asb=asb, ba=ba:
                                               v.tensor_copy(asb[:],
                                                             ba["t"][:, 0:TT]),
                                               deps=[a_rd[scg % 2]])
                            bam = pb.bank()
                            mam = mm_seq(pb, bam, bam["t"][:, 0:TT],
                                         [(wmT[0:16, scg * 128:(scg + 1) * 128],
                                           pmT[0:16, tt * TT:(tt + 1) * TT],
                                           [])])
                            amb = am_sb[scg % 2]
                            if k >= 0:
                                ame = read_bank(pb, "vector", bam,
                                                lambda v, amb=amb, k=k, bam=bam:
                                                v.tensor_tensor(
                                                    amb[:], bam["t"][:, 0:TT],
                                                    m4[:, k * TT:(k + 1) * TT],
                                                    ALU.mult),
                                                deps=[a_rd[scg % 2]])
                            else:
                                ame = read_bank(pb, "vector", bam,
                                                lambda v, amb=amb, bam=bam:
                                                v.tensor_copy(amb[:],
                                                              bam["t"][:, 0:TT]),
                                                deps=[a_rd[scg % 2]])
                            chv = f"ld{1 + scg % 2}"
                            pb.dma("sync", chv, rvp_t[scg % 2][:],
                                   rvp_hbm[scg * 128:(scg + 1) * 128, :],
                                   deps=[tok_rd[scg % 2]])
                            pb.dma("sync", chv, mvp_t[scg % 2][:],
                                   mvp_hbm[scg * 128:(scg + 1) * 128, :],
                                   deps=[tok_rd[scg % 2]])
                            tok_ev = pb.chan_total(chv)
                            lastm = None
                            for jj in range(4):
                                do = dh * 4 + jj
                                bank = cbanks[jj]
                                dd = [ae, tok_ev]
                                if first_c[jj]:
                                    dd += list(bank["rd"])
                                    if bank["wr"] is not None:
                                        dd.append(bank["wr"])
                                pb.op("tensor",
                                      lambda t, do=do, jj=jj, bank=bank,
                                      f=first_c[jj], i=scg % 2:
                                      t.matmul(bank["t"][:, 0:TT],
                                               rvp_t[i][:, do * 128:
                                                       (do + 1) * 128],
                                               ar_sb[i][:],
                                               start=f, stop=False,
                                               skip_group_check=True),
                                      deps=dd)
                                first_c[jj] = False
                                lastm = pb.op(
                                    "tensor",
                                    lambda t, do=do, jj=jj, bank=bank,
                                    i=scg % 2:
                                    t.matmul(bank["t"][:, 0:TT],
                                             mvp_t[i][:, do * 128:
                                                     (do + 1) * 128],
                                             am_sb[i][:],
                                             start=False, stop=False,
                                             skip_group_check=True),
                                    deps=[ame, tok_ev], inc=True)
                                lastc[jj] = lastm
                            a_rd[scg % 2] = lastm
                            tok_rd[scg % 2] = lastm
                    for jj in range(4):
                        do = dh * 4 + jj
                        bank = cbanks[jj]
                        bank["wr"] = lastc[jj]
                        bank["rd"] = []
                        pdma = pb.dma("sync", "ld3", pre_t[:],
                                      pre_hbm[do * 128:(do + 1) * 128,
                                              tt * TT:(tt + 1) * TT],
                                      deps=[pre_rd, pb.chan_total("ld3")])
                        ct = cph_t[jj % 2]
                        c1 = read_bank(pb, "vector", bank,
                                       lambda v, ct=ct, bank=bank:
                                       v.tensor_tensor(ct[:], bank["t"][:, 0:TT],
                                                       pre_t[:], ALU.add),
                                       deps=[pdma, cph_rd[jj % 2],
                                             cst_st[jj % 2]])
                        pre_rd = c1
                        c2 = pb.op("vector",
                                   lambda v, ct=ct, do=do, tt=tt:
                                   v.tensor_tensor(ct[:], ct[:],
                                                   h[do][:, tt * TT:
                                                        (tt + 1) * TT],
                                                   ALU.add),
                                   deps=[c1], inc=True)
                        cst_st[jj % 2] = pb.dma(
                            "sync", ("stq", "st2")[jj % 2],
                            cph_hbm[do * 128:(do + 1) * 128,
                                    tt * TT:(tt + 1) * TT],
                            ct[:], deps=[c2, cst_st[jj % 2]])
                        cph_rd[jj % 2] = c2

    sc24.__exit__(None, None, None)
    with pb.scope():
        morow = pb.sbuf("morow", [1, L], F32)
        rorow = pb.sbuf("rorow", [1, L], F32)
        strm = [[pb.sbuf(f"li{i}_{dc}", [128, TT], F32R) for dc in range(NDC)]
                for i in range(2)]
        with pb.phase():
            for _eng in PB.CENG:
                pb.waitev(_eng, pb.chan_total("stq"), pb.chan_total("st2"))
            war = [None, None]
            evs = [None, None]

            def get_x(dc, tt):
                if dc == 0:
                    ch = f"ld{4 + tt % 2}"
                    for c in range(NDC):
                        pb.dma("sync", ch, strm[tt % 2][c][:],
                               cph_hbm[c * 128:(c + 1) * 128,
                                       tt * TT:(tt + 1) * TT],
                               deps=[war[tt % 2]])
                    evs[tt % 2] = pb.chan_total(ch)
                return (strm[tt % 2][dc], 0, evs[tt % 2])

            ln_stats(pb, cst, get_x, m0row, r0row, "si",
                     war_hook=lambda tt, ev: war.__setitem__(tt % 2, ev))
        sqy = pb.sbuf("sqy", [128, TT], F32R)
        y_t = [pb.sbuf(f"y_t{i}", [128, TT], F32R) for i in range(2)]
        yvrow = pb.sbuf("yvrow", [1, TT], F32)
        with pb.phase():
            war = [None, None]
            y_st = [None, None]
            ysq = [None, None]
            ymm = [None, None]
            sq_rd = None
            prev_small = []
            for tt in range(NTT):
                ch = f"ld{4 + tt % 2}"
                for c in range(NDC):
                    pb.dma("sync", ch, strm[tt % 2][c][:],
                           cph_hbm[c * 128:(c + 1) * 128, tt * TT:(tt + 1) * TT],
                           deps=[war[tt % 2]])
                tot = pb.chan_total(ch)
                bm, br = bcast_rows(pb, cst, m0row, r0row, tt)
                bank_s = pb.bank_fixed("stat")
                bank_q = pb.bank_fixed("c3")
                ps = bank_s["t"][0:1, 0:TT]
                pq = bank_q["t"][0:1, 0:TT]
                last_me = None
                for dc in range(NDC):
                    yt = y_t[dc % 2]
                    e1 = read_bank(pb, "vector", bm,
                                   lambda v, dc=dc, tt=tt, bm=bm:
                                   v.tensor_tensor(scr[0][:],
                                                   strm[tt % 2][dc][:],
                                                   bm["t"][:, 0:TT],
                                                   ALU.subtract),
                                   deps=[tot, war[tt % 2]])
                    e2 = read_bank(pb, "vector", br,
                                   lambda v, br=br:
                                   v.tensor_tensor(scr[1][:], scr[0][:],
                                                   br["t"][:, 0:TT], ALU.mult),
                                   deps=[e1])
                    e3 = pb.op("vector",
                               lambda v, dc=dc, yt=yt:
                               v.tensor_scalar(yt[:], scr[1][:],
                                               bias_c[:, 40 + dc:41 + dc],
                                               bias_c[:, 48 + dc:49 + dc],
                                               ALU.mult, ALU.add),
                               deps=[e2, y_st[dc % 2], ysq[dc % 2],
                                     ymm[dc % 2]], inc=True)
                    y_st[dc % 2] = pb.dma(
                        "sync", ("stq", "st2")[dc % 2],
                        y_hbm[dc * 128:(dc + 1) * 128, tt * TT:(tt + 1) * TT],
                        yt[:], deps=[e3])
                    war[tt % 2] = e3
                    sev = pb.op("scalar",
                                lambda s, yt=yt:
                                s.activation(out=sqy[:], in_=yt[:],
                                             func=AF.Square),
                                deps=[e3, sq_rd], inc=True)
                    ysq[dc % 2] = sev
                    dd = [e3]
                    if dc == 0:
                        dd += list(bank_s["rd"])
                        if bank_s["wr"] is not None:
                            dd.append(bank_s["wr"])
                    pb.op("tensor",
                          lambda t, yt=yt, f=(dc == 0), l=(dc == NDC - 1), ps=ps:
                          t.matmul(ps, cst.ones_col_r[:], yt[:],
                                   start=f, stop=l,
                                   skip_group_check=True),
                          deps=dd)
                    dd2 = [sev]
                    if dc == 0:
                        dd2 += list(bank_q["rd"])
                        if bank_q["wr"] is not None:
                            dd2.append(bank_q["wr"])
                    last_me = pb.op(
                        "tensor",
                        lambda t, f=(dc == 0), l=(dc == NDC - 1), pq=pq:
                        t.matmul(pq, cst.ones_col_r[:], sqy[:],
                                 start=f, stop=l,
                                 skip_group_check=True),
                        deps=dd2, inc=True)
                    sq_rd = last_me
                    ymm[dc % 2] = last_me
                bank_s["wr"] = last_me
                bank_q["wr"] = last_me
                bank_s["rd"] = []
                bank_q["rd"] = []
                c0 = tt * TT
                e1 = read_bank(pb, "vector", bank_s,
                               lambda v, ps=ps, c0=c0:
                               v.tensor_scalar(morow[0:1, c0:c0 + TT], ps,
                                               1.0 / D, None, ALU.mult),
                               deps=prev_small)
                e2 = read_bank(pb, "vector", bank_q,
                               lambda v, pq=pq, c0=c0:
                               v.tensor_scalar(yvrow[0:1, :], pq,
                                               1.0 / D, None, ALU.mult),
                               deps=prev_small + [e1])
                e3b = pb.op("scalar",
                            lambda s, c0=c0:
                            s.activation(out=rorow[0:1, c0:c0 + TT],
                                         in_=morow[0:1, c0:c0 + TT],
                                         func=AF.Square),
                            deps=[e1], inc=True)
                e4 = pb.op("vector",
                           lambda v, c0=c0:
                           v.tensor_tensor(yvrow[0:1, :], yvrow[0:1, :],
                                           rorow[0:1, c0:c0 + TT],
                                           ALU.subtract),
                           deps=[e2, e3b], inc=True)
                e5 = pb.op("scalar",
                           lambda s, c0=c0:
                           s.activation(out=rorow[0:1, c0:c0 + TT],
                                        in_=yvrow[0:1, :], func=AF.Sqrt,
                                        bias=cst.cpool[0:1, C_EPS:C_EPS + 1],
                                        scale=1.0),
                           deps=[e4], inc=True)
                e6 = pb.op("vector",
                           lambda v, c0=c0:
                           v.reciprocal(rorow[0:1, c0:c0 + TT],
                                        rorow[0:1, c0:c0 + TT]),
                           deps=[e5], inc=True)
                prev_small = [e6]
        xs2 = [[pb.sbuf(f"xs2{i}_{c}", [128, TT], F32) for c in range(NDC)]
               for i in range(2)]
        out_t = [scr[2], scr[3]]
        with pb.phase():
            for _eng in PB.CENG:
                pb.waitev(_eng, pb.chan_total("stq"), pb.chan_total("st2"))
            xuse = [None, None]
            ost = [None, None]
            last = None
            for tt in range(NTT):
                ch = f"ld{1 + tt % 2}"
                for dc in range(NDC):
                    pb.dma("sync", ch, strm[tt % 2][dc][:],
                           y_hbm[dc * 128:(dc + 1) * 128, tt * TT:(tt + 1) * TT],
                           deps=[xuse[tt % 2]])
                    pb.dma("sync", ch, xs2[tt % 2][dc][:],
                           x_f[dc * 128:(dc + 1) * 128, tt * TT:(tt + 1) * TT],
                           deps=[xuse[tt % 2]])
                tot = pb.chan_total(ch)
                bm, br = bcast_rows(pb, cst, morow, rorow, tt)
                for dc in range(NDC):
                    ot = out_t[dc % 2]
                    e1 = read_bank(pb, "vector", bm,
                                   lambda v, dc=dc, tt=tt, bm=bm:
                                   v.tensor_tensor(scr[0][:],
                                                   strm[tt % 2][dc][:],
                                                   bm["t"][:, 0:TT],
                                                   ALU.subtract),
                                   deps=[last, tot])
                    e2 = read_bank(pb, "vector", br,
                                   lambda v, br=br:
                                   v.tensor_tensor(scr[1][:], scr[0][:],
                                                   br["t"][:, 0:TT], ALU.mult),
                                   deps=[e1])
                    e3 = pb.op("vector",
                               lambda v, dc=dc, tt=tt, ot=ot:
                               v.tensor_tensor(ot[:], scr[1][:],
                                               xs2[tt % 2][dc][:], ALU.add),
                               deps=[e2, ost[dc % 2]], inc=True)
                    ost[dc % 2] = pb.dma(
                        "sync", ("stq", "st2")[dc % 2],
                        xo_b[dc * 128:(dc + 1) * 128, tt * TT:(tt + 1) * TT],
                        ot[:], deps=[e3])
                    last = e3
                    xuse[tt % 2] = e3
            pb.waitev("sync", pb.chan_total("stq"), pb.chan_total("st2"))

    return


def prep_weights(inp):
    w = {}
    cc = np.ascontiguousarray
    f32 = lambda k: np.asarray(inp[k], np.float32)
    w["l0f_tapT"] = cc(f32("L0_ffn_conv_w").transpose(2, 1, 0))
    w["l0f_convb_c"] = cc(f32("L0_ffn_conv_b").reshape(8, 128).T)
    w["l0f_wqkvT"] = cc(f32("L0_ffn_wqkv").T)
    w["l0f_bqkv_c"] = cc(f32("L0_ffn_bqkv").reshape(24, 128).T)
    w["l0f_bqkv_row"] = cc(f32("L0_ffn_bqkv").reshape(1, 3 * 1024))
    w["l0f_woT"] = cc(f32("L0_ffn_wo").T)
    w["l0f_bo_c"] = cc(f32("L0_ffn_bo").reshape(8, 128).T)
    w["l0f_ln1g_c"] = cc(f32("L0_ffn_ln1_g").reshape(8, 128).T)
    w["l0f_ln1b_c"] = cc(f32("L0_ffn_ln1_b").reshape(8, 128).T)
    w["l0f_ln2g_c"] = cc(f32("L0_ffn_ln2_g").reshape(8, 128).T)
    w["l0f_ln2b_c"] = cc(f32("L0_ffn_ln2_b").reshape(8, 128).T)
    w["l0f_gatew"] = cc(f32("L0_ffn_gate_w"))
    w["l0f_gateb_row"] = cc(f32("L0_ffn_gate_b").reshape(1, 8))
    w["l0f_w1"] = cc(f32("L0_ffn_w1"))
    w["l0f_b1_c"] = cc(f32("L0_ffn_b1").reshape(8, 32, 128).transpose(0, 2, 1))
    w["l0f_w2"] = cc(f32("L0_ffn_w2"))
    w["l0f_b2"] = cc(f32("L0_ffn_b2"))
    w["l1f_WrT"] = cc(f32("L1_ffn_W_r").T)
    w["l1f_WkT"] = cc(f32("L1_ffn_W_k").T)
    w["l1f_WvT"] = cc(f32("L1_ffn_W_v").T)
    w["l0a_rvT"] = cc(f32("L0_attn_reg_val_w").T)
    w["l0a_rqT"] = cc(f32("L0_attn_reg_q_w").T)
    w["l0a_mvT"] = cc(f32("L0_attn_mem_val_w").T)
    w["l0a_mqT"] = cc(f32("L0_attn_mem_q_w").T)
    w["l0a_combT"] = cc(f32("L0_attn_comb_w").T)
    w["l0a_wsmall"] = cc(np.concatenate(
        [f32("L0_attn_reg_gate_w").T, f32("L0_attn_reg_addr_w").T,
         f32("L0_attn_mem_gate_w").T, f32("L0_attn_mem_addr_w").T], axis=1))
    w["l0a_bsmall"] = cc(np.concatenate(
        [f32("L0_attn_reg_gate_b"), f32("L0_attn_reg_addr_b"),
         f32("L0_attn_mem_gate_b"), f32("L0_attn_mem_addr_b")])[None, :])
    bia = np.zeros((128, 56), np.float32)
    bia[:, 0:8] = f32("L0_attn_reg_val_b").reshape(8, 128).T
    bia[:, 8:16] = f32("L0_attn_reg_q_b").reshape(8, 128).T
    bia[:, 16:24] = f32("L0_attn_mem_val_b").reshape(8, 128).T
    bia[:, 24:32] = f32("L0_attn_mem_q_b").reshape(8, 128).T
    bia[:, 32:40] = f32("L0_attn_comb_b").reshape(8, 128).T
    bia[:, 40:48] = f32("L0_attn_ln_g").reshape(8, 128).T
    bia[:, 48:56] = f32("L0_attn_ln_b").reshape(8, 128).T
    w["l0a_bias_c"] = cc(bia)
    dcv = np.zeros((128, 32), np.float64)
    pidx = np.arange(128)
    for sc in range(16):
        dcv[:, sc] = 0.995 ** (-(sc * 128.0 + pidx))
        dcv[:, 16 + sc] = 0.995 ** (sc * 128.0 + pidx)
    w["l0a_dcol"] = cc(dcv.astype(np.float32))
    m4a = np.zeros((4, 128, 512), np.float32)
    for k4 in range(4):
        for p4 in range(128):
            m4a[k4, p4, p4 + 128 * k4:] = 1.0
    w["l0a_m4"] = cc(m4a.transpose(1, 0, 2).reshape(128, 4 * 512))
    for nm in ("wq_lin", "wk_lin", "wv_lin", "wq_loc", "wk_loc", "wv_loc",
               "wo", "wg1", "wg2"):
        w["l1a_" + nm] = cc(f32("L1_attn_" + nm))
    bias = np.zeros((128, 64), np.float32)
    bias[:, 0:8] = f32("L1_attn_bq_lin").reshape(8, 128).T
    bias[:, 8:16] = f32("L1_attn_bq_loc").reshape(8, 128).T
    bias[:, 16:24] = f32("L1_attn_bk_loc").reshape(8, 128).T
    bias[:, 24:32] = f32("L1_attn_bg1").reshape(8, 128).T
    bias[:, 32:40] = f32("L1_attn_bg2").reshape(8, 128).T
    bias[:, 40:48] = f32("L1_attn_bo").reshape(8, 128).T
    bias[:, 48:56] = f32("L1_attn_ln_g").reshape(8, 128).T
    bias[:, 56:64] = f32("L1_attn_ln_b").reshape(8, 128).T
    w["l1a_bias_c"] = cc(bias)
    w["l1a_brow"] = cc(np.stack([f32("L1_attn_bk_lin"),
                                 f32("L1_attn_bv_lin"),
                                 f32("L1_attn_bv_loc")], axis=0))
    rb = f32("L1_attn_rel_bias")
    Q = np.full((16, 512), -10000.0, np.float32)
    for m in range(224, 289):
        Q[:, m] = rb[m - 256 + 64, :]
    BT = np.zeros((128, 16 * 384), np.float32)
    p_idx = np.arange(128)[:, None]
    i_idx = np.arange(128)[None, :]
    for hh2 in range(16):
        for c3 in range(3):
            blk = Q[hh2][(c3 - 1) * 128 + p_idx - i_idx + 256]
            BT[:, hh2 * 384 + c3 * 128:hh2 * 384 + (c3 + 1) * 128] = blk
    w["l1a_BT"] = cc(BT)
    mu = np.zeros((128, 16), np.float32)
    mu[:, 0:8] = f32("L1_ffn_mu_r").reshape(8, 128).T
    mu[:, 8:16] = f32("L1_ffn_mu_k").reshape(8, 128).T
    w["l1f_mu"] = cc(mu)
    return w


def _to_fm(x):
    return [np.ascontiguousarray(np.asarray(x[b], np.float32).T)
            for b in range(x.shape[0])]


def _from_fm(shards):
    return np.stack([np.ascontiguousarray(np.asarray(s, np.float32).T)
                     for s in shards], axis=0)


def run_block(pb, x_shards):
    in_maps = []
    for c in range(NCORES):
        m = dict(pb.inputs)
        m["x_in"] = x_shards[c]
        in_maps.append(m)
    res = run_bass_kernel_spmd(pb.nc, in_maps, core_ids=list(range(NCORES)))
    return [np.asarray(r["x_out"], np.float32) for r in res.results]


EMITTERS = {"l0attn": emit_l0attn, "l0ffn": emit_l0ffn,
            "l1attn": emit_l1attn, "l1ffn": emit_l1ffn}


def _mk_builder(tag):
    def build(weights):
        pb = PB(tag)
        cst = Consts(pb)
        pb.psum_banks(4)
        x_d = pb.nc.dram_tensor("x_in", [D, L], F32R, kind="ExternalInput").ap()
        xo_d = pb.dram_out("x_out", [D, L], F32)
        with pb.phase():
            cst.load(pb)
            for _eng in PB.CENG:
                pb.waitev(_eng, pb.chan_total("ldc"))
        EMITTERS[tag](pb, cst, x_d, xo_d, weights)
        return pb
    return build


build_l0attn = _mk_builder("l0attn")
build_l0ffn = _mk_builder("l0ffn")
build_l1attn = _mk_builder("l1attn")
build_l1ffn = _mk_builder("l1ffn")


def build_fused(weights):
    pb = PB("fused")
    cst = Consts(pb)
    pb.psum_banks(4)
    x0 = pb.nc.dram_tensor("x_in", [D, L], F32R, kind="ExternalInput").ap()
    x1 = pb.nc.dram_tensor("x1_hbm", [D, L], F32R, kind="Internal").ap()
    x2 = pb.nc.dram_tensor("x2_hbm", [D, L], F32R, kind="Internal").ap()
    x3 = pb.nc.dram_tensor("x3_hbm", [D, L], F32R, kind="Internal").ap()
    xo = pb.dram_out("x_out", [D, L], F32)
    with pb.phase():
        cst.load(pb)
        for _eng in PB.CENG:
            pb.waitev(_eng, pb.chan_total("ldc"))
    need_pre(pb, weights, PRE_ORDER)
    chain = [(emit_l0attn, "A_", x0, x1), (emit_l0ffn, "B_", x1, x2),
             (emit_l1attn, "C_", x2, x3), (emit_l1ffn, "D_", x3, xo)]
    for fn, pref, xi, xo_ in chain:
        pb.prefix = pref
        with pb.scope():
            with pb.phase():
                for _eng in PB.CENG:
                    pb.waitev(_eng, *[pb.chan_total(c) for c in
                                      ("stq", "st2", "st3", "st4")])
            fn(pb, cst, xi, xo_, weights)
    pb.prefix = ""
    return pb


def kernel(**inputs):
    w = prep_weights(inputs)
    x = _to_fm(inputs["L0_attn_u"])
    pb = build_fused(w)
    x = run_block(pb, x)
    return _from_fm(x)


# version 10
# speedup vs baseline: 1.0128x; 1.0128x over previous
import numpy as np
from contextlib import ExitStack

import concourse.bass as bass
import concourse.mybir as mybir
from concourse.bass_utils import run_bass_kernel_spmd

F32 = mybir.dt.float32
F32R = mybir.dt.float32r
BF16 = mybir.dt.bfloat16
AF = mybir.ActivationFunctionType
ALU = mybir.AluOpType
AX = mybir.AxisListType

D = 1024
L = 2048
NDC = D // 128
NTC = L // 128
TT = 512
NTT = L // TT
EPS = 1e-5
NCORES = 8

C_EPS = 0
C_NCONST = 4


def _host_cpool():
    p = np.zeros((128, C_NCONST), np.float32)
    p[:, C_EPS] = EPS
    return p


class PB:
    CENG = ("tensor", "vector", "scalar", "gpsimd", "sync")
    ENGSEM = {"tensor": "pe", "vector": "dve", "scalar": "act",
              "gpsimd": "pool", "sync": "sp"}

    def __init__(self, tag):
        self.tag = tag
        self.nc = bass.Bass("TRN2", target_bir_lowering=False)
        self.st = ExitStack()
        self.sem = {}
        self.cnt = {}
        self.water = {}
        self.opidx = {e: 0 for e in self.CENG}
        self.drained = {e: 0 for e in self.CENG}
        for s in ("pe", "dve", "act", "pool", "sp",
                  "ld0", "ld1", "ld2", "ld3", "ld4", "ld5",
                  "gp0", "gp1", "gp2", "gp3", "gp4", "gp5", "stq", "st2", "st3", "st4", "ldc",
                  "lx0", "lx1", "lx2", "lx3", "pc0", "pc1", "pc2", "pc3",
                  "pc4", "pc5", "pc6", "pc7", "pc8", "pc9", "pc10", "pc11",
                  "pc12", "pc13", "pc14"):
            self.sem[s] = self.st.enter_context(self.nc.semaphore(f"s_{s}"))
            self.cnt[s] = 0
        self._streams = None
        self.inputs = {}
        self._rot = []
        self._fixed = {}
        self._rr = 0
        self.prefix = ""
        self.moe_pre = None
        self.pre = {}

    def sbuf(self, name, shape, dt):
        return self.st.enter_context(
            self.nc.sbuf_tensor(self.prefix + name, shape, dt))

    def scope(self):
        pb = self

        class _Scope:
            def __enter__(selfs):
                selfs.saved = pb.st
                selfs.st = ExitStack()
                pb.st = selfs.st
                return selfs

            def __exit__(selfs, *exc):
                pb.st = selfs.saved
                selfs.st.close()
                return False
        return _Scope()

    def dram_in(self, name, arr, dt):
        ap = self.nc.dram_tensor(name, list(arr.shape), dt, kind="ExternalInput").ap()
        self.inputs[name] = np.ascontiguousarray(arr)
        return ap

    def dram_out(self, name, shape, dt):
        return self.nc.dram_tensor(name, shape, dt, kind="ExternalOutput").ap()

    def dram_tmp(self, name, shape, dt):
        return self.nc.dram_tensor(self.prefix + name, shape, dt,
                                   kind="Internal").ap()

    def psum_banks(self, n_rot=6):
        self._rot = [dict(t=self.st.enter_context(
            self.nc.psum_tensor(f"bank{i}", [128, 512], F32)), rd=[], wr=None)
            for i in range(n_rot)]
        nfx = 8 - n_rot
        self._fixed_slots = [dict(t=self.st.enter_context(
            self.nc.psum_tensor(f"fx{i}", [128, 512], F32)), rd=[], wr=None)
            for i in range(nfx)]
        self._fixed_map = {}
        self._fixed_rr = 0

    def bank(self):
        b = self._rot[self._rr % len(self._rot)]
        self._rr += 1
        return b

    def bank_fixed(self, name):
        if name not in self._fixed_map:
            slot = self._fixed_slots[self._fixed_rr % len(self._fixed_slots)]
            self._fixed_rr += 1
            for k in [k for k, v in self._fixed_map.items() if v is slot]:
                del self._fixed_map[k]
            self._fixed_map[name] = slot
        return self._fixed_map[name]

    def phase(self):
        pb = self

        class _Phase:
            def __enter__(selfp):
                assert pb._streams is None
                pb._streams = {e: [] for e in PB.CENG}
                return selfp

            def __exit__(selfp, *exc):
                if exc[0] is not None:
                    pb._streams = None
                    return False
                streams, pb._streams = pb._streams, None
                if any(streams.values()):
                    with pb.nc.Block() as blk:
                        for e in PB.CENG:
                            if streams[e]:
                                def runner(h, fns=tuple(streams[e])):
                                    for f in fns:
                                        f(h)
                                getattr(blk, e)(runner)
                return True
        return _Phase()

    def op(self, eng, fn, deps=(), inc=False):
        waits = []
        need_drain = False
        for ev in deps:
            if ev is None:
                continue
            kind, peng, sname, val, oi = ev
            if kind == "c" and peng == eng:
                if sname is not None and self.water.get((eng, sname), 0) >= val:
                    continue
                if oi > self.drained[eng]:
                    need_drain = True
            else:
                assert sname is not None, f"dep on {peng} without sem inc"
                if self.water.get((eng, sname), 0) < val:
                    waits.append((sname, val))
                    self.water[(eng, sname)] = val
        if need_drain:
            self.drained[eng] = self.opidx[eng]
        myinc = None
        if inc:
            sname = self.ENGSEM[eng]
            self.cnt[sname] += 1
            myinc = (sname, self.cnt[sname])
        self.opidx[eng] += 1
        oi = self.opidx[eng]
        semd = self.sem

        def closure(h, waits=tuple(waits), nd=need_drain, myinc=myinc, fn=fn):
            for s, v in waits:
                h.wait_ge(semd[s], v)
            if nd:
                h.drain()
            inst = fn(h)
            if myinc is not None:
                inst.then_inc(semd[myinc[0]], 1)
        self._streams[eng].append(closure)
        return ("c", eng, myinc[0] if myinc else None,
                myinc[1] if myinc else None, oi)

    def dma(self, queue, chan, out_ap, in_ap, deps=(), **kw):
        waits = []
        for ev in deps:
            if ev is None:
                continue
            kind, peng, sname, val, oi = ev
            assert sname is not None, "dma dep needs a sem"
            if self.water.get((queue, sname), 0) < val:
                waits.append((sname, val))
                self.water[(queue, sname)] = val
        self.cnt[chan] += 16
        val = self.cnt[chan]
        semd = self.sem
        self.opidx[queue] += 1

        def closure(h, waits=tuple(waits), kw=kw):
            for s, v in waits:
                h.wait_ge(semd[s], v)
            h.dma_start(out=out_ap, in_=in_ap, **kw).then_inc(semd[chan], 16)
        self._streams[queue].append(closure)
        return ("d", queue, chan, val, self.opidx[queue])

    def chan_total(self, chan):
        return ("d", None, chan, self.cnt[chan], 0)

    def waitev(self, eng, *evs):
        semd = self.sem
        waits = []
        for ev in evs:
            if ev is None:
                continue
            kind, peng, sname, val, oi = ev
            assert sname is not None
            if self.water.get((eng, sname), 0) < val:
                waits.append((sname, val))
                self.water[(eng, sname)] = val
        if waits:
            def closure(h, waits=tuple(waits)):
                for s, v in waits:
                    h.wait_ge(semd[s], v)
            self._streams[eng].append(closure)


def mm_seq(pb, bank, out_ap, mms, first_extra=(), inc_last=True, skip_gc=False):
    n = len(mms)
    ev = None
    for i, (lh, rh, deps) in enumerate(mms):
        dd = list(deps)
        if i == 0:
            dd += list(bank["rd"]) + list(first_extra)
            if bank["wr"] is not None:
                dd.append(bank["wr"])
        st_ = (i == 0)
        sp_ = (i == n - 1)
        ev = pb.op("tensor",
                   lambda t, lh=lh, rh=rh, st_=st_, sp_=sp_:
                   t.matmul(out_ap, lh, rh, start=st_, stop=sp_,
                            skip_group_check=skip_gc) if skip_gc else
                   t.matmul(out_ap, lh, rh, start=st_, stop=sp_),
                   deps=dd, inc=(inc_last and sp_))
    bank["wr"] = ev
    bank["rd"] = []
    return ev


def read_bank(pb, eng, bank, fn, deps=(), inc=True):
    ev = pb.op(eng, fn, deps=list(deps) + [bank["wr"]], inc=inc)
    bank["rd"].append(ev)
    return ev


class Consts:
    def __init__(self, pb):
        import ml_dtypes
        self.d = {}
        host = {
            "ones_col": np.ones((128, 1), np.float32),
            "ones_row": np.ones((1, 128), np.float32),
            "ident": np.eye(128, dtype=np.float32),
            "cpool": _host_cpool(),
        }
        self.d["ones_row_b"] = pb.dram_in(
            "c_ones_row_b", np.ones((1, 128), ml_dtypes.bfloat16), BF16)
        self.d["ones_col_b"] = pb.dram_in(
            "c_ones_col_b", np.ones((128, 1), ml_dtypes.bfloat16), BF16)
        dd = {k: pb.dram_in("c_" + k, v, F32) for k, v in host.items()}
        self.d.update(dd)
        self.d["ones_row_r"] = pb.dram_in("c_ones_row_r", host["ones_row"], F32R)
        self.d["ones_col_r"] = pb.dram_in("c_ones_col_r", host["ones_col"], F32R)
        self.ones_col = pb.sbuf("ones_col", [128, 1], F32)
        self.ones_row = pb.sbuf("ones_row", [1, 128], F32)
        self.ones_row_r = pb.sbuf("ones_row_r", [1, 128], F32R)
        self.ones_col_r = pb.sbuf("ones_col_r", [128, 1], F32R)
        self.ones_row_b = pb.sbuf("ones_row_b", [1, 128], BF16)
        self.ones_col_b = pb.sbuf("ones_col_b", [128, 1], BF16)
        self.ident = pb.sbuf("ident", [128, 128], F32)
        self.cpool = pb.sbuf("cpool", [128, C_NCONST], F32)

    def load(self, pb):
        for name, sb in [("ones_col", self.ones_col), ("ones_row", self.ones_row),
                         ("ones_row_r", self.ones_row_r),
                         ("ones_col_r", self.ones_col_r),
                         ("ones_row_b", self.ones_row_b),
                         ("ones_col_b", self.ones_col_b),
                         ("ident", self.ident), ("cpool", self.cpool)]:
            pb.dma("sync", "ldc", sb[:], self.d[name])

    def c(self, idx):
        return self.cpool[:, idx:idx + 1]


def ln_stats(pb, cst, get_x, mrow, rrow, name, beta_row=None, war_hook=None):
    sq_t = [pb.sbuf(f"{name}_sqa", [128, TT], F32R),
            pb.sbuf(f"{name}_sqb", [128, TT], F32R)]
    vrow = pb.sbuf(f"{name}_vrow", [1, TT], F32)
    grp_ev = [None, None]
    prev_small = []

    for tt in range(NTT):
        bank_s = pb.bank_fixed("stat")
        bank_q = pb.bank_fixed("c3")
        ps = bank_s["t"][0:1, 0:TT]
        pq = bank_q["t"][0:1, 0:TT]
        mev = None
        for dc in range(NDC):
            x_ap, off, x_ev = get_x(dc, tt)
            assert x_ap.dtype == F32R, f"{name}: ln_stats needs F32R source"
            sq = sq_t[dc % 2]
            sev = pb.op("scalar",
                        lambda s, sq=sq, x_ap=x_ap, o=off:
                        s.activation(out=sq[:], in_=x_ap[:, o:o + TT],
                                     func=AF.Square),
                        deps=[x_ev, grp_ev[dc % 2]], inc=True)
            dd = [x_ev]
            if dc == 0:
                dd += list(bank_s["rd"])
                if bank_s["wr"] is not None:
                    dd.append(bank_s["wr"])
            pb.op("tensor",
                  lambda t, x_ap=x_ap, o=off, f=(dc == 0), ps=ps:
                  t.matmul(ps, cst.ones_col_r[:], x_ap[:, o:o + TT],
                           start=f, stop=(False if dc < NDC - 1 else True),
                           skip_group_check=True),
                  deps=dd, inc=False)
            dd2 = [sev]
            if dc == 0:
                dd2 += list(bank_q["rd"])
                if bank_q["wr"] is not None:
                    dd2.append(bank_q["wr"])
            mev = pb.op("tensor",
                        lambda t, sq=sq, f=(dc == 0), pq=pq:
                        t.matmul(pq, cst.ones_col_r[:], sq[:],
                                 start=f, stop=(False if dc < NDC - 1 else True),
                                 skip_group_check=True),
                        deps=dd2, inc=True)
            grp_ev[dc % 2] = mev
        bank_s["wr"] = mev
        bank_q["wr"] = mev
        bank_s["rd"] = []
        bank_q["rd"] = []
        if war_hook is not None:
            war_hook(tt, mev)
        c0 = tt * TT
        e1 = read_bank(pb, "vector", bank_s,
                       lambda v, ps=ps, c0=c0:
                       v.tensor_scalar(mrow[0:1, c0:c0 + TT], ps,
                                       1.0 / D, None, ALU.mult),
                       deps=prev_small)
        e2 = read_bank(pb, "vector", bank_q,
                       lambda v, pq=pq, c0=c0:
                       v.tensor_scalar(vrow[0:1, :], pq,
                                       1.0 / D, None, ALU.mult),
                       deps=prev_small + [e1])
        e3 = pb.op("scalar",
                   lambda s, c0=c0:
                   s.activation(out=rrow[0:1, c0:c0 + TT],
                                in_=mrow[0:1, c0:c0 + TT], func=AF.Square),
                   deps=[e1], inc=True)
        e4 = pb.op("vector",
                   lambda v, c0=c0:
                   v.tensor_tensor(vrow[0:1, :], vrow[0:1, :],
                                   rrow[0:1, c0:c0 + TT], ALU.subtract),
                   deps=[e2, e3], inc=True)
        if beta_row is not None:
            b1 = pb.op("vector",
                       lambda v, c0=c0:
                       v.tensor_scalar(beta_row[0:1, c0:c0 + TT], vrow[0:1, :],
                                       1.0 + EPS, EPS * EPS, ALU.mult, ALU.add),
                       deps=[e4], inc=True)
            b2 = pb.op("scalar",
                       lambda s, c0=c0:
                       s.activation(out=beta_row[0:1, c0:c0 + TT],
                                    in_=beta_row[0:1, c0:c0 + TT], func=AF.Sqrt),
                       deps=[b1], inc=True)
            b3 = pb.op("vector",
                       lambda v, c0=c0:
                       v.reciprocal(beta_row[0:1, c0:c0 + TT],
                                    beta_row[0:1, c0:c0 + TT]),
                       deps=[b2], inc=True)
        e5 = pb.op("scalar",
                   lambda s, c0=c0:
                   s.activation(out=rrow[0:1, c0:c0 + TT], in_=vrow[0:1, :],
                                func=AF.Sqrt, bias=cst.cpool[0:1, C_EPS:C_EPS + 1],
                                scale=1.0),
                   deps=[e4], inc=True)
        e6 = pb.op("vector",
                   lambda v, c0=c0:
                   v.reciprocal(rrow[0:1, c0:c0 + TT], rrow[0:1, c0:c0 + TT]),
                   deps=[e5], inc=True)
        prev_small = [e6] + ([b3] if beta_row is not None else [])
    return prev_small


def bcast_rows(pb, cst, mrow, rrow, tt, width=TT):
    bm = pb.bank_fixed("bcm")
    mm_seq(pb, bm, bm["t"][:, 0:width],
           [(cst.ones_row[0:1, :], mrow[0:1, tt * TT:tt * TT + width], [])])
    br = pb.bank_fixed("bcr")
    mm_seq(pb, br, br["t"][:, 0:width],
           [(cst.ones_row[0:1, :], rrow[0:1, tt * TT:tt * TT + width], [])])
    return bm, br


def emit_l1ffn(pb, cst, x_d, xo_d, weights):
    x_f = x_d.bitcast(F32)
    xo_b = xo_d.bitcast(F32) if xo_d.dtype == F32R else xo_d
    need_pre(pb, weights, ["l1f_WrT", "l1f_WkT", "l1f_WvT"])
    WrT_d, Wr_ev = pb.pre["l1f_WrT"]
    WkT_d, Wk_ev = pb.pre["l1f_WkT"]
    WvT_d, Wv_ev = pb.pre["l1f_WvT"]
    mu_d = pb.dram_in("l1f_mu", weights["l1f_mu"], F32)

    x = [pb.sbuf(f"x{dc}", [128, L], F32R) for dc in range(NDC)]
    xbuf = [pb.sbuf(f"xb{dc}", [128, L], BF16) for dc in range(NDC)]
    r = [pb.sbuf(f"r{dc}", [128, L], BF16) for dc in range(NDC)]
    kact = [pb.sbuf(f"ka{dc}", [128, L], BF16) for dc in range(NDC)]
    mu = pb.sbuf("mu", [128, 16], F32)
    mu1m = pb.sbuf("mu1m", [128, 16], F32)
    mrow = pb.sbuf("mrow", [1, L], F32)
    rrow = pb.sbuf("rrow", [1, L], F32)
    scr = [pb.sbuf(f"scr{i}", [128, TT], F32) for i in range(3)]
    scrb = pb.sbuf("scrb", [128, TT], BF16)
    tail = pb.sbuf("tail", [128, NDC], F32)
    wbuf = [pb.sbuf(f"wbuf{i}", [128, D], BF16) for i in range(2)]

    with pb.phase():
        for dc in range(NDC):
            pb.dma("sync", "lx0", x[dc][:], x_d[dc * 128:(dc + 1) * 128, :])
        pb.dma("sync", "ld0", mu[:], mu_d)
        ld_all = pb.chan_total("ld0")
        gp_all = pb.chan_total("lx0")
        e = pb.op("vector",
                  lambda v: v.tensor_scalar(mu1m[:], mu[:], 1.0, None, ALU.subtract),
                  deps=[ld_all], inc=True)
        pb.op("vector",
              lambda v: v.tensor_scalar(mu1m[:], mu1m[:], -1.0, None, ALU.mult),
              deps=[e], inc=True)
        pb.waitev("vector", gp_all)
        ln_stats(pb, cst, lambda dc, tt: (x[dc], tt * TT, gp_all), mrow, rrow,
                 "snx")

    def emit_shift_mix(base):
        with pb.phase():
            tl_ev = [None] * NDC
            for dc in range(NDC):
                tl_ev[dc] = pb.op("vector",
                                  lambda v, dc=dc: v.memset(tail[:, dc:dc + 1], 0.0),
                                  inc=True)
            h_last = None
            ht2_last = None
            for tt in range(NTT):
                bm, br = bcast_rows(pb, cst, mrow, rrow, tt)
                for dc in range(NDC):
                    ht = scr[0]
                    h_t = scr[1]
                    e1 = read_bank(pb, "vector", bm,
                                   lambda v, dc=dc, tt=tt, bm=bm, ht=ht:
                                   v.tensor_tensor(ht[:], x[dc][:, tt * TT:(tt + 1) * TT],
                                                   bm["t"][:, 0:TT], ALU.subtract),
                                   deps=[h_last])
                    e2 = read_bank(pb, "vector", br,
                                   lambda v, br=br, ht=ht, h_t=h_t:
                                   v.tensor_tensor(h_t[:], ht[:], br["t"][:, 0:TT],
                                                   ALU.mult),
                                   deps=[e1])
                    et = pb.op("scalar",
                               lambda s, dc=dc, h_t=h_t:
                               s.activation(out=scrb[:], in_=h_t[:], func=AF.Copy,
                                            scale=mu[:, base + dc:base + dc + 1]),
                               deps=[e2, ht2_last], inc=True)
                    e3 = pb.op("vector",
                               lambda v, dc=dc, tt=tt, h_t=h_t:
                               v.scalar_tensor_tensor(
                                   xbuf[dc][:, tt * TT + 1:(tt + 1) * TT],
                                   h_t[:, 0:TT - 1],
                                   mu1m[:, base + dc:base + dc + 1],
                                   scrb[:, 1:TT], ALU.mult, ALU.add),
                               deps=[et, e2], inc=True)
                    e4 = pb.op("vector",
                               lambda v, dc=dc, tt=tt:
                               v.scalar_tensor_tensor(
                                   xbuf[dc][:, tt * TT:tt * TT + 1],
                                   tail[:, dc:dc + 1],
                                   mu1m[:, base + dc:base + dc + 1],
                                   scrb[:, 0:1], ALU.mult, ALU.add),
                               deps=[et, tl_ev[dc]], inc=True)
                    tl_ev[dc] = pb.op("vector",
                                      lambda v, dc=dc, h_t=h_t:
                                      v.tensor_copy(tail[:, dc:dc + 1],
                                                    h_t[:, TT - 1:TT]),
                                      deps=[e2, e4], inc=True)
                    h_last = tl_ev[dc]
                    ht2_last = e4

    def stream_w(w_pair, do, wrd, chan0=0):
        w_d, w_ev = w_pair
        wb = wbuf[do % 2]
        ev = pb.dma("sync", f"ld{1 + do % 2}",
                    wb[:].rearrange("p (c f) -> p c f", f=128),
                    w_d[:, do * 128:(do + 1) * 128].rearrange("(c p) f -> p c f", p=128),
                    deps=[wrd[do % 2]] + w_ev)
        return wb, ev

    r_ev = [[None] * NTT for _ in range(NDC)]
    ka_ev = [[None] * NTT for _ in range(NDC)]

    emit_shift_mix(0)
    with pb.phase():
        wrd = [None, None]
        for do in range(NDC):
            wb, wev = stream_w((WrT_d, Wr_ev), do, wrd)
            for tt in range(NTT):
                bo = pb.bank()
                mms = [(wb[:, dc * 128:(dc + 1) * 128],
                        xbuf[dc][:, tt * TT:(tt + 1) * TT], [wev])
                       for dc in range(NDC)]
                mev = mm_seq(pb, bo, bo["t"][:, 0:TT], mms)
                wrd[do % 2] = mev
                r_ev[do][tt] = read_bank(
                    pb, "scalar", bo,
                    lambda s, do=do, tt=tt, bo=bo:
                    s.activation(out=r[do][:, tt * TT:(tt + 1) * TT],
                                 in_=bo["t"][:, 0:TT], func=AF.Sigmoid))

    emit_shift_mix(8)
    with pb.phase():
        wrd = [None, None]
        relu_last = None
        for do in range(NDC):
            wb, wev = stream_w((WkT_d, Wk_ev), do, wrd)
            for tt in range(NTT):
                bo = pb.bank()
                mms = [(wb[:, dc * 128:(dc + 1) * 128],
                        xbuf[dc][:, tt * TT:(tt + 1) * TT], [wev])
                       for dc in range(NDC)]
                mev = mm_seq(pb, bo, bo["t"][:, 0:TT], mms)
                wrd[do % 2] = mev
                e1 = read_bank(pb, "vector", bo,
                               lambda v, bo=bo:
                               v.tensor_scalar(scr[2][:], bo["t"][:, 0:TT], 0.0,
                                               None, ALU.max),
                               deps=[relu_last])
                ka_ev[do][tt] = pb.op(
                    "scalar",
                    lambda s, do=do, tt=tt:
                    s.activation(out=kact[do][:, tt * TT:(tt + 1) * TT],
                                 in_=scr[2][:], func=AF.Square),
                    deps=[e1], inc=True)
                relu_last = ka_ev[do][tt]

    vrow = pb.sbuf("o_vrow", [1, TT], F32)
    oscr = [pb.sbuf(f"oscr{i}", [128, TT], F32R) for i in range(3)]
    with pb.phase():
        wrd = [None, None]
        bo_rd = [None, None]
        sq_rd = [None, None]
        prev_small = []
        for tt in range(NTT):
            bank_s = pb.bank_fixed("stat")
            bank_q = pb.bank_fixed("c3")
            ps = bank_s["t"][0:1, 0:TT]
            pq = bank_q["t"][0:1, 0:TT]
            last_me = None
            for do in range(NDC):
                wb, wev = stream_w((WvT_d, Wv_ev), do, wrd)
                bo = pb.bank()
                mms = [(wb[:, dc * 128:(dc + 1) * 128],
                        kact[dc][:, tt * TT:(tt + 1) * TT],
                        [wev, ka_ev[dc][tt]])
                       for dc in range(NDC)]
                mev = mm_seq(pb, bo, bo["t"][:, 0:TT], mms)
                wrd[do % 2] = mev
                bt = oscr[do % 2]
                sqt = oscr[2]
                e1 = read_bank(pb, "vector", bo,
                               lambda v, bt=bt, bo=bo, do=do, tt=tt:
                               v.tensor_tensor(bt[:], bo["t"][:, 0:TT],
                                               r[do][:, tt * TT:(tt + 1) * TT],
                                               ALU.mult),
                               deps=[r_ev[do][tt], bo_rd[do % 2]])
                e2 = pb.op("scalar",
                           lambda s, bt=bt, sqt=sqt:
                           s.activation(out=sqt[:], in_=bt[:], func=AF.Square),
                           deps=[e1, sq_rd[0]], inc=True)
                dd = [e1]
                if do == 0:
                    dd += list(bank_s["rd"])
                    if bank_s["wr"] is not None:
                        dd.append(bank_s["wr"])
                pb.op("tensor",
                      lambda t, bt=bt, f=(do == 0), l=(do == NDC - 1), ps=ps:
                      t.matmul(ps, cst.ones_col_r[:], bt[:],
                               start=f, stop=l,
                               skip_group_check=True),
                      deps=dd)
                dd2 = [e2]
                if do == 0:
                    dd2 += list(bank_q["rd"])
                    if bank_q["wr"] is not None:
                        dd2.append(bank_q["wr"])
                last_me = pb.op(
                    "tensor",
                    lambda t, sqt=sqt, f=(do == 0), l=(do == NDC - 1), pq=pq:
                    t.matmul(pq, cst.ones_col_r[:], sqt[:],
                             start=f, stop=l,
                             skip_group_check=True),
                    deps=dd2, inc=True)
                bo_rd[do % 2] = last_me
                sq_rd[0] = last_me
            bank_s["wr"] = last_me
            bank_q["wr"] = last_me
            bank_s["rd"] = []
            bank_q["rd"] = []
            c0 = tt * TT
            e1 = read_bank(pb, "vector", bank_s,
                           lambda v, ps=ps, c0=c0:
                           v.tensor_scalar(mrow[0:1, c0:c0 + TT], ps,
                                           1.0 / D, None, ALU.mult),
                           deps=prev_small)
            e2 = read_bank(pb, "vector", bank_q,
                           lambda v, pq=pq, c0=c0:
                           v.tensor_scalar(vrow[0:1, :], pq,
                                           1.0 / D, None, ALU.mult),
                           deps=prev_small + [e1])
            e3 = pb.op("scalar",
                       lambda s, c0=c0:
                       s.activation(out=rrow[0:1, c0:c0 + TT],
                                    in_=mrow[0:1, c0:c0 + TT], func=AF.Square),
                       deps=[e1], inc=True)
            e4 = pb.op("vector",
                       lambda v, c0=c0:
                       v.tensor_tensor(vrow[0:1, :], vrow[0:1, :],
                                       rrow[0:1, c0:c0 + TT], ALU.subtract),
                       deps=[e2, e3], inc=True)
            e5 = pb.op("scalar",
                       lambda s, c0=c0:
                       s.activation(out=rrow[0:1, c0:c0 + TT], in_=vrow[0:1, :],
                                    func=AF.Sqrt,
                                    bias=cst.cpool[0:1, C_EPS:C_EPS + 1],
                                    scale=1.0),
                       deps=[e4], inc=True)
            e6 = pb.op("vector",
                       lambda v, c0=c0:
                       v.reciprocal(rrow[0:1, c0:c0 + TT],
                                    rrow[0:1, c0:c0 + TT]),
                       deps=[e5], inc=True)
            prev_small = [e6]

    with pb.phase():
        wrd = [None, None]
        t1_rd = [None, None]
        xadd_ev = [[None] * NTT for _ in range(NDC)]
        for tt in range(NTT):
            bm, br = bcast_rows(pb, cst, mrow, rrow, tt)
            for do in range(NDC):
                wb, wev = stream_w((WvT_d, Wv_ev), do, wrd)
                bo = pb.bank()
                mms = [(wb[:, dc * 128:(dc + 1) * 128],
                        kact[dc][:, tt * TT:(tt + 1) * TT],
                        [wev, ka_ev[dc][tt]])
                       for dc in range(NDC)]
                mev = mm_seq(pb, bo, bo["t"][:, 0:TT], mms)
                wrd[do % 2] = mev
                tb = scr[do % 2]
                e1 = read_bank(pb, "vector", bo,
                               lambda v, tb=tb, bo=bo, do=do, tt=tt:
                               v.tensor_tensor(tb[:], bo["t"][:, 0:TT],
                                               r[do][:, tt * TT:(tt + 1) * TT],
                                               ALU.mult),
                               deps=[r_ev[do][tt], t1_rd[do % 2]])
                e2 = read_bank(pb, "vector", bm,
                               lambda v, tb=tb, bm=bm:
                               v.tensor_tensor(tb[:], tb[:], bm["t"][:, 0:TT],
                                               ALU.subtract),
                               deps=[e1])
                e3 = read_bank(pb, "vector", br,
                               lambda v, tb=tb, br=br:
                               v.tensor_tensor(tb[:], tb[:], br["t"][:, 0:TT],
                                               ALU.mult),
                               deps=[e2])
                e4 = pb.op("vector",
                           lambda v, do=do, tt=tt, tb=tb:
                           v.tensor_tensor(x[do][:, tt * TT:(tt + 1) * TT],
                                           x[do][:, tt * TT:(tt + 1) * TT],
                                           tb[:], ALU.add),
                           deps=[e3, xadd_ev[do][tt - 1] if tt > 0 else None],
                           inc=True)
                t1_rd[do % 2] = e4
                xadd_ev[do][tt] = e4
        for dc in range(NDC):
            pb.dma("sync", "stq", xo_b[dc * 128:(dc + 1) * 128, :],
                   x[dc][:].bitcast(F32),
                   deps=[xadd_ev[dc][tt] for tt in range(NTT)])
        pb.waitev("sync", pb.chan_total("stq"))

    return


PRE_ORDER = ["l0a_rqT", "l0a_mqT", "l0a_combT", "l0a_rvT", "l0a_mvT",
             "l0f_tapT", "l0f_wqkvT", "l0f_woT",
             "l0f_w1", "l0f_w2",
             "l1a_wk_lin", "l1a_wv_lin", "l1a_wq_lin", "l1a_wq_loc",
             "l1a_wk_loc", "l1a_wv_loc", "l1a_wg1", "l1a_wg2", "l1a_wo",
             "l1f_WrT", "l1f_WkT", "l1f_WvT"]


def _pre_chans(nm):
    table = {"l0a_rqT": ["pc4"], "l0a_mqT": ["pc5"],
             "l0a_combT": ["pc6", "pc7"],
             "l0a_rvT": ["pc8"], "l0a_mvT": ["pc9"],
             "l0f_tapT": ["pc10"], "l0f_wqkvT": ["pc11"], "l0f_woT": ["pc12"],
             "l0f_w1": ["pc0", "pc1"], "l0f_w2": ["pc2", "pc3"]}
    if nm in table:
        return table[nm]
    return ["pc13"] if nm.startswith("l1a_") else ["pc14"]


def need_pre(pb, weights, names):
    names = [n for n in names if n not in pb.pre]
    if not names:
        return
    used = {}
    with pb.phase():
        for nm in names:
            arr = weights[nm]
            chans = _pre_chans(nm)
            src_ap = pb.dram_in(nm, arr, F32R)
            dst = pb.nc.dram_tensor(nm + "_bf", list(arr.shape), BF16,
                                    kind="Internal").ap()
            if arr.ndim == 3 and arr.shape[0] == 8:
                for e in range(8):
                    pb.dma("gpsimd", chans[e % len(chans)], dst[e], src_ap[e])
            elif len(chans) > 1:
                half = arr.shape[0] // 2
                pb.dma("gpsimd", chans[0], dst[0:half], src_ap[0:half])
                pb.dma("gpsimd", chans[1], dst[half:], src_ap[half:])
            else:
                pb.dma("gpsimd", chans[0], dst, src_ap)
            used[nm] = (dst, chans)
        for nm, (dst, chans) in used.items():
            pb.pre[nm] = (dst, [pb.chan_total(c) for c in chans])


def emit_l0ffn(pb, cst, x_d, xo_d, weights):
    x_f = x_d.bitcast(F32)
    xo_b = xo_d.bitcast(F32) if xo_d.dtype == F32R else xo_d
    h_hbm = pb.dram_tmp("h_hbm", [D, L], BF16)
    ctx_hbm = pb.dram_tmp("ctx_hbm", [D, L], BF16)

    need_pre(pb, weights, ["l0f_tapT", "l0f_wqkvT", "l0f_woT",
                           "l0f_w1", "l0f_w2"])
    tapT_d, tap_ev = pb.pre["l0f_tapT"]
    wqkvT_d, wqkv_ev = pb.pre["l0f_wqkvT"]
    woT_d, wo_ev = pb.pre["l0f_woT"]
    convb_d = pb.dram_in("l0f_convb_c", weights["l0f_convb_c"], F32)
    bqkv_c_d = pb.dram_in("l0f_bqkv_c", weights["l0f_bqkv_c"], F32)
    bqkv_r_d = pb.dram_in("l0f_bqkv_row", weights["l0f_bqkv_row"], F32R)
    bo_c_d = pb.dram_in("l0f_bo_c", weights["l0f_bo_c"], F32)
    ln1g_d = pb.dram_in("l0f_ln1g_c", weights["l0f_ln1g_c"], F32)
    ln1b_d = pb.dram_in("l0f_ln1b_c", weights["l0f_ln1b_c"], F32)
    ln2g_d = pb.dram_in("l0f_ln2g_c", weights["l0f_ln2g_c"], F32)
    ln2b_d = pb.dram_in("l0f_ln2b_c", weights["l0f_ln2b_c"], F32)
    gatew_d = pb.dram_in("l0f_gatew", weights["l0f_gatew"], F32R)
    gateb_d = pb.dram_in("l0f_gateb_row", weights["l0f_gateb_row"], F32R)
    w1bf_d, w1_ev = pb.pre["l0f_w1"]
    w2bf_d, w2_ev = pb.pre["l0f_w2"]
    pre_ev = list(dict.fromkeys(w1_ev + w2_ev))
    b1c_d = pb.dram_in("l0f_b1_c", weights["l0f_b1_c"], F32)
    b2_d = pb.dram_in("l0f_b2", weights["l0f_b2"], F32R)

    xz = [pb.sbuf(f"xz{dc}", [128, L], F32R) for dc in range(NDC)]
    mrow = pb.sbuf("mrow", [1, L], F32)
    rrow = pb.sbuf("rrow", [1, L], F32)
    scr = [pb.sbuf(f"scr{i}", [128, TT], F32) for i in range(3)]
    bias_c = pb.sbuf("bias_c", [128, 64], F32)
    ln2_c = pb.sbuf("ln2_c", [128, 16], F32)
    bqkv_row = pb.sbuf("bqkv_row", [1, D], BF16)
    gateb_row = pb.sbuf("gateb_row", [1, 8], F32R)
    gatew = pb.sbuf("gatew", [128, 64], F32R)
    ones_row_bf = pb.sbuf("ones_row_bf", [1, 128], BF16)
    probs = pb.sbuf("probs", [128, 128], F32)

    with pb.scope():
        xs4 = [[pb.sbuf(f"xs{tt}_{dc}", [128, TT], F32R) for dc in range(NDC)]
               for tt in range(NTT)]
        scr_r = [pb.sbuf(f"scr_r{i}", [128, TT], BF16) for i in range(2)]
        with pb.phase():
            pb.dma("sync", "ld0", bias_c[:, 0:8], convb_d)
            pb.dma("sync", "ld0", bias_c[:, 8:32], bqkv_c_d)
            pb.dma("sync", "ld0", bias_c[:, 32:40], bo_c_d)
            pb.dma("sync", "ld0", bias_c[:, 40:48], ln1g_d)
            pb.dma("sync", "ld0", bias_c[:, 48:56], ln1b_d)
            pb.dma("sync", "ld0", ln2_c[:, 0:8], ln2g_d)
            pb.dma("sync", "ld0", ln2_c[:, 8:16], ln2b_d)
            pb.dma("gpsimd", "gp0", bqkv_row[:], bqkv_r_d[0:1, 2 * D:3 * D])
            pb.dma("sync", "ld0", gateb_row[:], gateb_d)
            pb.dma("sync", "ld0", gatew[:].rearrange("p (c e) -> p c e", e=8),
                   gatew_d.rearrange("(c p) e -> p c e", p=128))
            pb.dma("gpsimd", "gp0", ones_row_bf[:], cst.d["ones_row"])
            for _eng in PB.CENG:
                pb.waitev(_eng, pb.chan_total("ld0"), pb.chan_total("gp0"))
            tot = []
            for tt in range(NTT):
                ch = f"lx{tt}"
                for dc in range(NDC):
                    pb.dma("sync", ch, xs4[tt][dc][:],
                           x_d[dc * 128:(dc + 1) * 128, tt * TT:(tt + 1) * TT])
                tot.append(pb.chan_total(ch))
            ln_stats(pb, cst, lambda dc, tt: (xs4[tt][dc], 0, tot[tt]),
                     mrow, rrow, "s0x")
        with pb.phase():
            last_store = [None, None]
            last1 = None
            for tt in range(NTT):
                bm, br = bcast_rows(pb, cst, mrow, rrow, tt)
                for dc in range(NDC):
                    sr = scr_r[dc % 2]
                    e1 = read_bank(pb, "vector", bm,
                                   lambda v, dc=dc, tt=tt, bm=bm:
                                   v.tensor_tensor(scr[0][:], xs4[tt][dc][:],
                                                   bm["t"][:, 0:TT], ALU.subtract),
                                   deps=[last1])
                    e2 = read_bank(pb, "vector", br,
                                   lambda v, br=br, sr=sr:
                                   v.tensor_tensor(sr[:], scr[0][:],
                                                   br["t"][:, 0:TT], ALU.mult),
                                   deps=[e1, last_store[dc % 2]])
                    last_store[dc % 2] = pb.dma(
                        "sync", ("stq", "st2")[dc % 2],
                        h_hbm[dc * 128:(dc + 1) * 128, tt * TT:(tt + 1) * TT],
                        sr[:], deps=[e2])
                    last1 = e2
        h_done = pb.chan_total("stq")
        h_done2 = pb.chan_total("st2")

    with pb.scope():
        hw = [[pb.sbuf(f"hw{i}_{dc}", [128, TT + 2], BF16) for dc in range(NDC)]
              for i in range(2)]
        wtap = [pb.sbuf(f"wtap{i}", [128, D], BF16) for i in range(3)]
        with pb.phase():
            huse = [None, None]
            wrd = [None, None, None]
            for tt in range(NTT):
                ch = f"ld{1 + tt % 2}"
                lo, hi = tt * TT - 1, tt * TT + TT + 1
                clo, chi = max(lo, 0), min(hi, L)
                for dc in range(NDC):
                    buf = hw[tt % 2][dc]
                    mse = []
                    if lo < 0:
                        mse.append(pb.op("vector",
                                         lambda v, buf=buf: v.memset(buf[:, 0:1], 0.0),
                                         deps=[huse[tt % 2]], inc=True))
                    if hi > L:
                        mse.append(pb.op("vector",
                                         lambda v, buf=buf:
                                         v.memset(buf[:, TT + 1:TT + 2], 0.0),
                                         deps=[huse[tt % 2]], inc=True))
                    pb.dma("sync", ch, buf[:, clo - lo:chi - lo],
                           h_hbm[dc * 128:(dc + 1) * 128, clo:chi],
                           deps=[huse[tt % 2], h_done, h_done2] + mse)
                hev = pb.chan_total(ch)
                for do in range(NDC):
                    bo = pb.bank()
                    mms = []
                    for k in range(3):
                        wt = wtap[k]
                        wev = pb.dma("sync", f"ld{3 + k}",
                                     wt[:].rearrange("p (c f) -> p c f", f=128),
                                     tapT_d[k][:, do * 128:(do + 1) * 128]
                                     .rearrange("(c p) f -> p c f", p=128),
                                     deps=[wrd[k]] + tap_ev)
                        for dc in range(NDC):
                            mms.append((wt[:, dc * 128:(dc + 1) * 128],
                                        hw[tt % 2][dc][:, k:k + TT],
                                        [wev, hev]))
                    mev = mm_seq(pb, bo, bo["t"][:, 0:TT], mms)
                    wrd = [mev, mev, mev]
                    read_bank(pb, "vector", bo,
                              lambda v, do=do, tt=tt, bo=bo:
                              v.tensor_scalar(xz[do][:, tt * TT:(tt + 1) * TT],
                                              bo["t"][:, 0:TT],
                                              bias_c[:, do:do + 1], None, ALU.add))
                    huse[tt % 2] = mev

    with pb.scope():
        v_all = pb.sbuf("v_all", [128, NTC * 1056], BF16)
        xzb = [pb.sbuf(f"xzb{dc}", [128, L], BF16) for dc in range(NDC)]
        with pb.scope():
          wv = pb.sbuf("wv", [128, NDC * TT], BF16)
          with pb.phase():
              msv = pb.op("vector", lambda v: v.memset(v_all[:], 1.0), inc=True)
              xzb_ev = [None] * NDC
              for dc in range(NDC):
                  eng = ("vector", "scalar")[dc % 2]
                  if dc % 2 == 0:
                      xzb_ev[dc] = pb.op("vector",
                                         lambda v, dc=dc:
                                         v.tensor_copy(xzb[dc][:], xz[dc][:]),
                                         inc=True)
                  else:
                      xzb_ev[dc] = pb.op("scalar",
                                         lambda s, dc=dc:
                                         s.activation(out=xzb[dc][:],
                                                      in_=xz[dc][:], func=AF.Copy),
                                         inc=True)
              prev_cp = None
              last_mv = None
              for half in range(2):
                  wev = pb.dma("sync", "ld3",
                               wv[:].rearrange("p (c f) -> p c f", f=TT),
                               wqkvT_d[:, 2 * D + half * TT:2 * D + (half + 1) * TT]
                               .rearrange("(c p) f -> p c f", p=128),
                               deps=[pb.chan_total("ld3"), last_mv] + wqkv_ev)
                  for tc in range(NTC):
                      bv = pb.bank()
                      mms = [(ones_row_bf[0:1, :],
                              bqkv_row[0:1, half * TT:(half + 1) * TT],
                              [])]
                      mms += [(xzb[dc][:, tc * 128:(tc + 1) * 128],
                               wv[:, dc * TT:(dc + 1) * TT],
                               [wev, xzb_ev[dc]])
                              for dc in range(NDC)]
                      mv_ = mm_seq(pb, bv, bv["t"][:, 0:TT], mms)
                      last_mv = mv_
                      for bq in range(4):
                          hp = half * 4 + bq
                          base = tc * 1056 + hp * 132
                          prev_cp = read_bank(
                              pb, "vector", bv,
                              lambda v, bv=bv, bq=bq, base=base:
                              v.tensor_copy(v_all[:, base:base + 64],
                                            bv["t"][:, bq * 128:bq * 128 + 64]),
                              deps=[msv, prev_cp])
                          prev_cp = read_bank(
                              pb, "vector", bv,
                              lambda v, bv=bv, bq=bq, base=base:
                              v.tensor_copy(v_all[:, base + 66:base + 130],
                                            bv["t"][:, bq * 128 + 64:bq * 128 + 128]),
                              deps=[prev_cp])
        qp = pb.sbuf("qp", [128, L], BF16)
        kp = pb.sbuf("kp", [128, L], BF16)
        wq = [pb.sbuf(f"wq{i}", [128, D], BF16) for i in range(2)]
        expt = [pb.sbuf(f"expt{i}", [128, TT], BF16) for i in range(4)]
        ctxp = pb.sbuf("ctxp", [128, L], BF16)
        sums_r = pb.sbuf("sums_r", [1, TT], F32)
        rc_sb = pb.sbuf("rc_sb", [128, TT], F32)
        with pb.phase():
            qp_ev = [None] * NTT
            kp_ev = [None] * NTT
            wrd_q = [None, None]
            last_pe_prev_hp = None
            prev_ctx_store = None
            prev_sums = None
            prev_rc = None
            for hp in range(NDC):
                wev_q = pb.dma("sync", "ld1",
                               wq[0][:].rearrange("p (c f) -> p c f", f=128),
                               wqkvT_d[:, hp * 128:(hp + 1) * 128]
                               .rearrange("(c p) f -> p c f", p=128),
                               deps=[wrd_q[0]] + wqkv_ev)
                wev_k = pb.dma("sync", "ld2",
                               wq[1][:].rearrange("p (c f) -> p c f", f=128),
                               wqkvT_d[:, D + hp * 128:D + (hp + 1) * 128]
                               .rearrange("(c p) f -> p c f", p=128),
                               deps=[wrd_q[1]] + wqkv_ev)
                for tt in range(NTT):
                    bq = pb.bank()
                    mms = [(wq[0][:, dc * 128:(dc + 1) * 128],
                            xzb[dc][:, tt * TT:(tt + 1) * TT],
                            [wev_q, xzb_ev[dc]])
                           for dc in range(NDC)]
                    mq = mm_seq(pb, bq, bq["t"][:, 0:TT], mms)
                    qp_ev[tt] = read_bank(pb, "vector", bq,
                                          lambda v, tt=tt, bq=bq, hp=hp:
                                          v.tensor_scalar(
                                              qp[:, tt * TT:(tt + 1) * TT],
                                              bq["t"][:, 0:TT],
                                              bias_c[:, 8 + hp:9 + hp],
                                              0.125, ALU.add, ALU.mult),
                                          deps=[last_pe_prev_hp])
                    bk = pb.bank()
                    mms = [(wq[1][:, dc * 128:(dc + 1) * 128],
                            xzb[dc][:, tt * TT:(tt + 1) * TT],
                            [wev_k, xzb_ev[dc]])
                           for dc in range(NDC)]
                    mk = mm_seq(pb, bk, bk["t"][:, 0:TT], mms)
                    kp_ev[tt] = read_bank(pb, "vector", bk,
                                          lambda v, tt=tt, bk=bk, hp=hp:
                                          v.tensor_scalar(
                                              kp[:, tt * TT:(tt + 1) * TT],
                                              bk["t"][:, 0:TT],
                                              bias_c[:, 16 + hp:17 + hp],
                                              None, ALU.add),
                                          deps=[last_pe_prev_hp])
                    wrd_q[0] = mq
                    wrd_q[1] = mk
                last_mm = None
                for qt in range(NTT):
                    bca = pb.bank_fixed("bcm")
                    bcb = pb.bank_fixed("bcr")
                    exp_rd = [None, None, None, None]
                    mca = mcb = None
                    for skc in range(NTC):
                        bsA = pb.bank()
                        mA = mm_seq(pb, bsA, bsA["t"][:, 0:TT],
                                    [(kp[0:64, skc * 128:(skc + 1) * 128],
                                      qp[0:64, qt * TT:(qt + 1) * TT],
                                      [kp_ev[skc // 4], qp_ev[qt]])])
                        eA = read_bank(pb, "scalar", bsA,
                                       lambda s, bsA=bsA, i=(skc % 2):
                                       s.activation(out=expt[i][:],
                                                    in_=bsA["t"][:, 0:TT],
                                                    func=AF.Exp),
                                       deps=[exp_rd[skc % 2]])
                        bsB = pb.bank()
                        mB = mm_seq(pb, bsB, bsB["t"][:, 0:TT],
                                    [(kp[64:128, skc * 128:(skc + 1) * 128],
                                      qp[64:128, qt * TT:(qt + 1) * TT],
                                      [kp_ev[skc // 4], qp_ev[qt]])])
                        eB = read_bank(pb, "scalar", bsB,
                                       lambda s, bsB=bsB, i=2 + (skc % 2):
                                       s.activation(out=expt[i][:],
                                                    in_=bsB["t"][:, 0:TT],
                                                    func=AF.Exp),
                                       deps=[exp_rd[2 + skc % 2]])
                        vbase = skc * 1056 + hp * 132
                        ddA = [eA]
                        if skc == 0:
                            ddA += list(bca["rd"]) + ([bca["wr"]] if bca["wr"] else [])
                        mca = pb.op("tensor",
                                    lambda t, vbase=vbase, i=(skc % 2), bca=bca,
                                    skc=skc:
                                    t.matmul(bca["t"][0:66, 0:TT],
                                             v_all[:, vbase:vbase + 66],
                                             expt[i][:], start=(skc == 0),
                                             stop=(skc == NTC - 1)),
                                    deps=ddA, inc=True)
                        exp_rd[skc % 2] = mca
                        ddB = [eB]
                        if skc == 0:
                            ddB += list(bcb["rd"]) + ([bcb["wr"]] if bcb["wr"] else [])
                        mcb = pb.op("tensor",
                                    lambda t, vbase=vbase, i=2 + (skc % 2), bcb=bcb,
                                    skc=skc:
                                    t.matmul(bcb["t"][0:66, 0:TT],
                                             v_all[:, vbase + 66:vbase + 132],
                                             expt[i][:], start=(skc == 0),
                                             stop=(skc == NTC - 1)),
                                    deps=ddB, inc=True)
                        exp_rd[2 + skc % 2] = mcb
                    bca["wr"] = mca
                    bca["rd"] = []
                    bcb["wr"] = mcb
                    bcb["rd"] = []
                    last_mm = mcb
                    for side, bank_c, base in (("A", bca, 0), ("B", bcb, 64)):
                        e1 = read_bank(pb, "vector", bank_c,
                                       lambda v, bank_c=bank_c:
                                       v.tensor_copy(sums_r[0:1, :],
                                                     bank_c["t"][64:65, 0:TT]),
                                       deps=[prev_sums])
                        e2 = pb.op("vector",
                                   lambda v: v.reciprocal(sums_r[0:1, :],
                                                          sums_r[0:1, :]),
                                   deps=[e1], inc=True)
                        prev_sums = e2
                        brc = pb.bank()
                        mrc = mm_seq(pb, brc, brc["t"][0:64, 0:TT],
                                     [(cst.ones_row[0:1, 0:64], sums_r[0:1, :],
                                       [e2])])
                        e3 = read_bank(pb, "vector", brc,
                                       lambda v, base=base, brc=brc:
                                       v.tensor_copy(rc_sb[base:base + 64, :],
                                                     brc["t"][0:64, 0:TT]),
                                       deps=[prev_rc])
                        prev_rc = e3
                        e4 = read_bank(pb, "vector", bank_c,
                                       lambda v, base=base, qt=qt, bank_c=bank_c:
                                       v.tensor_tensor(
                                           ctxp[base:base + 64,
                                                qt * TT:(qt + 1) * TT],
                                           bank_c["t"][0:64, 0:TT],
                                           rc_sb[base:base + 64, :], ALU.mult),
                                       deps=[e3, prev_ctx_store])
                        last_ctx = e4
                prev_ctx_store = pb.dma("sync", "stq",
                                        ctx_hbm[hp * 128:(hp + 1) * 128, :], ctxp[:],
                                        deps=[last_ctx,
                                              pb.chan_total("stq")])
                last_pe_prev_hp = last_mm

    with pb.scope():
        ctxs = [[pb.sbuf(f"ctxs{i}_{c}", [128, TT], BF16) for c in range(NDC)]
                for i in range(2)]
        wo_bf = [pb.sbuf(f"wo_bf{i}", [128, D], BF16) for i in range(2)]
        with pb.phase():
            cuse = [None, None]
            wrd = [None, None]
            for tt in range(NTT):
                ch = f"ld{1 + tt % 2}"
                for c in range(NDC):
                    pb.dma("sync", ch, ctxs[tt % 2][c][:],
                           ctx_hbm[c * 128:(c + 1) * 128, tt * TT:(tt + 1) * TT],
                           deps=[cuse[tt % 2], pb.chan_total("stq")])
                ctot = pb.chan_total(ch)
                for do in range(NDC):
                    wb = wo_bf[do % 2]
                    wev = pb.dma("sync", f"ld{3 + do % 2}",
                                 wb[:].rearrange("p (c f) -> p c f", f=128),
                                 woT_d[:, do * 128:(do + 1) * 128]
                                 .rearrange("(c p) f -> p c f", p=128),
                                 deps=[wrd[do % 2]] + wo_ev)
                    bo = pb.bank()
                    mms = [(wb[:, c * 128:(c + 1) * 128], ctxs[tt % 2][c][:],
                            [wev, ctot]) for c in range(NDC)]
                    mev = mm_seq(pb, bo, bo["t"][:, 0:TT], mms)
                    wrd[do % 2] = mev
                    read_bank(pb, "vector", bo,
                              lambda v, do=do, tt=tt, bo=bo:
                              v.scalar_tensor_tensor(
                                  xz[do][:, tt * TT:(tt + 1) * TT],
                                  bo["t"][:, 0:TT], bias_c[:, 32 + do:33 + do],
                                  xz[do][:, tt * TT:(tt + 1) * TT],
                                  ALU.add, ALU.add))
                    cuse[tt % 2] = mev

    with pb.phase():
        ln_stats(pb, cst, lambda dc, tt: (xz[dc], tt * TT, None), mrow, rrow, "s1")
    with pb.phase():
        last = None
        for tt in range(NTT):
            bm, br = bcast_rows(pb, cst, mrow, rrow, tt)
            for dc in range(NDC):
                e1 = read_bank(pb, "vector", bm,
                               lambda v, dc=dc, tt=tt, bm=bm:
                               v.tensor_tensor(scr[0][:],
                                               xz[dc][:, tt * TT:(tt + 1) * TT],
                                               bm["t"][:, 0:TT], ALU.subtract),
                               deps=[last])
                e2 = read_bank(pb, "vector", br,
                               lambda v, br=br:
                               v.tensor_tensor(scr[1][:], scr[0][:],
                                               br["t"][:, 0:TT], ALU.mult),
                               deps=[e1])
                e3 = pb.op("vector",
                           lambda v, dc=dc, tt=tt:
                           v.tensor_scalar(xz[dc][:, tt * TT:(tt + 1) * TT],
                                           scr[1][:],
                                           bias_c[:, 40 + dc:41 + dc],
                                           bias_c[:, 48 + dc:49 + dc],
                                           ALU.mult, ALU.add),
                           deps=[e2], inc=True)
                last = e3

    with pb.phase():
        gp = pb.sbuf("gp", [128, 8], F32)
        mx = pb.sbuf("gmx", [128, 1], F32)
        exg = pb.sbuf("exg", [128, 8], F32)
        sm = pb.sbuf("gsm", [128, 1], F32)
        prevp = None
        for tc in range(NTC):
            bg = pb.bank()
            mms = [(cst.ones_row_r[0:1, :], gateb_row[0:1, :], [])]
            mms += [(xz[dc][:, tc * 128:(tc + 1) * 128],
                     gatew[:, dc * 8:(dc + 1) * 8], []) for dc in range(NDC)]
            mg = mm_seq(pb, bg, bg["t"][:, 0:8], mms)
            g1 = read_bank(pb, "vector", bg,
                           lambda v, bg=bg: v.tensor_copy(gp[:], bg["t"][:, 0:8]),
                           deps=[prevp])
            g2 = pb.op("vector", lambda v: v.reduce_max(mx[:], gp[:], axis=AX.X),
                       deps=[g1], inc=True)
            g3 = pb.op("vector", lambda v: v.tensor_scalar(mx[:], mx[:], -1.0,
                                                           None, ALU.mult),
                       deps=[g2], inc=True)
            g4 = pb.op("scalar", lambda s: s.activation(out=exg[:], in_=gp[:],
                                                        func=AF.Exp,
                                                        bias=mx[:, 0:1], scale=1.0),
                       deps=[g3], inc=True)
            g5 = pb.op("vector", lambda v: v.reduce_sum(sm[:], exg[:], axis=AX.X),
                       deps=[g4], inc=True)
            g6 = pb.op("vector", lambda v: v.reciprocal(sm[:], sm[:]),
                       deps=[g5], inc=True)
            g7 = pb.op("vector",
                       lambda v, tc=tc:
                       v.tensor_scalar(probs[:, tc * 8:(tc + 1) * 8], exg[:],
                                       sm[:, 0:1], None, ALU.mult),
                       deps=[g6], inc=True)
            prevp = g7

    with pb.scope():
        h_e = [pb.sbuf(f"he{fc}", [128, TT], BF16) for fc in range(32)]
        y_acc = [pb.sbuf(f"ya{tc}", [128, D], F32) for tc in range(4)]
        x2bf = [pb.sbuf(f"x2bf{dc}", [128, L], BF16) for dc in range(NDC)]
        w1b = [pb.sbuf(f"w1b{i}", [128, 8 * TT], BF16) for i in range(2)]
        w2b = [pb.sbuf(f"w2b{i}", [128, 2 * TT], BF16) for i in range(2)]
        b2row = pb.sbuf("b2row", [1, D], BF16)
        b1c = pb.sbuf("b1c", [128, 32], F32)
        with pb.phase():
            x2bf_ev = [None] * NDC
            for dc in range(NDC):
                x2bf_ev[dc] = pb.op("vector",
                                    lambda v, dc=dc:
                                    v.tensor_copy(x2bf[dc][:], xz[dc][:]),
                                    inc=True)
            he_ev = [None] * 32
            prev_y_last = None
            prev_tr_last = None
            wrd = [None, None]
            wrd2 = [None, None]
            zadd_prev = [None] * NDC
            for ts in range(NTT):
                ya_ev = [pb.op("vector",
                               lambda v, tc=tc: v.memset(y_acc[tc][:], 0.0),
                               deps=[prev_tr_last], inc=True) for tc in range(4)]
                yb = [pb.bank_fixed(f"moe{tc}") for tc in range(4)]
                for e in range(8):
                    b2ev = pb.dma("gpsimd", "gp5", b2row[:], b2_d[e:e + 1, :],
                                  deps=[prev_y_last])
                    b1ev = pb.dma("sync", "ld3", b1c[:], b1c_d[e],
                                  deps=[he_ev[31]])
                    wevg = [None] * 8

                    def w1_load(g, dep, e=e):
                        return pb.dma(
                            "sync", f"ld{1 + g % 2}",
                            w1b[g % 2][:].rearrange("p (c f) -> p c f", f=TT),
                            w1bf_d[e][:, g * TT:(g + 1) * TT]
                            .rearrange("(c p) f -> p c f", p=128),
                            deps=[dep] + pre_ev)
                    for g in range(2):
                        wevg[g] = w1_load(g, wrd[g % 2])
                    for fc in range(32):
                        g = fc // 4
                        wb = w1b[g % 2]
                        off = (fc % 4) * 128
                        bh = pb.bank()
                        mms = [(wb[:, dc * TT + off:dc * TT + off + 128],
                                x2bf[dc][:, ts * TT:(ts + 1) * TT],
                                [wevg[g], x2bf_ev[dc]])
                               for dc in range(NDC)]
                        mh = mm_seq(pb, bh, bh["t"][:, 0:TT], mms)
                        if fc % 4 == 3:
                            wrd[g % 2] = mh
                            if g + 2 < 8:
                                wevg[g + 2] = w1_load(g + 2, mh)
                        he_ev[fc] = read_bank(
                            pb, "scalar", bh,
                            lambda s, fc=fc, bh=bh:
                            s.activation(out=h_e[fc][:], in_=bh["t"][:, 0:TT],
                                         func=AF.Relu, bias=b1c[:, fc:fc + 1],
                                         scale=1.0),
                            deps=[b1ev, prev_y_last])
                    for dh in range(2):
                        for tc4 in range(4):
                            bank = yb[tc4]
                            dd = list(bank["rd"]) + \
                                ([bank["wr"]] if bank["wr"] else [])
                            pb.op("tensor",
                                  lambda t, dh=dh, bank=bank:
                                  t.matmul(bank["t"][:, 0:TT],
                                           ones_row_bf[0:1, :],
                                           b2row[0:1, dh * TT:(dh + 1) * TT],
                                           start=True, stop=False,
                                           skip_group_check=True),
                                  deps=dd + [b2ev])
                            bank["rd"] = []
                        myev = None
                        for fc in range(32):
                            g2 = fc // 2
                            wb2 = w2b[g2 % 2]
                            if fc % 2 == 0:
                                wev2 = pb.dma(
                                    "sync", f"ld{4 + g2 % 2}",
                                    wb2[:].rearrange("p (a f) -> p a f", f=TT),
                                    w2bf_d[e][fc * 128:(fc + 2) * 128,
                                              dh * TT:(dh + 1) * TT]
                                    .rearrange("(a p) f -> p a f", p=128),
                                    deps=[wrd2[g2 % 2]] + pre_ev)
                            woff = (fc % 2) * TT
                            for tc4 in range(4):
                                bank = yb[tc4]
                                myev = pb.op(
                                    "tensor",
                                    lambda t, fc=fc, tc4=tc4, bank=bank, wb2=wb2,
                                    woff=woff:
                                    t.matmul(bank["t"][:, 0:TT],
                                             h_e[fc][:, tc4 * 128:(tc4 + 1) * 128],
                                             wb2[:, woff:woff + TT],
                                             start=False, stop=(fc == 31),
                                             skip_group_check=True),
                                    deps=[wev2, he_ev[fc]],
                                    inc=(tc4 == 3))
                            if fc % 2 == 1:
                                wrd2[g2 % 2] = myev
                        prev_y_last = myev
                        for tc4 in range(4):
                            bank = yb[tc4]
                            bank["wr"] = myev
                            tcg = ts * 4 + tc4
                            ya_ev[tc4] = read_bank(
                                pb, "vector", bank,
                                lambda v, tc4=tc4, dh=dh, e=e, tcg=tcg, bank=bank:
                                v.scalar_tensor_tensor(
                                    y_acc[tc4][:, dh * TT:(dh + 1) * TT],
                                    bank["t"][:, 0:TT],
                                    probs[:, tcg * 8 + e:tcg * 8 + e + 1],
                                    y_acc[tc4][:, dh * TT:(dh + 1) * TT],
                                    ALU.mult, ALU.add),
                                deps=[ya_ev[tc4]])
                for tc4 in range(4):
                    tcg = ts * 4 + tc4
                    for dc in range(NDC):
                        bt = pb.bank()
                        dd = [ya_ev[tc4]] + list(bt["rd"]) + \
                            ([bt["wr"]] if bt["wr"] else [])
                        te = pb.op("tensor",
                                   lambda t, tc4=tc4, dc=dc, bt=bt:
                                   t.transpose(bt["t"][:, 0:128],
                                               y_acc[tc4][:, dc * 128:(dc + 1) * 128],
                                               cst.ident[:]),
                                   deps=dd, inc=True)
                        bt["wr"] = te
                        bt["rd"] = []
                        prev_tr_last = te
                        zadd_prev[dc] = read_bank(
                            pb, "vector", bt,
                            lambda v, dc=dc, tcg=tcg, bt=bt:
                            v.tensor_tensor(xz[dc][:, tcg * 128:(tcg + 1) * 128],
                                            xz[dc][:, tcg * 128:(tcg + 1) * 128],
                                            bt["t"][:, 0:128], ALU.add),
                            deps=[zadd_prev[dc]])

    with pb.phase():
        ln_stats(pb, cst, lambda dc, tt: (xz[dc], tt * TT, None), mrow, rrow, "s2")
    with pb.phase():
        last = None
        for tt in range(NTT):
            bm, br = bcast_rows(pb, cst, mrow, rrow, tt)
            for dc in range(NDC):
                e1 = read_bank(pb, "vector", bm,
                               lambda v, dc=dc, tt=tt, bm=bm:
                               v.tensor_tensor(scr[0][:],
                                               xz[dc][:, tt * TT:(tt + 1) * TT],
                                               bm["t"][:, 0:TT], ALU.subtract),
                               deps=[last])
                e2 = read_bank(pb, "vector", br,
                               lambda v, br=br:
                               v.tensor_tensor(scr[1][:], scr[0][:],
                                               br["t"][:, 0:TT], ALU.mult),
                               deps=[e1])
                e3 = pb.op("vector",
                           lambda v, dc=dc, tt=tt:
                           v.tensor_scalar(xz[dc][:, tt * TT:(tt + 1) * TT],
                                           scr[1][:],
                                           ln2_c[:, dc:dc + 1],
                                           ln2_c[:, 8 + dc:9 + dc],
                                           ALU.mult, ALU.add),
                           deps=[e2], inc=True)
                last = e3
    with pb.phase():
        ln_stats(pb, cst, lambda dc, tt: (xz[dc], tt * TT, None), mrow, rrow, "s3")

    with pb.scope():
        xs2 = [[pb.sbuf(f"xf{i}_{dc}", [128, TT], F32) for dc in range(NDC)]
               for i in range(2)]
        out_t = [pb.sbuf(f"out_t{i}", [128, TT], F32) for i in range(2)]
        with pb.phase():
            xuse = [None, None]
            ost = [None, None]
            last = None
            for tt in range(NTT):
                ch = f"ld{1 + tt % 2}"
                for dc in range(NDC):
                    pb.dma("sync", ch, xs2[tt % 2][dc][:],
                           x_f[dc * 128:(dc + 1) * 128, tt * TT:(tt + 1) * TT],
                           deps=[xuse[tt % 2]])
                tot = pb.chan_total(ch)
                bm, br = bcast_rows(pb, cst, mrow, rrow, tt)
                for dc in range(NDC):
                    ot = out_t[dc % 2]
                    e1 = read_bank(pb, "vector", bm,
                                   lambda v, dc=dc, tt=tt, bm=bm:
                                   v.tensor_tensor(scr[0][:],
                                                   xz[dc][:, tt * TT:(tt + 1) * TT],
                                                   bm["t"][:, 0:TT], ALU.subtract),
                                   deps=[last])
                    e2 = read_bank(pb, "vector", br,
                                   lambda v, br=br:
                                   v.tensor_tensor(scr[1][:], scr[0][:],
                                                   br["t"][:, 0:TT], ALU.mult),
                                   deps=[e1])
                    e3 = pb.op("vector",
                               lambda v, dc=dc, tt=tt, ot=ot:
                               v.tensor_tensor(ot[:], scr[1][:],
                                               xs2[tt % 2][dc][:], ALU.add),
                               deps=[e2, tot, ost[dc % 2]], inc=True)
                    ost[dc % 2] = pb.dma(
                        "sync", ("stq", "st2")[dc % 2],
                        xo_b[dc * 128:(dc + 1) * 128, tt * TT:(tt + 1) * TT],
                        ot[:], deps=[e3])
                    last = e3
                    xuse[tt % 2] = e3
            pb.waitev("sync", pb.chan_total("stq"), pb.chan_total("st2"))

    return


def emit_l1attn(pb, cst, x_d, xo_d, weights):
    x_f = x_d.bitcast(F32)
    xo_b = xo_d.bitcast(F32) if xo_d.dtype == F32R else xo_d
    lin_hbm = pb.dram_tmp("lin_hbm", [D, L], BF16)
    win_hbm = pb.dram_tmp("win_hbm", [D, L], BF16)
    z_hbm = pb.dram_tmp("z_hbm", [D, L], F32R)

    need_pre(pb, weights, ["l1a_wk_lin", "l1a_wv_lin", "l1a_wq_lin",
                           "l1a_wq_loc", "l1a_wk_loc", "l1a_wv_loc",
                           "l1a_wg1", "l1a_wg2", "l1a_wo"])
    wql_d, wql_ev = pb.pre["l1a_wq_lin"]
    wkl_d, wkl_ev = pb.pre["l1a_wk_lin"]
    wvl_d, wvl_ev = pb.pre["l1a_wv_lin"]
    wqo_d, wqo_ev = pb.pre["l1a_wq_loc"]
    wko_d, wko_ev = pb.pre["l1a_wk_loc"]
    wvo_d, wvo_ev = pb.pre["l1a_wv_loc"]
    wo_d, wol_ev = pb.pre["l1a_wo"]
    wg1_d, wg1_ev = pb.pre["l1a_wg1"]
    wg2_d, wg2_ev = pb.pre["l1a_wg2"]
    bias_d = pb.dram_in("l1a_bias_c", weights["l1a_bias_c"], F32)
    brow_d = pb.dram_in("l1a_brow", weights["l1a_brow"], F32R)
    bt_d = pb.dram_in("l1a_BT", weights["l1a_BT"], F32)

    h = [pb.sbuf(f"h{dc}", [128, L], BF16) for dc in range(NDC)]
    m0row = pb.sbuf("m0row", [1, L], F32)
    r0row = pb.sbuf("r0row", [1, L], F32)
    scr = [pb.sbuf(f"scr{i}", [128, TT], F32) for i in range(4)]
    bias_c = pb.sbuf("bias_c", [128, 64], F32)
    brow = pb.sbuf("brow", [1, 3 * D], BF16)

    with pb.scope():
        betarow = pb.sbuf("betarow", [1, L], F32)
        xs4 = [[pb.sbuf(f"xs{tt}_{dc}", [128, TT], F32R) for dc in range(NDC)]
               for tt in range(NTT)]
        with pb.phase():
            pb.dma("sync", "ld0", bias_c[:], bias_d)
            pb.dma("gpsimd", "gp0", brow[:], brow_d.rearrange("a b -> (a b)").unsqueeze(0))
            for _eng in PB.CENG:
                pb.waitev(_eng, pb.chan_total("ld0"), pb.chan_total("gp0"))
            tot = []
            for tt in range(NTT):
                ch = f"lx{tt}"
                for dc in range(NDC):
                    pb.dma("sync", ch, xs4[tt][dc][:],
                           x_d[dc * 128:(dc + 1) * 128, tt * TT:(tt + 1) * TT])
                tot.append(pb.chan_total(ch))
            ln_stats(pb, cst, lambda dc, tt: (xs4[tt][dc], 0, tot[tt]),
                     m0row, r0row, "sx", beta_row=betarow)
        with pb.phase():
            last = None
            for tt in range(NTT):
                bm, br = bcast_rows(pb, cst, m0row, betarow, tt)
                for dc in range(NDC):
                    e1 = read_bank(pb, "vector", bm,
                                   lambda v, dc=dc, tt=tt, bm=bm:
                                   v.tensor_tensor(scr[0][:], xs4[tt][dc][:],
                                                   bm["t"][:, 0:TT], ALU.subtract),
                                   deps=[last])
                    e2 = read_bank(pb, "vector", br,
                                   lambda v, br=br:
                                   v.tensor_tensor(scr[1][:], scr[0][:],
                                                   br["t"][:, 0:TT], ALU.mult),
                                   deps=[e1])
                    e3 = pb.op("vector",
                               lambda v, dc=dc, tt=tt:
                               v.tensor_scalar(h[dc][:, tt * TT:(tt + 1) * TT],
                                               scr[1][:],
                                               bias_c[:, 48 + dc:49 + dc],
                                               bias_c[:, 56 + dc:57 + dc],
                                               ALU.mult, ALU.add),
                               deps=[e2], inc=True)
                    last = e3

    with pb.scope():
        kv = [pb.sbuf(f"kv{dc}", [128, D], BF16) for dc in range(NDC)]
        ksum_c = pb.sbuf("ksum_c", [128, 8], BF16)
        kv_ev = [[None] * 2 for _ in range(NDC)]
        ksc_ev = None
        with pb.scope():
            kvf = [pb.sbuf(f"kvf{dc}", [128, D], F32) for dc in range(NDC)]
            ktok = [pb.sbuf(f"ktok{i}", [128, D], BF16) for i in range(2)]
            vtok = [pb.sbuf(f"vtok{i}", [128, D], BF16) for i in range(2)]
            wkh = [pb.sbuf(f"wkh{i}", [128, NDC * TT], BF16) for i in range(2)]
            wvh = [pb.sbuf(f"wvh{i}", [128, NDC * TT], BF16) for i in range(2)]
            with pb.phase():
                for half in range(2):
                    pb.dma("sync", f"ld{1 + half}",
                           wkh[half][:].rearrange("p (c f) -> p c f", f=TT),
                           wkl_d[:, half * TT:(half + 1) * TT]
                           .rearrange("(c p) f -> p c f", p=128),
                           deps=wkl_ev)
                    pb.dma("sync", f"ld{3 + half}",
                           wvh[half][:].rearrange("p (c f) -> p c f", f=TT),
                           wvl_d[:, half * TT:(half + 1) * TT]
                           .rearrange("(c p) f -> p c f", p=128),
                           deps=wvl_ev)
                wk_ev = [pb.chan_total("ld1"), pb.chan_total("ld2")]
                wv_ev = [pb.chan_total("ld3"), pb.chan_total("ld4")]
                el_last = {}
                ktok_rd = [None] * 2
                vtok_rd = [None] * 2
                bstat = pb.bank_fixed("stat")
                first_ks = True
                ks_last = None
                for sg in range(8):
                    kt_ev = [None] * 2
                    vt_ev = [None] * 2
                    for sc4 in range(2):
                        scg = sg * 2 + sc4
                        for half in range(2):
                            bk = pb.bank()
                            mms = [(cst.ones_row_b[0:1, :],
                                    brow[0:1, half * TT:(half + 1) * TT], [])]
                            mms += [(h[dc][:, scg * 128:(scg + 1) * 128],
                                     wkh[half][:, dc * TT:(dc + 1) * TT],
                                     [wk_ev[half]]) for dc in range(NDC)]
                            mk = mm_seq(pb, bk, bk["t"][:, 0:TT], mms)
                            e1 = read_bank(pb, "vector", bk,
                                           lambda v, bk=bk:
                                           v.tensor_scalar(scr[0][:],
                                                           bk["t"][:, 0:TT],
                                                           0.0, None, ALU.min),
                                           deps=[el_last.get("s0")])
                            e2 = pb.op("scalar",
                                       lambda s:
                                       s.activation(out=scr[1][:], in_=scr[0][:],
                                                    func=AF.Exp),
                                       deps=[e1, el_last.get("s1")], inc=True)
                            e3 = read_bank(pb, "vector", bk,
                                           lambda v, bk=bk:
                                           v.tensor_scalar(scr[2][:],
                                                           bk["t"][:, 0:TT],
                                                           0.0, None, ALU.max),
                                           deps=[el_last.get("s2")])
                            e4 = pb.op("vector",
                                       lambda v, sc4=sc4, half=half:
                                       v.tensor_tensor(
                                           ktok[sc4][:, half * TT:(half + 1) * TT],
                                           scr[1][:], scr[2][:], ALU.add),
                                       deps=[e2, e3, ktok_rd[sc4]], inc=True)
                            el_last["s0"] = e4
                            el_last["s1"] = e4
                            el_last["s2"] = e4
                            kt_ev[sc4] = e4
                            bv = pb.bank()
                            mms = [(cst.ones_row_b[0:1, :],
                                    brow[0:1, D + half * TT:D + (half + 1) * TT],
                                    [])]
                            mms += [(h[dc][:, scg * 128:(scg + 1) * 128],
                                     wvh[half][:, dc * TT:(dc + 1) * TT],
                                     [wv_ev[half]]) for dc in range(NDC)]
                            mv = mm_seq(pb, bv, bv["t"][:, 0:TT], mms)
                            vt_ev[sc4] = read_bank(
                                pb, "vector", bv,
                                lambda v, sc4=sc4, half=half, bv=bv:
                                v.tensor_copy(
                                    vtok[sc4][:, half * TT:(half + 1) * TT],
                                    bv["t"][:, 0:TT]),
                                deps=[vtok_rd[sc4]])
                        for dc in range(NDC):
                            dd = [kt_ev[sc4]]
                            if first_ks:
                                dd += list(bstat["rd"])
                                if bstat["wr"] is not None:
                                    dd.append(bstat["wr"])
                            ks_last = pb.op(
                                "tensor",
                                lambda t, sc4=sc4, dc=dc, f=first_ks:
                                t.matmul(bstat["t"][:, dc:dc + 1],
                                         ktok[sc4][:, dc * 128:(dc + 1) * 128],
                                         cst.ones_col_b[:], start=f, stop=False,
                                         skip_group_check=True),
                                deps=dd, inc=(dc == NDC - 1))
                            first_ks = False
                    mkv = None
                    for eh in range(2):
                        for dc in range(NDC):
                            bc = pb.bank()
                            mms = [(ktok[sc4][:, dc * 128:(dc + 1) * 128],
                                    vtok[sc4][:, eh * TT:(eh + 1) * TT],
                                    [kt_ev[sc4], vt_ev[sc4]])
                                   for sc4 in range(2)]
                            mkv = mm_seq(pb, bc, bc["t"][:, 0:TT], mms)
                            if sg == 0:
                                kv_ev[dc][eh] = read_bank(
                                    pb, "vector", bc,
                                    lambda v, dc=dc, eh=eh, bc=bc:
                                    v.tensor_copy(
                                        kvf[dc][:, eh * TT:(eh + 1) * TT],
                                        bc["t"][:, 0:TT]),
                                    deps=[kv_ev[dc][eh]])
                            else:
                                kv_ev[dc][eh] = read_bank(
                                    pb, "vector", bc,
                                    lambda v, dc=dc, eh=eh, bc=bc:
                                    v.tensor_tensor(
                                        kvf[dc][:, eh * TT:(eh + 1) * TT],
                                        kvf[dc][:, eh * TT:(eh + 1) * TT],
                                        bc["t"][:, 0:TT], ALU.add),
                                    deps=[kv_ev[dc][eh]])
                    for sc4 in range(2):
                        ktok_rd[sc4] = mkv
                        vtok_rd[sc4] = mkv
                bstat["wr"] = ks_last
                bstat["rd"] = []
                ksc_ev = read_bank(pb, "vector", bstat,
                                   lambda v: v.tensor_copy(ksum_c[:],
                                                           bstat["t"][:, 0:8]))
                for dc in range(NDC):
                    cev = pb.op("vector",
                                lambda v, dc=dc: v.tensor_copy(kv[dc][:],
                                                               kvf[dc][:]),
                                deps=[kv_ev[dc][0], kv_ev[dc][1]], inc=True)
                    kv_ev[dc][0] = cev
                    kv_ev[dc][1] = cev

        with pb.scope():
            wqh = [pb.sbuf(f"wqh{i}", [128, NDC * TT], BF16) for i in range(2)]
            qt_ = [pb.sbuf(f"qt{dc}", [128, TT], BF16) for dc in range(NDC)]
            den_r = pb.sbuf("den_r", [1, TT], F32)
            rcb = pb.sbuf("rcb", [128, TT], F32)
            lout = [pb.sbuf(f"lout{i}", [128, TT], BF16) for i in range(2)]
            with pb.phase():
                for half in range(2):
                    pb.dma("sync", f"ld{1 + half}",
                           wqh[half][:].rearrange("p (c f) -> p c f", f=TT),
                           wql_d[:, half * TT:(half + 1) * TT]
                           .rearrange("(c p) f -> p c f", p=128),
                           deps=wql_ev)
                wq_ev = [pb.chan_total("ld1"), pb.chan_total("ld2")]
                el_last = {}
                q_rd = [None] * NDC
                den_last = None
                rcb_last = None
                lin_st = [None, None]
                for tt in range(NTT):
                    q_ev = [None] * NDC
                    for do in range(NDC):
                        bq = pb.bank()
                        mms = [(wqh[do // 4][:, dc * TT + (do % 4) * 128:
                                            dc * TT + (do % 4) * 128 + 128],
                                h[dc][:, tt * TT:(tt + 1) * TT],
                                [wq_ev[do // 4]]) for dc in range(NDC)]
                        mq = mm_seq(pb, bq, bq["t"][:, 0:TT], mms)
                        e1 = read_bank(pb, "vector", bq,
                                       lambda v, bq=bq, do=do:
                                       v.tensor_scalar(scr[0][:], bq["t"][:, 0:TT],
                                                       bias_c[:, do:do + 1], 0.0,
                                                       ALU.add, ALU.min),
                                       deps=[el_last.get("s0")])
                        e2 = pb.op("scalar",
                                   lambda s: s.activation(out=scr[1][:],
                                                          in_=scr[0][:],
                                                          func=AF.Exp),
                                   deps=[e1, el_last.get("s1")], inc=True)
                        e3 = read_bank(pb, "vector", bq,
                                       lambda v, bq=bq, do=do:
                                       v.tensor_scalar(scr[2][:], bq["t"][:, 0:TT],
                                                       bias_c[:, do:do + 1], 0.0,
                                                       ALU.add, ALU.max),
                                       deps=[el_last.get("s2")])
                        e4 = pb.op("vector",
                                   lambda v, do=do:
                                   v.tensor_tensor(qt_[do][:], scr[1][:],
                                                   scr[2][:], ALU.add),
                                   deps=[e2, e3, q_rd[do]], inc=True)
                        el_last["s0"] = e4
                        el_last["s1"] = e4
                        el_last["s2"] = e4
                        q_ev[do] = e4
                    bden = pb.bank()
                    mms = [(ksum_c[:, dc:dc + 1], qt_[dc][:],
                            [q_ev[dc], ksc_ev]) for dc in range(NDC)]
                    mden = mm_seq(pb, bden, bden["t"][0:1, 0:TT], mms)
                    d1 = read_bank(pb, "vector", bden,
                                   lambda v, bden=bden:
                                   v.tensor_scalar(den_r[0:1, :],
                                                   bden["t"][0:1, 0:TT],
                                                   1e-6, None, ALU.add),
                                   deps=[den_last])
                    d2 = pb.op("vector",
                               lambda v: v.reciprocal(den_r[0:1, :],
                                                      den_r[0:1, :]),
                               deps=[d1], inc=True)
                    den_last = d2
                    brcb = pb.bank()
                    mm_seq(pb, brcb, brcb["t"][:, 0:TT],
                           [(cst.ones_row[0:1, :], den_r[0:1, :], [d2])])
                    rc_ev = read_bank(pb, "vector", brcb,
                                      lambda v, brcb=brcb:
                                      v.tensor_copy(rcb[:], brcb["t"][:, 0:TT]),
                                      deps=[rcb_last])
                    rcb_last = rc_ev
                    for ec in range(NDC):
                        bn = pb.bank()
                        mms = [(kv[dc][:, ec * 128:(ec + 1) * 128], qt_[dc][:],
                                [q_ev[dc], kv_ev[dc][ec // 4]])
                               for dc in range(NDC)]
                        mn = mm_seq(pb, bn, bn["t"][:, 0:TT], mms)
                        q_rd[ec] = mn
                        lo = lout[ec % 2]
                        le = read_bank(pb, "vector", bn,
                                       lambda v, bn=bn, lo=lo:
                                       v.tensor_tensor(lo[:], bn["t"][:, 0:TT],
                                                       rcb[:], ALU.mult),
                                       deps=[rc_ev, lin_st[ec % 2]])
                        lin_st[ec % 2] = pb.dma(
                            "sync", ("stq", "st2")[ec % 2],
                            lin_hbm[ec * 128:(ec + 1) * 128,
                                    tt * TT:(tt + 1) * TT],
                            lo[:], deps=[le])

    lin_done = [pb.chan_total("stq"), pb.chan_total("st2")]
    with pb.scope():
        expbt = pb.sbuf("expbt", [128, 16 * 384], BF16)
        expbt_f = pb.sbuf("expbt_f", [128, 384], F32)
        qloc = [pb.sbuf(f"qloc{dc}", [128, TT], BF16) for dc in range(NDC)]
        kspan = [pb.sbuf(f"kspan{dc}", [128, 768], BF16) for dc in range(NDC)]
        v66 = [pb.sbuf(f"v66_{ci}", [128, 16 * 66], BF16) for ci in range(6)]
        exps = pb.sbuf("exps", [128, 384], F32)
        expT = pb.sbuf("expT", [128, 384], BF16)
        srow = pb.sbuf("srow", [1, 128], F32)
        rcw = pb.sbuf("rcw", [128, 128], F32)
        win_t = [pb.sbuf(f"win_t{dc}", [128, TT], BF16) for dc in range(NDC)]
        wcol = [pb.sbuf(f"wcol{i}", [128, D], BF16) for i in range(2)]
        wvh2 = pb.sbuf("wvh2", [128, NDC * TT], BF16)
        with pb.phase():
            for _eng in PB.CENG:
                pb.waitev(_eng, *lin_done)
            bts = None
            for hh in range(16):
                ld = pb.dma("sync", "ld1", expbt_f[:],
                            bt_d[:, hh * 384:(hh + 1) * 384], deps=[bts])
                bts = pb.op("scalar",
                            lambda s, hh=hh:
                            s.activation(out=expbt[:, hh * 384:(hh + 1) * 384],
                                         in_=expbt_f[:], func=AF.Exp),
                            deps=[pb.chan_total("ld1")], inc=True)
            msz = []
            for ci in range(6):
                msz.append(pb.op("vector",
                                 lambda v, ci=ci:
                                 v.memset(v66[ci][:].bitcast(F32), 0.0),
                                 inc=True))
            ones_ev = None
            for ci in range(6):
                for hh in range(16):
                    ones_ev = pb.op(
                        "vector",
                        lambda v, ci=ci, hh=hh:
                        v.memset(v66[ci][:, hh * 66 + 64:hh * 66 + 65], 1.0),
                        deps=[msz[ci]], inc=True)
        with pb.phase():
            q_rd = [None] * NDC
            k_rd = [None] * NDC
            v_rd = [None] * 6
            w_rd = [None, None]
            wv_rd = None
            exps_rd = None
            expT_rd = None
            sr_rd = None
            rc_rd = None
            win_rd = {}
            win_st = [None, None]
            for tile in range(NTT):
                t0 = tile * TT
                for do in range(NDC):
                    wev = pb.dma("sync", f"ld{1 + do % 2}",
                                 wcol[do % 2][:].rearrange("p (c f) -> p c f",
                                                           f=128),
                                 wqo_d[:, do * 128:(do + 1) * 128]
                                 .rearrange("(c p) f -> p c f", p=128),
                                 deps=[w_rd[do % 2]] + wqo_ev)
                    bq = pb.bank()
                    mms = [(wcol[do % 2][:, dc * 128:(dc + 1) * 128],
                            h[dc][:, t0:t0 + TT], [wev]) for dc in range(NDC)]
                    mq = mm_seq(pb, bq, bq["t"][:, 0:TT], mms)
                    q_rd[do] = read_bank(
                        pb, "vector", bq,
                        lambda v, do=do, bq=bq:
                        v.tensor_scalar(qloc[do][:], bq["t"][:, 0:TT],
                                        bias_c[:, 8 + do:9 + do], 0.125,
                                        ALU.add, ALU.mult),
                        deps=[q_rd[do]])
                    wev2 = pb.dma("sync", f"ld{1 + do % 2}",
                                  wcol[do % 2][:].rearrange("p (c f) -> p c f",
                                                            f=128),
                                  wko_d[:, do * 128:(do + 1) * 128]
                                  .rearrange("(c p) f -> p c f", p=128),
                                  deps=[mq] + wko_ev)
                    if tile == 0:
                        segs = [(128, 640), (640, 768)]
                    elif tile == NTT - 1:
                        segs = [(0, 512), (512, 640)]
                    else:
                        segs = [(0, 512), (512, 768)]
                    mse = []
                    if tile == 0:
                        mse.append(pb.op("vector",
                                         lambda v, do=do:
                                         v.memset(kspan[do][:, 0:128]
                                                  .bitcast(F32), 0.0),
                                         deps=[k_rd[do]], inc=True))
                    if tile == NTT - 1:
                        mse.append(pb.op("vector",
                                         lambda v, do=do:
                                         v.memset(kspan[do][:, 640:768]
                                                  .bitcast(F32), 0.0),
                                         deps=[k_rd[do]], inc=True))
                    mk = None
                    for (a, b) in segs:
                        bkk = pb.bank()
                        mms = [(wcol[do % 2][:, dc * 128:(dc + 1) * 128],
                                h[dc][:, t0 - 128 + a:t0 - 128 + b], [wev2])
                               for dc in range(NDC)]
                        mk = mm_seq(pb, bkk, bkk["t"][:, 0:b - a], mms)
                        k_rd[do] = read_bank(
                            pb, "vector", bkk,
                            lambda v, do=do, a=a, b=b, bkk=bkk:
                            v.tensor_scalar(kspan[do][:, a:b],
                                            bkk["t"][:, 0:b - a],
                                            bias_c[:, 16 + do:17 + do],
                                            None, ALU.add),
                            deps=[k_rd[do]] + mse)
                    w_rd[do % 2] = mk
                for half in range(2):
                    wvev = pb.dma("sync", "ld3",
                                  wvh2[:].rearrange("p (c f) -> p c f", f=TT),
                                  wvo_d[:, half * TT:(half + 1) * TT]
                                  .rearrange("(c p) f -> p c f", p=128),
                                  deps=[wv_rd] + wvo_ev)
                    mv = None
                    for ci in range(6):
                        g = tile * 4 - 1 + ci
                        if g < 0 or g >= NTC:
                            continue
                        bv = pb.bank()
                        mms = [(cst.ones_row_b[0:1, :],
                                brow[0:1, 2 * D + half * TT:
                                     2 * D + (half + 1) * TT], [])]
                        mms += [(h[dc][:, g * 128:(g + 1) * 128],
                                 wvh2[:, dc * TT:(dc + 1) * TT], [wvev])
                                for dc in range(NDC)]
                        mv = mm_seq(pb, bv, bv["t"][:, 0:TT], mms)
                        for hh in range(half * 8, half * 8 + 8):
                            v_rd[ci] = read_bank(
                                pb, "vector", bv,
                                lambda v, ci=ci, hh=hh, half=half, bv=bv:
                                v.tensor_copy(
                                    v66[ci][:, hh * 66:hh * 66 + 64],
                                    bv["t"][:, (hh - half * 8) * 64:
                                            (hh - half * 8) * 64 + 64]),
                                deps=[v_rd[ci], ones_ev])
                    wv_rd = mv
                if tile == NTT - 1:
                    mzero = pb.op("vector",
                                  lambda v: v.memset(v66[5][:].bitcast(F32), 0.0),
                                  deps=[v_rd[5]], inc=True)
                    v_rd[5] = mzero
                for qc in range(4):
                    for hh in range(16):
                        hc, hb = hh // 2, (hh % 2) * 64
                        bs = pb.bank()
                        first = True
                        msc = None
                        for c in range(3):
                            dd = [q_rd[hc], k_rd[hc]]
                            if first:
                                dd += list(bs["rd"]) + \
                                    ([bs["wr"]] if bs["wr"] else [])
                            msc = pb.op(
                                "tensor",
                                lambda t, hc=hc, hb=hb, qc=qc, c=c, bs=bs,
                                first=first:
                                t.matmul(bs["t"][:, c * 128:(c + 1) * 128],
                                         kspan[hc][hb:hb + 64,
                                                   qc * 128 + c * 128:
                                                   qc * 128 + (c + 1) * 128],
                                         qloc[hc][hb:hb + 64,
                                                  qc * 128:(qc + 1) * 128],
                                         start=first, stop=(c == 2),
                                         skip_group_check=True),
                                deps=dd, inc=(c == 2))
                            first = False
                        bs["wr"] = msc
                        bs["rd"] = []
                        ee = read_bank(pb, "scalar", bs,
                                       lambda s, bs=bs:
                                       s.activation(out=exps[:],
                                                    in_=bs["t"][:, 0:384],
                                                    func=AF.Exp),
                                       deps=[exps_rd])
                        em = pb.op("vector",
                                   lambda v, hh=hh:
                                   v.tensor_tensor(expT[:], exps[:],
                                                   expbt[:, hh * 384:
                                                         (hh + 1) * 384],
                                                   ALU.mult),
                                   deps=[ee, expT_rd], inc=True)
                        exps_rd = em
                        bw = pb.bank()
                        mw = None
                        first = True
                        for c in range(3):
                            ci = qc + c
                            dd = [em, v_rd[ci]]
                            if first:
                                dd += list(bw["rd"]) + \
                                    ([bw["wr"]] if bw["wr"] else [])
                            mw = pb.op(
                                "tensor",
                                lambda t, ci=ci, hh=hh, c=c, bw=bw, first=first:
                                t.matmul(bw["t"][0:66, 0:128],
                                         v66[ci][:, hh * 66:hh * 66 + 66],
                                         expT[:, c * 128:(c + 1) * 128],
                                         start=first, stop=(c == 2),
                                         skip_group_check=True),
                                deps=dd, inc=(c == 2))
                            first = False
                        expT_rd = mw
                        bw["wr"] = mw
                        bw["rd"] = []
                        s1 = read_bank(pb, "vector", bw,
                                       lambda v, bw=bw:
                                       v.tensor_copy(srow[0:1, :],
                                                     bw["t"][64:65, 0:128]),
                                       deps=[sr_rd])
                        s2 = pb.op("vector",
                                   lambda v: v.reciprocal(srow[0:1, :],
                                                          srow[0:1, :]),
                                   deps=[s1], inc=True)
                        sr_rd = s2
                        brr = pb.bank()
                        mm_seq(pb, brr, brr["t"][0:64, 0:128],
                               [(cst.ones_row[0:1, 0:64], srow[0:1, :], [s2])])
                        r3 = read_bank(pb, "vector", brr,
                                       lambda v, brr=brr:
                                       v.tensor_copy(rcw[0:64, :],
                                                     brr["t"][0:64, 0:128]),
                                       deps=[rc_rd])
                        rc_rd = r3
                        wkey = (hc, hb)
                        wv4 = read_bank(
                            pb, "vector", bw,
                            lambda v, hc=hc, hb=hb, qc=qc, bw=bw:
                            v.tensor_tensor(
                                win_t[hc][hb:hb + 64,
                                          qc * 128:(qc + 1) * 128],
                                bw["t"][0:64, 0:128], rcw[0:64, :],
                                ALU.mult),
                            deps=[r3, win_rd.get(wkey)])
                        win_rd[wkey] = wv4
                wst_evs = list(win_rd.values())
                for dc in range(NDC):
                    win_st[dc % 2] = pb.dma(
                        "sync", ("stq", "st2")[dc % 2],
                        win_hbm[dc * 128:(dc + 1) * 128, t0:t0 + TT],
                        win_t[dc][:], deps=wst_evs + [win_st[dc % 2]])
                for kkey in list(win_rd):
                    win_rd[kkey] = win_st[kkey[0] % 2]

    win_done = [pb.chan_total("stq"), pb.chan_total("st2")]
    mzrow = pb.sbuf("mzrow", [1, L], F32)
    rzrow = pb.sbuf("rzrow", [1, L], F32)
    with pb.scope():
        lin_s = [[pb.sbuf(f"lins{i}_{c}", [128, TT], BF16) for c in range(NDC)]
                 for i in range(2)]
        win_s = [[pb.sbuf(f"wins{i}_{c}", [128, TT], BF16) for c in range(NDC)]
                 for i in range(2)]
        xs_s = [[pb.sbuf(f"xss{i}_{c}", [128, TT], F32) for c in range(NDC)]
                for i in range(2)]
        g1 = [pb.sbuf(f"g1_{dc}", [128, TT], BF16) for dc in range(NDC)]
        outc = [pb.sbuf(f"outc{dc}", [128, TT], BF16) for dc in range(NDC)]
        wg1b = [pb.sbuf(f"wg1b{i}", [128, 24 * 128], BF16) for i in range(2)]
        wcol2 = [pb.sbuf(f"wcol2_{i}", [128, D], BF16) for i in range(2)]
        gate_t = pb.sbuf("gate_t", [128, TT], F32)
        z_t = [pb.sbuf(f"z_t{i}", [128, TT], F32R) for i in range(2)]
        sqz = pb.sbuf("sqz", [128, TT], F32R)
        zvrow = pb.sbuf("zvrow", [1, TT], F32)
        with pb.phase():
            for _eng in PB.CENG:
                pb.waitev(_eng, *win_done)
            luse = [None, None]
            w_rd = [None, None]
            w2_rd = [None, None]
            g1_rd = [None] * NDC
            gate_rd = None
            outc_rd = [None] * NDC
            sq_rd = None
            z_st = [None, None]
            prev_small = []
            cmb_last = None
            for tt in range(NTT):
                ch = f"ld{1 + tt % 2}"
                for c in range(NDC):
                    pb.dma("sync", ch, lin_s[tt % 2][c][:],
                           lin_hbm[c * 128:(c + 1) * 128, tt * TT:(tt + 1) * TT],
                           deps=[luse[tt % 2]] + lin_done)
                    pb.dma("sync", ch, win_s[tt % 2][c][:],
                           win_hbm[c * 128:(c + 1) * 128, tt * TT:(tt + 1) * TT],
                           deps=[luse[tt % 2]] + win_done)
                    pb.dma("sync", ch, xs_s[tt % 2][c][:],
                           x_f[c * 128:(c + 1) * 128, tt * TT:(tt + 1) * TT],
                           deps=[luse[tt % 2]])
                ltot = pb.chan_total(ch)
                for do in range(NDC):
                    wev = pb.dma("sync", f"ld{3 + do % 2}",
                                 wg1b[do % 2][:].rearrange("p (c f) -> p c f",
                                                           f=128),
                                 wg1_d[:, do * 128:(do + 1) * 128]
                                 .rearrange("(c p) f -> p c f", p=128),
                                 deps=[w_rd[do % 2]] + wg1_ev)
                    bg = pb.bank()
                    mms = []
                    for dc in range(NDC):
                        mms.append((wg1b[do % 2][:, dc * 128:(dc + 1) * 128],
                                    lin_s[tt % 2][dc][:], [wev, ltot]))
                    for dc in range(NDC):
                        mms.append((wg1b[do % 2][:, (8 + dc) * 128:
                                                (9 + dc) * 128],
                                    win_s[tt % 2][dc][:], [wev, ltot]))
                    for dc in range(NDC):
                        mms.append((wg1b[do % 2][:, (16 + dc) * 128:
                                                (17 + dc) * 128],
                                    h[dc][:, tt * TT:(tt + 1) * TT], [wev]))
                    mg = mm_seq(pb, bg, bg["t"][:, 0:TT], mms)
                    w_rd[do % 2] = mg
                    g1_rd[do] = read_bank(
                        pb, "vector", bg,
                        lambda v, do=do, bg=bg:
                        v.tensor_scalar(g1[do][:], bg["t"][:, 0:TT],
                                        bias_c[:, 24 + do:25 + do], 0.0,
                                        ALU.add, ALU.max),
                        deps=[g1_rd[do]])
                bstat = pb.bank_fixed("stat")
                bqstat = pb.bank_fixed("c3")
                first_stat = True
                last_me = None
                for do in range(NDC):
                    wev = pb.dma("sync", ("ld5", "ld0")[do % 2],
                                 wcol2[do % 2][:].rearrange("p (c f) -> p c f",
                                                            f=128),
                                 wg2_d[:, do * 128:(do + 1) * 128]
                                 .rearrange("(c p) f -> p c f", p=128),
                                 deps=[w2_rd[do % 2]] + wg2_ev)
                    bgt = pb.bank()
                    mms = [(wcol2[do % 2][:, dc * 128:(dc + 1) * 128],
                            g1[dc][:], [wev, g1_rd[dc]]) for dc in range(NDC)]
                    mgt = mm_seq(pb, bgt, bgt["t"][:, 0:TT], mms)
                    w2_rd[do % 2] = mgt
                    eg = read_bank(pb, "scalar", bgt,
                                   lambda s, do=do, bgt=bgt:
                                   s.activation(out=gate_t[:],
                                                in_=bgt["t"][:, 0:TT],
                                                func=AF.Sigmoid,
                                                bias=bias_c[:, 32 + do:33 + do],
                                                scale=1.0),
                                   deps=[gate_rd])
                    c1 = pb.op("vector",
                               lambda v, do=do, tt=tt:
                               v.tensor_tensor(scr[0][:], lin_s[tt % 2][do][:],
                                               win_s[tt % 2][do][:],
                                               ALU.subtract),
                               deps=[ltot, cmb_last], inc=True)
                    c2 = pb.op("vector",
                               lambda v: v.tensor_tensor(scr[1][:], scr[0][:],
                                                         gate_t[:], ALU.mult),
                               deps=[c1, eg], inc=True)
                    gate_rd = c2
                    c3 = pb.op("vector",
                               lambda v, do=do, tt=tt:
                               v.tensor_tensor(outc[do][:], scr[1][:],
                                               win_s[tt % 2][do][:], ALU.add),
                               deps=[c2, outc_rd[do]], inc=True)
                    outc_rd[do] = c3
                    cmb_last = c3
                for do in range(NDC):
                    wev = pb.dma("sync", ("ld5", "ld0")[do % 2],
                                 wcol2[do % 2][:].rearrange("p (c f) -> p c f",
                                                            f=128),
                                 wo_d[:, do * 128:(do + 1) * 128]
                                 .rearrange("(c p) f -> p c f", p=128),
                                 deps=[w2_rd[do % 2], w_rd[do % 2]] + wol_ev)
                    bz = pb.bank()
                    mms = [(wcol2[do % 2][:, dc * 128:(dc + 1) * 128],
                            outc[dc][:], [wev, outc_rd[dc]])
                           for dc in range(NDC)]
                    mz = mm_seq(pb, bz, bz["t"][:, 0:TT], mms)
                    w2_rd[do % 2] = mz
                    w_rd[do % 2] = mz
                    zt = z_t[do % 2]
                    bmu, bru = bcast_rows(pb, cst, m0row, r0row, tt)
                    u1 = read_bank(pb, "vector", bmu,
                                   lambda v, do=do, tt=tt, bmu=bmu:
                                   v.tensor_tensor(scr[2][:],
                                                   xs_s[tt % 2][do][:],
                                                   bmu["t"][:, 0:TT],
                                                   ALU.subtract),
                                   deps=[ltot, z_st[do % 2]])
                    u2 = read_bank(pb, "vector", bru,
                                   lambda v, bru=bru:
                                   v.tensor_tensor(scr[3][:], scr[2][:],
                                                   bru["t"][:, 0:TT], ALU.mult),
                                   deps=[u1])
                    z1 = read_bank(pb, "vector", bz,
                                   lambda v, do=do, zt=zt, bz=bz:
                                   v.scalar_tensor_tensor(
                                       zt[:], bz["t"][:, 0:TT],
                                       bias_c[:, 40 + do:41 + do],
                                       scr[3][:], ALU.add, ALU.add),
                                   deps=[u2, z_st[do % 2]])
                    zs = pb.dma("sync", ("stq", "st2")[do % 2],
                                z_hbm[do * 128:(do + 1) * 128,
                                      tt * TT:(tt + 1) * TT],
                                zt[:], deps=[z1])
                    z_st[do % 2] = zs
                    sev = pb.op("scalar",
                                lambda s, zt=zt: s.activation(out=sqz[:],
                                                              in_=zt[:],
                                                              func=AF.Square),
                                deps=[z1, sq_rd], inc=True)
                    dd = [z1]
                    if first_stat:
                        dd += list(bstat["rd"])
                        if bstat["wr"] is not None:
                            dd.append(bstat["wr"])
                    pb.op("tensor",
                          lambda t, zt=zt, f=first_stat, l=(do == NDC - 1):
                          t.matmul(bstat["t"][0:1, 0:TT], cst.ones_col_r[:],
                                   zt[:], start=f, stop=l,
                                   skip_group_check=True),
                          deps=dd)
                    dd2 = [sev]
                    if first_stat:
                        dd2 += list(bqstat["rd"])
                        if bqstat["wr"] is not None:
                            dd2.append(bqstat["wr"])
                    first_stat = False
                    last_me = pb.op(
                        "tensor",
                        lambda t, f2=(do == 0), l2=(do == NDC - 1):
                        t.matmul(bqstat["t"][0:1, 0:TT],
                                 cst.ones_col_r[:], sqz[:],
                                 start=f2, stop=l2,
                                 skip_group_check=True),
                        deps=dd2, inc=True)
                    sq_rd = last_me
                bstat["wr"] = last_me
                bqstat["wr"] = last_me
                bstat["rd"] = []
                bqstat["rd"] = []
                c0 = tt * TT
                e1 = read_bank(pb, "vector", bstat,
                               lambda v, c0=c0:
                               v.tensor_scalar(mzrow[0:1, c0:c0 + TT],
                                               bstat["t"][0:1, 0:TT],
                                               1.0 / D, None, ALU.mult),
                               deps=prev_small)
                e2 = read_bank(pb, "vector", bqstat,
                               lambda v, c0=c0:
                               v.tensor_scalar(zvrow[0:1, :],
                                               bqstat["t"][0:1, 0:TT],
                                               1.0 / D, None, ALU.mult),
                               deps=prev_small + [e1])
                e3b = pb.op("scalar",
                            lambda s, c0=c0:
                            s.activation(out=rzrow[0:1, c0:c0 + TT],
                                         in_=mzrow[0:1, c0:c0 + TT],
                                         func=AF.Square),
                            deps=[e1], inc=True)
                e4 = pb.op("vector",
                           lambda v, c0=c0:
                           v.tensor_tensor(zvrow[0:1, :], zvrow[0:1, :],
                                           rzrow[0:1, c0:c0 + TT],
                                           ALU.subtract),
                           deps=[e2, e3b], inc=True)
                e5 = pb.op("scalar",
                           lambda s, c0=c0:
                           s.activation(out=rzrow[0:1, c0:c0 + TT],
                                        in_=zvrow[0:1, :], func=AF.Sqrt,
                                        bias=cst.cpool[0:1, C_EPS:C_EPS + 1],
                                        scale=1.0),
                           deps=[e4], inc=True)
                e6 = pb.op("vector",
                           lambda v, c0=c0:
                           v.reciprocal(rzrow[0:1, c0:c0 + TT],
                                        rzrow[0:1, c0:c0 + TT]),
                           deps=[e5], inc=True)
                prev_small = [e6]
                luse[tt % 2] = z1

    z_done = [pb.chan_total("stq"), pb.chan_total("st2")]
    with pb.scope():
        zs2 = [[pb.sbuf(f"zs2{i}_{c}", [128, TT], F32R) for c in range(NDC)]
               for i in range(2)]
        xs2 = [[pb.sbuf(f"xs2{i}_{c}", [128, TT], F32) for c in range(NDC)]
               for i in range(2)]
        out_t = [pb.sbuf(f"outt{i}", [128, TT], F32) for i in range(2)]
        with pb.phase():
            for _eng in PB.CENG:
                pb.waitev(_eng, *z_done)
            xuse = [None, None]
            ost = [None, None]
            last = None
            for tt in range(NTT):
                ch = f"ld{1 + tt % 2}"
                for dc in range(NDC):
                    pb.dma("sync", ch, zs2[tt % 2][dc][:],
                           z_hbm[dc * 128:(dc + 1) * 128, tt * TT:(tt + 1) * TT],
                           deps=[xuse[tt % 2]] + z_done)
                    pb.dma("sync", ch, xs2[tt % 2][dc][:],
                           x_f[dc * 128:(dc + 1) * 128, tt * TT:(tt + 1) * TT],
                           deps=[xuse[tt % 2]])
                tot = pb.chan_total(ch)
                bm, br = bcast_rows(pb, cst, mzrow, rzrow, tt)
                for dc in range(NDC):
                    ot = out_t[dc % 2]
                    e1 = read_bank(pb, "vector", bm,
                                   lambda v, dc=dc, tt=tt, bm=bm:
                                   v.tensor_tensor(scr[0][:],
                                                   zs2[tt % 2][dc][:],
                                                   bm["t"][:, 0:TT],
                                                   ALU.subtract),
                                   deps=[last, tot])
                    e2 = read_bank(pb, "vector", br,
                                   lambda v, br=br:
                                   v.tensor_tensor(scr[1][:], scr[0][:],
                                                   br["t"][:, 0:TT], ALU.mult),
                                   deps=[e1])
                    e3 = pb.op("vector",
                               lambda v, dc=dc, tt=tt, ot=ot:
                               v.tensor_tensor(ot[:], scr[1][:],
                                               xs2[tt % 2][dc][:], ALU.add),
                               deps=[e2, ost[dc % 2]], inc=True)
                    ost[dc % 2] = pb.dma(
                        "sync", ("stq", "st2")[dc % 2],
                        xo_b[dc * 128:(dc + 1) * 128, tt * TT:(tt + 1) * TT],
                        ot[:], deps=[e3])
                    last = e3
                    xuse[tt % 2] = e3
            pb.waitev("sync", pb.chan_total("stq"), pb.chan_total("st2"))

    return


def emit_l0attn(pb, cst, x_d, xo_d, weights):
    x_f = x_d.bitcast(F32)
    xo_b = xo_d.bitcast(F32) if xo_d.dtype == F32R else xo_d
    rv_hbm = pb.dram_tmp("rv_hbm", [D, L], BF16)
    mv_hbm = pb.dram_tmp("mv_hbm", [D, L], BF16)
    rq_hbm = pb.dram_tmp("rq_hbm", [D, L], BF16)
    mq_hbm = pb.dram_tmp("mq_hbm", [D, L], BF16)
    pre_hbm = pb.dram_tmp("pre_hbm", [D, L], F32R)
    rvp_hbm = pb.dram_tmp("rvp_hbm", [L, D], BF16)
    mvp_hbm = pb.dram_tmp("mvp_hbm", [L, D], BF16)
    cph_hbm = pb.dram_tmp("cph_hbm", [D, L], F32R)
    y_hbm = pb.dram_tmp("y_hbm", [D, L], F32R)

    need_pre(pb, weights, ["l0a_rqT", "l0a_mqT", "l0a_combT",
                           "l0a_rvT", "l0a_mvT"])
    wvalT_d, wval_ev = pb.pre["l0a_rvT"]
    wrqT_d, wrq_ev = pb.pre["l0a_rqT"]
    wmvT_d, wmv_ev = pb.pre["l0a_mvT"]
    wmqT_d, wmq_ev = pb.pre["l0a_mqT"]
    combT_d, comb_ev = pb.pre["l0a_combT"]
    wsmall_d = pb.dram_in("l0a_wsmall", weights["l0a_wsmall"], F32R)
    bsmall_d = pb.dram_in("l0a_bsmall", weights["l0a_bsmall"], F32R)
    bias_d = pb.dram_in("l0a_bias_c", weights["l0a_bias_c"], F32)
    dcol_d = pb.dram_in("l0a_dcol", weights["l0a_dcol"], F32)
    m4_d = pb.dram_in("l0a_m4", weights["l0a_m4"], F32)

    SC = 1.0 / 32.0

    h = [pb.sbuf(f"h{dc}", [128, L], F32R) for dc in range(NDC)]
    m0row = pb.sbuf("m0row", [1, L], F32)
    r0row = pb.sbuf("r0row", [1, L], F32)
    scr = [pb.sbuf(f"scr{i}", [128, TT], F32) for i in range(4)]
    bias_c = pb.sbuf("bias_c", [128, 56], F32)
    dcol = pb.sbuf("dcol", [128, 32], F32)
    m4 = pb.sbuf("m4", [128, 4 * TT], F32)
    w_r_all = pb.sbuf("w_r_all", [128, 16 * 8], F32)
    wm_all = pb.sbuf("wm_all", [128, 16 * 16], F32)

    with pb.scope():
        xs4 = [[pb.sbuf(f"xs{tt}_{dc}", [128, TT], F32R) for dc in range(NDC)]
               for tt in range(NTT)]
        wsmall = None
        bsmall = None
        with pb.phase():
            pb.dma("sync", "ld0", bias_c[:], bias_d)
            pb.dma("sync", "ld0", dcol[:], dcol_d)
            pb.dma("sync", "ld0", m4[:], m4_d)
            for _eng in PB.CENG:
                pb.waitev(_eng, pb.chan_total("ld0"))
            tot = []
            for tt in range(NTT):
                ch = f"lx{tt}"
                for dc in range(NDC):
                    pb.dma("sync", ch, xs4[tt][dc][:],
                           x_d[dc * 128:(dc + 1) * 128, tt * TT:(tt + 1) * TT])
                tot.append(pb.chan_total(ch))
            ln_stats(pb, cst, lambda dc, tt: (xs4[tt][dc], 0, tot[tt]),
                     m0row, r0row, "sx")
        with pb.phase():
            last = None
            for tt in range(NTT):
                bm, br = bcast_rows(pb, cst, m0row, r0row, tt)
                for dc in range(NDC):
                    e1 = read_bank(pb, "vector", bm,
                                   lambda v, dc=dc, tt=tt, bm=bm:
                                   v.tensor_tensor(scr[0][:], xs4[tt][dc][:],
                                                   bm["t"][:, 0:TT], ALU.subtract),
                                   deps=[last])
                    e2 = read_bank(pb, "vector", br,
                                   lambda v, dc=dc, tt=tt, br=br:
                                   v.tensor_tensor(h[dc][:, tt * TT:(tt + 1) * TT],
                                                   scr[0][:],
                                                   br["t"][:, 0:TT], ALU.mult),
                                   deps=[e1])
                    last = e2

    sc24 = pb.scope()
    sc24.__enter__()
    wrT = pb.sbuf("wrT", [8, L], BF16)
    wmT = pb.sbuf("wmT", [16, L], BF16)
    prT = pb.sbuf("prT", [8, L], BF16)
    pmT = pb.sbuf("pmT", [16, L], BF16)
    with pb.scope():
        wcolb = [pb.sbuf(f"wcolb{i}", [128, D], BF16) for i in range(2)]
        ptile = [pb.sbuf(f"pt{i}", [128, TT], F32R) for i in range(2)]
        ptb = [pb.sbuf(f"ptb{i}", [128, TT], BF16) for i in range(2)]
        hb = [pb.sbuf(f"hb{dc}", [128, L], BF16) for dc in range(NDC)]
        rvt = [pb.sbuf(f"rvt{dc}", [128, TT], BF16) for dc in range(NDC)]
        whalf = [pb.sbuf(f"whalf{i}", [128, NDC * TT], BF16) for i in range(2)]
        wsmall = pb.sbuf("wsmall", [128, 8 * 26], BF16)
        bsmall = pb.sbuf("bsmall", [1, 26], BF16)
        smT = pb.sbuf("smT", [128, 16], F32)
        sm = pb.sbuf("sm", [128, 26], F32)
        rgm = pb.sbuf("rgm", [128, 2], F32)
        smx = pb.sbuf("smx", [128, 1], F32)
        sme = pb.sbuf("sme", [128, 16], F32)
        sms = pb.sbuf("sms", [128, 1], F32)
        with pb.phase():
            pb.dma("gpsimd", "gp4", wsmall[:].rearrange("p (c e) -> p c e", e=26),
                   wsmall_d.rearrange("(c p) e -> p c e", p=128))
            pb.dma("gpsimd", "gp4", bsmall[:], bsmall_d)
            wsm_ev = pb.chan_total("gp4")
            hb_ev = [None] * NDC
            for dc in range(NDC):
                if dc % 2 == 0:
                    hb_ev[dc] = pb.op("vector",
                                      lambda v, dc=dc:
                                      v.tensor_copy(hb[dc][:], h[dc][:]),
                                      inc=True)
                else:
                    hb_ev[dc] = pb.op("scalar",
                                      lambda s, dc=dc:
                                      s.activation(out=hb[dc][:], in_=h[dc][:],
                                                   func=AF.Copy),
                                      inc=True)
            w_rd = [None, None]
            p_rd = [None, None]
            st_pr = [None, None]
            for (wd, dst, bcol, wrow0, wpre) in (
                    (wrqT_d, rq_hbm, 8, None, wrq_ev),
                    (wmqT_d, mq_hbm, 24, None, wmq_ev),
                    (combT_d, pre_hbm, 32, 0, comb_ev)):
                for tt in range(NTT):
                    for do in range(NDC):
                        src = wd if wrow0 is None else wd
                        r0 = 0 if wrow0 is None else wrow0
                        wev = pb.dma("sync", f"ld{1 + do % 2}",
                                     wcolb[do % 2][:]
                                     .rearrange("p (c f) -> p c f", f=128),
                                     src[r0:r0 + D, do * 128:(do + 1) * 128]
                                     .rearrange("(c p) f -> p c f", p=128),
                                     deps=[w_rd[do % 2]] + wpre)
                        bo = pb.bank()
                        mms = [(wcolb[do % 2][:, dc * 128:(dc + 1) * 128],
                                hb[dc][:, tt * TT:(tt + 1) * TT],
                                [wev, hb_ev[dc]])
                               for dc in range(NDC)]
                        mev = mm_seq(pb, bo, bo["t"][:, 0:TT], mms)
                        w_rd[do % 2] = mev
                        pt = (ptile if wrow0 is not None else ptb)[do % 2]
                        ev = read_bank(pb, "vector", bo,
                                       lambda v, pt=pt, do=do, bcol=bcol, bo=bo:
                                       v.tensor_scalar(pt[:], bo["t"][:, 0:TT],
                                                       bias_c[:, bcol + do:
                                                              bcol + do + 1],
                                                       None, ALU.add),
                                       deps=[p_rd[do % 2], st_pr[do % 2]])
                        p_rd[do % 2] = ev
                        st_pr[do % 2] = pb.dma(
                            "sync", ("stq", "st2")[do % 2],
                            dst[do * 128:(do + 1) * 128, tt * TT:(tt + 1) * TT],
                            pt[:], deps=[ev])
            rv_rd = [None] * NDC
            tokst = [None, None]
            tok_rd = [None, None]
            for (wd, dstfm, dsttok, bcol, crow0, wpre) in (
                    (wvalT_d, rv_hbm, rvp_hbm, 0, D, wval_ev),
                    (wmvT_d, mv_hbm, mvp_hbm, 16, 2 * D, wmv_ev)):
                for half in range(2):
                    pb.dma("sync", f"ld{3 + half}",
                           whalf[half][:].rearrange("p (c f) -> p c f", f=TT),
                           combT_d[crow0:crow0 + D, half * TT:(half + 1) * TT]
                           .rearrange("(c p) f -> p c f", p=128),
                           deps=[tok_rd[half]] + comb_ev)
                wh_ev = [pb.chan_total("ld3"), pb.chan_total("ld4")]
                for tt in range(NTT):
                    rv_ev = [None] * NDC
                    for do in range(NDC):
                        wev = pb.dma("sync", f"ld{1 + do % 2}",
                                     wcolb[do % 2][:]
                                     .rearrange("p (c f) -> p c f", f=128),
                                     wd[:, do * 128:(do + 1) * 128]
                                     .rearrange("(c p) f -> p c f", p=128),
                                     deps=[w_rd[do % 2]] + wpre)
                        bo = pb.bank()
                        mms = [(wcolb[do % 2][:, dc * 128:(dc + 1) * 128],
                                hb[dc][:, tt * TT:(tt + 1) * TT],
                                [wev, hb_ev[dc]])
                               for dc in range(NDC)]
                        mev = mm_seq(pb, bo, bo["t"][:, 0:TT], mms)
                        w_rd[do % 2] = mev
                        rv_ev[do] = read_bank(
                            pb, "vector", bo,
                            lambda v, do=do, bcol=bcol, bo=bo:
                            v.tensor_scalar(rvt[do][:], bo["t"][:, 0:TT],
                                            bias_c[:, bcol + do:bcol + do + 1],
                                            None, ALU.add),
                            deps=[rv_rd[do]])
                        st_pr[do % 2] = pb.dma(
                            "sync", ("stq", "st2")[do % 2],
                            dstfm[do * 128:(do + 1) * 128,
                                  tt * TT:(tt + 1) * TT],
                            rvt[do][:], deps=[rv_ev[do], st_pr[do % 2]])
                    for sc4 in range(4):
                        scg = tt * 4 + sc4
                        for half in range(2):
                            bo = pb.bank()
                            mms = [(rvt[dc][:, sc4 * 128:(sc4 + 1) * 128],
                                    whalf[half][:, dc * TT:(dc + 1) * TT],
                                    [rv_ev[dc], wh_ev[half]])
                                   for dc in range(NDC)]
                            mev = mm_seq(pb, bo, bo["t"][:, 0:TT], mms)
                            rv_rd = [mev] * NDC
                            tok_rd[half] = mev
                            pt = ptb[half]
                            ev = read_bank(pb, "vector", bo,
                                           lambda v, pt=pt, bo=bo:
                                           v.tensor_copy(pt[:],
                                                         bo["t"][:, 0:TT]),
                                           deps=[tokst[half], p_rd[half],
                                                 st_pr[half]])
                            p_rd[half] = ev
                            tokst[half] = pb.dma(
                                "sync", ("st3", "st4")[half],
                                dsttok[scg * 128:(scg + 1) * 128,
                                       half * TT:(half + 1) * TT],
                                pt[:], deps=[ev])
            prev = [wsm_ev]
            tr_rd = None
            for tc in range(NTC):
                bg = pb.bank()
                mms = [(cst.ones_row_b[0:1, :], bsmall[0:1, :], [wsm_ev])]
                mms += [(hb[dc][:, tc * 128:(tc + 1) * 128],
                         wsmall[:, dc * 26:(dc + 1) * 26], [wsm_ev, hb_ev[dc]])
                        for dc in range(NDC)]
                mg = mm_seq(pb, bg, bg["t"][:, 0:26], mms)
                g1 = read_bank(pb, "vector", bg,
                               lambda v, bg=bg: v.tensor_copy(sm[:],
                                                              bg["t"][:, 0:26]),
                               deps=prev)
                g2 = pb.op("scalar",
                           lambda s: s.activation(out=rgm[:, 0:1],
                                                  in_=sm[:, 0:1],
                                                  func=AF.Sigmoid, scale=5.0),
                           deps=[g1], inc=True)
                g3 = pb.op("scalar",
                           lambda s: s.activation(out=rgm[:, 1:2],
                                                  in_=sm[:, 9:10],
                                                  func=AF.Sigmoid, scale=5.0),
                           deps=[g1, g2], inc=True)
                evs = []
                for (c0, n, dstall, gcol) in ((1, 8, w_r_all, 0),
                                              (10, 16, wm_all, 1)):
                    a1 = pb.op("vector",
                               lambda v, c0=c0, n=n:
                               v.reduce_max(smx[:], sm[:, c0:c0 + n], axis=AX.X),
                               deps=[g1] + evs, inc=True)
                    a2 = pb.op("vector",
                               lambda v: v.tensor_scalar(smx[:], smx[:], -1.0,
                                                         None, ALU.mult),
                               deps=[a1], inc=True)
                    a3 = pb.op("scalar",
                               lambda s, c0=c0, n=n:
                               s.activation(out=sme[:, 0:n], in_=sm[:, c0:c0 + n],
                                            func=AF.Exp, bias=smx[:, 0:1],
                                            scale=1.0),
                               deps=[a2, g3], inc=True)
                    a4 = pb.op("vector",
                               lambda v, n=n:
                               v.reduce_sum(sms[:], sme[:, 0:n], axis=AX.X),
                               deps=[a3], inc=True)
                    a5 = pb.op("vector",
                               lambda v: v.reciprocal(sms[:], sms[:]),
                               deps=[a4], inc=True)
                    a6 = pb.op("vector",
                               lambda v, n=n:
                               v.tensor_scalar(sme[:, 0:n], sme[:, 0:n],
                                               sms[:, 0:1], None, ALU.mult),
                               deps=[a5], inc=True)
                    if gcol == 0:
                        a7 = pb.op("vector",
                                   lambda v, tc=tc, n=n, dstall=dstall:
                                   v.tensor_scalar(
                                       dstall[:, tc * n:(tc + 1) * n],
                                       sme[:, 0:n], rgm[:, 0:1], None,
                                       ALU.mult),
                                   deps=[a6, g2], inc=True)
                    else:
                        a7 = pb.op("vector",
                                   lambda v, tc=tc, n=n, dstall=dstall:
                                   v.tensor_scalar(
                                       dstall[:, tc * n:(tc + 1) * n],
                                       sme[:, 0:n], rgm[:, 1:2],
                                       dcol[:, tc:tc + 1],
                                       ALU.mult, ALU.mult),
                                   deps=[a6, g3], inc=True)
                    evs.append(a7)
                prev = evs
                for (dstall, n, dstT) in ((w_r_all, 8, wrT), (wm_all, 16, wmT)):
                    bt_ = pb.bank()
                    te = pb.op("tensor",
                               lambda t, bt_=bt_, dstall=dstall, n=n, tc=tc:
                               t.transpose(bt_["t"][0:n, 0:128],
                                           dstall[:, tc * n:(tc + 1) * n],
                                           cst.ident[:]),
                               deps=evs + list(bt_["rd"]) +
                                    ([bt_["wr"]] if bt_["wr"] else []),
                               inc=True)
                    bt_["wr"] = te
                    bt_["rd"] = []
                    tr_rd = read_bank(pb, "vector", bt_,
                                      lambda v, dstT=dstT, n=n, tc=tc, bt_=bt_:
                                      v.tensor_copy(
                                          dstT[0:n, tc * 128:(tc + 1) * 128],
                                          bt_["t"][0:n, 0:128]),
                                      deps=[tr_rd])

    with pb.scope():
        rqt = [pb.sbuf(f"rqt{dc}", [128, TT], BF16) for dc in range(NDC)]
        mqt = [pb.sbuf(f"mqt{dc}", [128, TT], BF16) for dc in range(NDC)]
        rvs = [pb.sbuf(f"rvs{dc}", [128, TT], BF16) for dc in range(NDC)]
        mvs = [pb.sbuf(f"mvs{dc}", [128, TT], BF16) for dc in range(NDC)]
        g_sb = [pb.sbuf(f"g_sb{i}", [128, TT], F32) for i in range(4)]
        sm8 = pb.sbuf("sm8", [128, 16], F32)
        smx2 = pb.sbuf("smx2", [128, 1], F32)
        sms2 = pb.sbuf("sms2", [128, 1], F32)
        pr_t = pb.sbuf("pr_t", [128, 16], F32)
        with pb.phase():
            for _eng in PB.CENG:
                pb.waitev(_eng, pb.chan_total("stq"), pb.chan_total("st2"),
                          pb.chan_total("st3"), pb.chan_total("st4"))
            g_rd = [None] * 4
            tr_rd = None
            sm_rd = None
            rsc_banks = [pb.bank_fixed(nm) for nm in ("stat", "bcm", "bcr", "c3")]
            for tt in range(NTT):
                ch = "ld1"
                for dc in range(NDC):
                    pb.dma("sync", ch, rqt[dc][:],
                           rq_hbm[dc * 128:(dc + 1) * 128, tt * TT:(tt + 1) * TT],
                           deps=[g_rd[0]])
                    pb.dma("sync", ch, mqt[dc][:],
                           mq_hbm[dc * 128:(dc + 1) * 128, tt * TT:(tt + 1) * TT],
                           deps=[g_rd[1]])
                q_ev = pb.chan_total(ch)
                first_rsc = [True] * 4
                last_rsc = [None] * 4
                for sct in range(tt + 1):
                    ch2 = f"ld{2 + sct % 2}"
                    for dc in range(NDC):
                        pb.dma("sync", ch2, rvs[dc][:],
                               rv_hbm[dc * 128:(dc + 1) * 128,
                                      sct * TT:(sct + 1) * TT],
                               deps=[g_rd[2]])
                        pb.dma("sync", ch2, mvs[dc][:],
                               mv_hbm[dc * 128:(dc + 1) * 128,
                                      sct * TT:(sct + 1) * TT],
                               deps=[g_rd[3]])
                    v_ev = pb.chan_total(ch2)
                    for sc4 in range(4):
                        scg = sct * 4 + sc4
                        if scg > tt * 4 + 3:
                            break
                        for (qt_src, v_src, gi, wall, nsl, ccol) in (
                                (rqt, rvs, 0, w_r_all, 8, 0),
                                (mqt, mvs, 1, wm_all, 16, 8)):
                            bgg = pb.bank()
                            mms = [(v_src[dc][:, sc4 * 128:(sc4 + 1) * 128],
                                    qt_src[dc][:], [q_ev, v_ev])
                                   for dc in range(NDC)]
                            mg = mm_seq(pb, bgg, bgg["t"][:, 0:TT], mms)
                            gs = g_sb[gi]
                            k = scg - tt * 4
                            if k >= 0:
                                ge = read_bank(pb, "vector", bgg,
                                               lambda v, gs=gs, k=k, bgg=bgg:
                                               v.tensor_tensor(
                                                   gs[:], bgg["t"][:, 0:TT],
                                                   m4[:, k * TT:(k + 1) * TT],
                                                   ALU.mult),
                                               deps=[g_rd[gi]])
                            else:
                                ge = read_bank(pb, "vector", bgg,
                                               lambda v, gs=gs, bgg=bgg:
                                               v.tensor_copy(gs[:],
                                                             bgg["t"][:, 0:TT]),
                                               deps=[g_rd[gi]])
                            lastm = None
                            for j in range(4):
                                bank = rsc_banks[j]
                                dd = [ge]
                                if first_rsc[j] and ccol == 0:
                                    dd += list(bank["rd"])
                                    if bank["wr"] is not None:
                                        bank_wr = bank["wr"]
                                        dd.append(bank_wr)
                                lastm = pb.op(
                                    "tensor",
                                    lambda t, gs=gs, j=j, bank=bank,
                                    f=(first_rsc[j] and ccol == 0),
                                    nsl=nsl, ccol=ccol, scg=scg:
                                    t.matmul(bank["t"][:, ccol:ccol + nsl],
                                             gs[:, j * 128:(j + 1) * 128],
                                             (w_r_all if nsl == 8 else wm_all)
                                             [:, scg * nsl:(scg + 1) * nsl],
                                             start=f, stop=False,
                                             skip_group_check=True),
                                    deps=dd, inc=True)
                                if ccol == 0:
                                    first_rsc[j] = False
                            g_rd[gi] = lastm
                            if gi == 0:
                                g_rd[2] = lastm
                            else:
                                g_rd[3] = lastm
                for j in range(4):
                    tcg = tt * 4 + j
                    bank = rsc_banks[j]
                    bank["wr"] = g_rd[1]
                    bank["rd"] = []
                    for (c0, n, dstT, dp) in ((0, 8, prT, False),
                                              (8, 16, pmT, True)):
                        a1 = pb.op("vector",
                                   lambda v, c0=c0, n=n, bank=bank, tcg=tcg, dp=dp:
                                   v.tensor_scalar(pr_t[:, 0:n],
                                                   bank["t"][:, c0:c0 + n],
                                                   SC, None, ALU.mult)
                                   if not dp else
                                   v.tensor_scalar(pr_t[:, 0:n],
                                                   bank["t"][:, c0:c0 + n],
                                                   SC,
                                                   dcol[:, 16 + tcg:17 + tcg],
                                                   ALU.mult, ALU.mult),
                                   deps=[sm_rd, bank["wr"]], inc=True)
                        bank["rd"].append(a1)
                        a2 = pb.op("vector",
                                   lambda v, n=n:
                                   v.reduce_max(smx2[:], pr_t[:, 0:n], axis=AX.X),
                                   deps=[a1], inc=True)
                        a3 = pb.op("vector",
                                   lambda v: v.tensor_scalar(smx2[:], smx2[:],
                                                             -1.0, None,
                                                             ALU.mult),
                                   deps=[a2], inc=True)
                        a4 = pb.op("scalar",
                                   lambda s, n=n:
                                   s.activation(out=pr_t[:, 0:n],
                                                in_=pr_t[:, 0:n],
                                                func=AF.Exp, bias=smx2[:, 0:1],
                                                scale=1.0),
                                   deps=[a3], inc=True)
                        a5 = pb.op("vector",
                                   lambda v, n=n:
                                   v.reduce_sum(sms2[:], pr_t[:, 0:n], axis=AX.X),
                                   deps=[a4], inc=True)
                        a6 = pb.op("vector",
                                   lambda v: v.reciprocal(sms2[:], sms2[:]),
                                   deps=[a5], inc=True)
                        if dp:
                            a7 = pb.op("vector",
                                       lambda v, n=n, tcg=tcg:
                                       v.tensor_scalar(pr_t[:, 0:n],
                                                       pr_t[:, 0:n],
                                                       sms2[:, 0:1],
                                                       dcol[:, 16 + tcg:
                                                            17 + tcg],
                                                       ALU.mult, ALU.mult),
                                       deps=[a6], inc=True)
                        else:
                            a7 = pb.op("vector",
                                       lambda v, n=n:
                                       v.tensor_scalar(pr_t[:, 0:n],
                                                       pr_t[:, 0:n],
                                                       sms2[:, 0:1], None,
                                                       ALU.mult),
                                       deps=[a6], inc=True)
                        bt_ = pb.bank()
                        te = pb.op("tensor",
                                   lambda t, bt_=bt_, n=n:
                                   t.transpose(bt_["t"][0:n, 0:128],
                                               pr_t[:, 0:n], cst.ident[:]),
                                   deps=[a7] + list(bt_["rd"]) +
                                        ([bt_["wr"]] if bt_["wr"] else []),
                                   inc=True)
                        bt_["wr"] = te
                        bt_["rd"] = []
                        tr_rd = read_bank(pb, "vector", bt_,
                                          lambda v, dstT=dstT, n=n, tcg=tcg,
                                          bt_=bt_:
                                          v.tensor_copy(
                                              dstT[0:n,
                                                   tcg * 128:(tcg + 1) * 128],
                                              bt_["t"][0:n, 0:128]),
                                          deps=[tr_rd])
                        sm_rd = a7
                    sm_rd = tr_rd

    with pb.scope():
        ar_sb = [pb.sbuf(f"ar{i}", [128, TT], BF16) for i in range(2)]
        am_sb = [pb.sbuf(f"am{i}", [128, TT], BF16) for i in range(2)]
        rvp_t = [pb.sbuf(f"rvp{i}", [128, D], BF16) for i in range(2)]
        mvp_t = [pb.sbuf(f"mvp{i}", [128, D], BF16) for i in range(2)]
        pre_t = pb.sbuf("pre_t", [128, TT], F32R)
        cph_t = [pb.sbuf(f"cph{i}", [128, TT], F32R) for i in range(2)]
        with pb.phase():
            for _eng in PB.CENG:
                pb.waitev(_eng, pb.chan_total("stq"), pb.chan_total("st2"),
                          pb.chan_total("st3"), pb.chan_total("st4"))
            a_rd = [None, None]
            tok_rd = [None, None]
            pre_rd = None
            cph_rd = [None, None]
            cst_st = [None, None]
            cbanks = [pb.bank_fixed(nm) for nm in ("stat", "bcm", "bcr", "c3")]
            for tt in range(NTT):
                for dh in range(2):
                    first_c = [True] * 4
                    lastc = [None] * 4
                    for sct in range(tt + 1):
                        for sc4 in range(4):
                            scg = sct * 4 + sc4
                            if scg > tt * 4 + 3:
                                break
                            k = scg - tt * 4
                            ba = pb.bank()
                            ma = mm_seq(pb, ba, ba["t"][:, 0:TT],
                                        [(wrT[0:8, scg * 128:(scg + 1) * 128],
                                          prT[0:8, tt * TT:(tt + 1) * TT], [])])
                            asb = ar_sb[scg % 2]
                            if k >= 0:
                                ae = read_bank(pb, "vector", ba,
                                               lambda v, asb=asb, k=k, ba=ba:
                                               v.tensor_tensor(
                                                   asb[:], ba["t"][:, 0:TT],
                                                   m4[:, k * TT:(k + 1) * TT],
                                                   ALU.mult),
                                               deps=[a_rd[scg % 2]])
                            else:
                                ae = read_bank(pb, "vector", ba,
                                               lambda v, asb=asb, ba=ba:
                                               v.tensor_copy(asb[:],
                                                             ba["t"][:, 0:TT]),
                                               deps=[a_rd[scg % 2]])
                            bam = pb.bank()
                            mam = mm_seq(pb, bam, bam["t"][:, 0:TT],
                                         [(wmT[0:16, scg * 128:(scg + 1) * 128],
                                           pmT[0:16, tt * TT:(tt + 1) * TT],
                                           [])])
                            amb = am_sb[scg % 2]
                            if k >= 0:
                                ame = read_bank(pb, "vector", bam,
                                                lambda v, amb=amb, k=k, bam=bam:
                                                v.tensor_tensor(
                                                    amb[:], bam["t"][:, 0:TT],
                                                    m4[:, k * TT:(k + 1) * TT],
                                                    ALU.mult),
                                                deps=[a_rd[scg % 2]])
                            else:
                                ame = read_bank(pb, "vector", bam,
                                                lambda v, amb=amb, bam=bam:
                                                v.tensor_copy(amb[:],
                                                              bam["t"][:, 0:TT]),
                                                deps=[a_rd[scg % 2]])
                            chv = f"ld{1 + scg % 2}"
                            pb.dma("sync", chv, rvp_t[scg % 2][:],
                                   rvp_hbm[scg * 128:(scg + 1) * 128, :],
                                   deps=[tok_rd[scg % 2]])
                            pb.dma("sync", chv, mvp_t[scg % 2][:],
                                   mvp_hbm[scg * 128:(scg + 1) * 128, :],
                                   deps=[tok_rd[scg % 2]])
                            tok_ev = pb.chan_total(chv)
                            lastm = None
                            for jj in range(4):
                                do = dh * 4 + jj
                                bank = cbanks[jj]
                                dd = [ae, tok_ev]
                                if first_c[jj]:
                                    dd += list(bank["rd"])
                                    if bank["wr"] is not None:
                                        dd.append(bank["wr"])
                                pb.op("tensor",
                                      lambda t, do=do, jj=jj, bank=bank,
                                      f=first_c[jj], i=scg % 2:
                                      t.matmul(bank["t"][:, 0:TT],
                                               rvp_t[i][:, do * 128:
                                                       (do + 1) * 128],
                                               ar_sb[i][:],
                                               start=f, stop=False,
                                               skip_group_check=True),
                                      deps=dd)
                                first_c[jj] = False
                                lastm = pb.op(
                                    "tensor",
                                    lambda t, do=do, jj=jj, bank=bank,
                                    i=scg % 2:
                                    t.matmul(bank["t"][:, 0:TT],
                                             mvp_t[i][:, do * 128:
                                                     (do + 1) * 128],
                                             am_sb[i][:],
                                             start=False, stop=False,
                                             skip_group_check=True),
                                    deps=[ame, tok_ev], inc=True)
                                lastc[jj] = lastm
                            a_rd[scg % 2] = lastm
                            tok_rd[scg % 2] = lastm
                    for jj in range(4):
                        do = dh * 4 + jj
                        bank = cbanks[jj]
                        bank["wr"] = lastc[jj]
                        bank["rd"] = []
                        pdma = pb.dma("sync", "ld3", pre_t[:],
                                      pre_hbm[do * 128:(do + 1) * 128,
                                              tt * TT:(tt + 1) * TT],
                                      deps=[pre_rd, pb.chan_total("ld3")])
                        ct = cph_t[jj % 2]
                        c1 = read_bank(pb, "vector", bank,
                                       lambda v, ct=ct, bank=bank:
                                       v.tensor_tensor(ct[:], bank["t"][:, 0:TT],
                                                       pre_t[:], ALU.add),
                                       deps=[pdma, cph_rd[jj % 2],
                                             cst_st[jj % 2]])
                        pre_rd = c1
                        c2 = pb.op("vector",
                                   lambda v, ct=ct, do=do, tt=tt:
                                   v.tensor_tensor(ct[:], ct[:],
                                                   h[do][:, tt * TT:
                                                        (tt + 1) * TT],
                                                   ALU.add),
                                   deps=[c1], inc=True)
                        cst_st[jj % 2] = pb.dma(
                            "sync", ("stq", "st2")[jj % 2],
                            cph_hbm[do * 128:(do + 1) * 128,
                                    tt * TT:(tt + 1) * TT],
                            ct[:], deps=[c2, cst_st[jj % 2]])
                        cph_rd[jj % 2] = c2

    sc24.__exit__(None, None, None)
    with pb.scope():
        morow = pb.sbuf("morow", [1, L], F32)
        rorow = pb.sbuf("rorow", [1, L], F32)
        strm = [[pb.sbuf(f"li{i}_{dc}", [128, TT], F32R) for dc in range(NDC)]
                for i in range(2)]
        with pb.phase():
            for _eng in PB.CENG:
                pb.waitev(_eng, pb.chan_total("stq"), pb.chan_total("st2"))
            war = [None, None]
            evs = [None, None]

            def get_x(dc, tt):
                if dc == 0:
                    ch = f"ld{4 + tt % 2}"
                    for c in range(NDC):
                        pb.dma("sync", ch, strm[tt % 2][c][:],
                               cph_hbm[c * 128:(c + 1) * 128,
                                       tt * TT:(tt + 1) * TT],
                               deps=[war[tt % 2]])
                    evs[tt % 2] = pb.chan_total(ch)
                return (strm[tt % 2][dc], 0, evs[tt % 2])

            ln_stats(pb, cst, get_x, m0row, r0row, "si",
                     war_hook=lambda tt, ev: war.__setitem__(tt % 2, ev))
        sqy = pb.sbuf("sqy", [128, TT], F32R)
        y_t = [pb.sbuf(f"y_t{i}", [128, TT], F32R) for i in range(2)]
        yvrow = pb.sbuf("yvrow", [1, TT], F32)
        with pb.phase():
            war = [None, None]
            y_st = [None, None]
            ysq = [None, None]
            ymm = [None, None]
            sq_rd = None
            prev_small = []
            for tt in range(NTT):
                ch = f"ld{4 + tt % 2}"
                for c in range(NDC):
                    pb.dma("sync", ch, strm[tt % 2][c][:],
                           cph_hbm[c * 128:(c + 1) * 128, tt * TT:(tt + 1) * TT],
                           deps=[war[tt % 2]])
                tot = pb.chan_total(ch)
                bm, br = bcast_rows(pb, cst, m0row, r0row, tt)
                bank_s = pb.bank_fixed("stat")
                bank_q = pb.bank_fixed("c3")
                ps = bank_s["t"][0:1, 0:TT]
                pq = bank_q["t"][0:1, 0:TT]
                last_me = None
                for dc in range(NDC):
                    yt = y_t[dc % 2]
                    e1 = read_bank(pb, "vector", bm,
                                   lambda v, dc=dc, tt=tt, bm=bm:
                                   v.tensor_tensor(scr[0][:],
                                                   strm[tt % 2][dc][:],
                                                   bm["t"][:, 0:TT],
                                                   ALU.subtract),
                                   deps=[tot, war[tt % 2]])
                    e2 = read_bank(pb, "vector", br,
                                   lambda v, br=br:
                                   v.tensor_tensor(scr[1][:], scr[0][:],
                                                   br["t"][:, 0:TT], ALU.mult),
                                   deps=[e1])
                    e3 = pb.op("vector",
                               lambda v, dc=dc, yt=yt:
                               v.tensor_scalar(yt[:], scr[1][:],
                                               bias_c[:, 40 + dc:41 + dc],
                                               bias_c[:, 48 + dc:49 + dc],
                                               ALU.mult, ALU.add),
                               deps=[e2, y_st[dc % 2], ysq[dc % 2],
                                     ymm[dc % 2]], inc=True)
                    y_st[dc % 2] = pb.dma(
                        "sync", ("stq", "st2")[dc % 2],
                        y_hbm[dc * 128:(dc + 1) * 128, tt * TT:(tt + 1) * TT],
                        yt[:], deps=[e3])
                    war[tt % 2] = e3
                    sev = pb.op("scalar",
                                lambda s, yt=yt:
                                s.activation(out=sqy[:], in_=yt[:],
                                             func=AF.Square),
                                deps=[e3, sq_rd], inc=True)
                    ysq[dc % 2] = sev
                    dd = [e3]
                    if dc == 0:
                        dd += list(bank_s["rd"])
                        if bank_s["wr"] is not None:
                            dd.append(bank_s["wr"])
                    pb.op("tensor",
                          lambda t, yt=yt, f=(dc == 0), l=(dc == NDC - 1), ps=ps:
                          t.matmul(ps, cst.ones_col_r[:], yt[:],
                                   start=f, stop=l,
                                   skip_group_check=True),
                          deps=dd)
                    dd2 = [sev]
                    if dc == 0:
                        dd2 += list(bank_q["rd"])
                        if bank_q["wr"] is not None:
                            dd2.append(bank_q["wr"])
                    last_me = pb.op(
                        "tensor",
                        lambda t, f=(dc == 0), l=(dc == NDC - 1), pq=pq:
                        t.matmul(pq, cst.ones_col_r[:], sqy[:],
                                 start=f, stop=l,
                                 skip_group_check=True),
                        deps=dd2, inc=True)
                    sq_rd = last_me
                    ymm[dc % 2] = last_me
                bank_s["wr"] = last_me
                bank_q["wr"] = last_me
                bank_s["rd"] = []
                bank_q["rd"] = []
                c0 = tt * TT
                e1 = read_bank(pb, "vector", bank_s,
                               lambda v, ps=ps, c0=c0:
                               v.tensor_scalar(morow[0:1, c0:c0 + TT], ps,
                                               1.0 / D, None, ALU.mult),
                               deps=prev_small)
                e2 = read_bank(pb, "vector", bank_q,
                               lambda v, pq=pq, c0=c0:
                               v.tensor_scalar(yvrow[0:1, :], pq,
                                               1.0 / D, None, ALU.mult),
                               deps=prev_small + [e1])
                e3b = pb.op("scalar",
                            lambda s, c0=c0:
                            s.activation(out=rorow[0:1, c0:c0 + TT],
                                         in_=morow[0:1, c0:c0 + TT],
                                         func=AF.Square),
                            deps=[e1], inc=True)
                e4 = pb.op("vector",
                           lambda v, c0=c0:
                           v.tensor_tensor(yvrow[0:1, :], yvrow[0:1, :],
                                           rorow[0:1, c0:c0 + TT],
                                           ALU.subtract),
                           deps=[e2, e3b], inc=True)
                e5 = pb.op("scalar",
                           lambda s, c0=c0:
                           s.activation(out=rorow[0:1, c0:c0 + TT],
                                        in_=yvrow[0:1, :], func=AF.Sqrt,
                                        bias=cst.cpool[0:1, C_EPS:C_EPS + 1],
                                        scale=1.0),
                           deps=[e4], inc=True)
                e6 = pb.op("vector",
                           lambda v, c0=c0:
                           v.reciprocal(rorow[0:1, c0:c0 + TT],
                                        rorow[0:1, c0:c0 + TT]),
                           deps=[e5], inc=True)
                prev_small = [e6]
        xs2 = [[pb.sbuf(f"xs2{i}_{c}", [128, TT], F32) for c in range(NDC)]
               for i in range(2)]
        out_t = [scr[2], scr[3]]
        with pb.phase():
            for _eng in PB.CENG:
                pb.waitev(_eng, pb.chan_total("stq"), pb.chan_total("st2"))
            xuse = [None, None]
            ost = [None, None]
            last = None
            for tt in range(NTT):
                ch = f"ld{1 + tt % 2}"
                for dc in range(NDC):
                    pb.dma("sync", ch, strm[tt % 2][dc][:],
                           y_hbm[dc * 128:(dc + 1) * 128, tt * TT:(tt + 1) * TT],
                           deps=[xuse[tt % 2]])
                    pb.dma("sync", ch, xs2[tt % 2][dc][:],
                           x_f[dc * 128:(dc + 1) * 128, tt * TT:(tt + 1) * TT],
                           deps=[xuse[tt % 2]])
                tot = pb.chan_total(ch)
                bm, br = bcast_rows(pb, cst, morow, rorow, tt)
                for dc in range(NDC):
                    ot = out_t[dc % 2]
                    e1 = read_bank(pb, "vector", bm,
                                   lambda v, dc=dc, tt=tt, bm=bm:
                                   v.tensor_tensor(scr[0][:],
                                                   strm[tt % 2][dc][:],
                                                   bm["t"][:, 0:TT],
                                                   ALU.subtract),
                                   deps=[last, tot])
                    e2 = read_bank(pb, "vector", br,
                                   lambda v, br=br:
                                   v.tensor_tensor(scr[1][:], scr[0][:],
                                                   br["t"][:, 0:TT], ALU.mult),
                                   deps=[e1])
                    e3 = pb.op("vector",
                               lambda v, dc=dc, tt=tt, ot=ot:
                               v.tensor_tensor(ot[:], scr[1][:],
                                               xs2[tt % 2][dc][:], ALU.add),
                               deps=[e2, ost[dc % 2]], inc=True)
                    ost[dc % 2] = pb.dma(
                        "sync", ("stq", "st2")[dc % 2],
                        xo_b[dc * 128:(dc + 1) * 128, tt * TT:(tt + 1) * TT],
                        ot[:], deps=[e3])
                    last = e3
                    xuse[tt % 2] = e3
            pb.waitev("sync", pb.chan_total("stq"), pb.chan_total("st2"))

    return


def prep_weights(inp):
    w = {}
    cc = np.ascontiguousarray
    f32 = lambda k: np.asarray(inp[k], np.float32)
    w["l0f_tapT"] = cc(f32("L0_ffn_conv_w").transpose(2, 1, 0))
    w["l0f_convb_c"] = cc(f32("L0_ffn_conv_b").reshape(8, 128).T)
    w["l0f_wqkvT"] = cc(f32("L0_ffn_wqkv").T)
    w["l0f_bqkv_c"] = cc(f32("L0_ffn_bqkv").reshape(24, 128).T)
    w["l0f_bqkv_row"] = cc(f32("L0_ffn_bqkv").reshape(1, 3 * 1024))
    w["l0f_woT"] = cc(f32("L0_ffn_wo").T)
    w["l0f_bo_c"] = cc(f32("L0_ffn_bo").reshape(8, 128).T)
    w["l0f_ln1g_c"] = cc(f32("L0_ffn_ln1_g").reshape(8, 128).T)
    w["l0f_ln1b_c"] = cc(f32("L0_ffn_ln1_b").reshape(8, 128).T)
    w["l0f_ln2g_c"] = cc(f32("L0_ffn_ln2_g").reshape(8, 128).T)
    w["l0f_ln2b_c"] = cc(f32("L0_ffn_ln2_b").reshape(8, 128).T)
    w["l0f_gatew"] = cc(f32("L0_ffn_gate_w"))
    w["l0f_gateb_row"] = cc(f32("L0_ffn_gate_b").reshape(1, 8))
    w["l0f_w1"] = cc(f32("L0_ffn_w1"))
    w["l0f_b1_c"] = cc(f32("L0_ffn_b1").reshape(8, 32, 128).transpose(0, 2, 1))
    w["l0f_w2"] = cc(f32("L0_ffn_w2"))
    w["l0f_b2"] = cc(f32("L0_ffn_b2"))
    w["l1f_WrT"] = cc(f32("L1_ffn_W_r").T)
    w["l1f_WkT"] = cc(f32("L1_ffn_W_k").T)
    w["l1f_WvT"] = cc(f32("L1_ffn_W_v").T)
    w["l0a_rvT"] = cc(f32("L0_attn_reg_val_w").T)
    w["l0a_rqT"] = cc(f32("L0_attn_reg_q_w").T)
    w["l0a_mvT"] = cc(f32("L0_attn_mem_val_w").T)
    w["l0a_mqT"] = cc(f32("L0_attn_mem_q_w").T)
    w["l0a_combT"] = cc(f32("L0_attn_comb_w").T)
    w["l0a_wsmall"] = cc(np.concatenate(
        [f32("L0_attn_reg_gate_w").T, f32("L0_attn_reg_addr_w").T,
         f32("L0_attn_mem_gate_w").T, f32("L0_attn_mem_addr_w").T], axis=1))
    w["l0a_bsmall"] = cc(np.concatenate(
        [f32("L0_attn_reg_gate_b"), f32("L0_attn_reg_addr_b"),
         f32("L0_attn_mem_gate_b"), f32("L0_attn_mem_addr_b")])[None, :])
    bia = np.zeros((128, 56), np.float32)
    bia[:, 0:8] = f32("L0_attn_reg_val_b").reshape(8, 128).T
    bia[:, 8:16] = f32("L0_attn_reg_q_b").reshape(8, 128).T
    bia[:, 16:24] = f32("L0_attn_mem_val_b").reshape(8, 128).T
    bia[:, 24:32] = f32("L0_attn_mem_q_b").reshape(8, 128).T
    bia[:, 32:40] = f32("L0_attn_comb_b").reshape(8, 128).T
    bia[:, 40:48] = f32("L0_attn_ln_g").reshape(8, 128).T
    bia[:, 48:56] = f32("L0_attn_ln_b").reshape(8, 128).T
    w["l0a_bias_c"] = cc(bia)
    dcv = np.zeros((128, 32), np.float64)
    pidx = np.arange(128)
    for sc in range(16):
        dcv[:, sc] = 0.995 ** (-(sc * 128.0 + pidx))
        dcv[:, 16 + sc] = 0.995 ** (sc * 128.0 + pidx)
    w["l0a_dcol"] = cc(dcv.astype(np.float32))
    m4a = np.zeros((4, 128, 512), np.float32)
    for k4 in range(4):
        for p4 in range(128):
            m4a[k4, p4, p4 + 128 * k4:] = 1.0
    w["l0a_m4"] = cc(m4a.transpose(1, 0, 2).reshape(128, 4 * 512))
    for nm in ("wq_lin", "wk_lin", "wv_lin", "wq_loc", "wk_loc", "wv_loc",
               "wo", "wg1", "wg2"):
        w["l1a_" + nm] = cc(f32("L1_attn_" + nm))
    bias = np.zeros((128, 64), np.float32)
    bias[:, 0:8] = f32("L1_attn_bq_lin").reshape(8, 128).T
    bias[:, 8:16] = f32("L1_attn_bq_loc").reshape(8, 128).T
    bias[:, 16:24] = f32("L1_attn_bk_loc").reshape(8, 128).T
    bias[:, 24:32] = f32("L1_attn_bg1").reshape(8, 128).T
    bias[:, 32:40] = f32("L1_attn_bg2").reshape(8, 128).T
    bias[:, 40:48] = f32("L1_attn_bo").reshape(8, 128).T
    bias[:, 48:56] = f32("L1_attn_ln_g").reshape(8, 128).T
    bias[:, 56:64] = f32("L1_attn_ln_b").reshape(8, 128).T
    w["l1a_bias_c"] = cc(bias)
    w["l1a_brow"] = cc(np.stack([f32("L1_attn_bk_lin"),
                                 f32("L1_attn_bv_lin"),
                                 f32("L1_attn_bv_loc")], axis=0))
    rb = f32("L1_attn_rel_bias")
    Q = np.full((16, 512), -10000.0, np.float32)
    for m in range(224, 289):
        Q[:, m] = rb[m - 256 + 64, :]
    BT = np.zeros((128, 16 * 384), np.float32)
    p_idx = np.arange(128)[:, None]
    i_idx = np.arange(128)[None, :]
    for hh2 in range(16):
        for c3 in range(3):
            blk = Q[hh2][(c3 - 1) * 128 + p_idx - i_idx + 256]
            BT[:, hh2 * 384 + c3 * 128:hh2 * 384 + (c3 + 1) * 128] = blk
    w["l1a_BT"] = cc(BT)
    mu = np.zeros((128, 16), np.float32)
    mu[:, 0:8] = f32("L1_ffn_mu_r").reshape(8, 128).T
    mu[:, 8:16] = f32("L1_ffn_mu_k").reshape(8, 128).T
    w["l1f_mu"] = cc(mu)
    return w


def _to_fm(x):
    return [np.ascontiguousarray(np.asarray(x[b], np.float32).T)
            for b in range(x.shape[0])]


def _from_fm(shards):
    return np.stack([np.ascontiguousarray(np.asarray(s, np.float32).T)
                     for s in shards], axis=0)


def run_block(pb, x_shards):
    in_maps = []
    for c in range(NCORES):
        m = dict(pb.inputs)
        m["x_in"] = x_shards[c]
        in_maps.append(m)
    res = run_bass_kernel_spmd(pb.nc, in_maps, core_ids=list(range(NCORES)))
    return [np.asarray(r["x_out"], np.float32) for r in res.results]


EMITTERS = {"l0attn": emit_l0attn, "l0ffn": emit_l0ffn,
            "l1attn": emit_l1attn, "l1ffn": emit_l1ffn}


def _mk_builder(tag):
    def build(weights):
        pb = PB(tag)
        cst = Consts(pb)
        pb.psum_banks(4)
        x_d = pb.nc.dram_tensor("x_in", [D, L], F32R, kind="ExternalInput").ap()
        xo_d = pb.dram_out("x_out", [D, L], F32)
        with pb.phase():
            cst.load(pb)
            for _eng in PB.CENG:
                pb.waitev(_eng, pb.chan_total("ldc"))
        EMITTERS[tag](pb, cst, x_d, xo_d, weights)
        return pb
    return build


build_l0attn = _mk_builder("l0attn")
build_l0ffn = _mk_builder("l0ffn")
build_l1attn = _mk_builder("l1attn")
build_l1ffn = _mk_builder("l1ffn")


def build_fused(weights):
    pb = PB("fused")
    cst = Consts(pb)
    pb.psum_banks(4)
    x0 = pb.nc.dram_tensor("x_in", [D, L], F32R, kind="ExternalInput").ap()
    x1 = pb.nc.dram_tensor("x1_hbm", [D, L], F32R, kind="Internal").ap()
    x2 = pb.nc.dram_tensor("x2_hbm", [D, L], F32R, kind="Internal").ap()
    x3 = pb.nc.dram_tensor("x3_hbm", [D, L], F32R, kind="Internal").ap()
    xo = pb.dram_out("x_out", [D, L], F32)
    with pb.phase():
        cst.load(pb)
        for _eng in PB.CENG:
            pb.waitev(_eng, pb.chan_total("ldc"))
    need_pre(pb, weights, PRE_ORDER)
    chain = [(emit_l0attn, "A_", x0, x1), (emit_l0ffn, "B_", x1, x2),
             (emit_l1attn, "C_", x2, x3), (emit_l1ffn, "D_", x3, xo)]
    for fn, pref, xi, xo_ in chain:
        pb.prefix = pref
        with pb.scope():
            with pb.phase():
                for _eng in PB.CENG:
                    pb.waitev(_eng, *[pb.chan_total(c) for c in
                                      ("stq", "st2", "st3", "st4")])
            fn(pb, cst, xi, xo_, weights)
    pb.prefix = ""
    return pb


def kernel(**inputs):
    w = prep_weights(inputs)
    x = _to_fm(inputs["L0_attn_u"])
    pb = build_fused(w)
    x = run_block(pb, x)
    return _from_fm(x)
